# version 02
# baseline (speedup 1.0000x reference)
from contextlib import ExitStack

import numpy as np

import concourse.bass as bass
import concourse.mybir as mybir
from concourse.bass_utils import run_bass_kernel_spmd

F32 = mybir.dt.float32
BF16 = mybir.dt.float16
AF = mybir.ActivationFunctionType
ALU = mybir.AluOpType
AX = mybir.AxisListType

T = 4096
D = 1024
H = 16
DQ = 64
NB = 10
ML = 1024
FF = 4096
NEG = -30000.0
EPS = 1e-5
NTT = T // 128
NDC = D // 128
NFC = FF // 128
NTB = T // 512

DEBUG_OUTPUTS = ()
PHASE_LIMIT = None
F3_NR = 4
F3_NE = 4
PHASES_SELECT = None


class Prog:

    PERSISTENT = {"PB"}

    def __init__(self):
        self.nc = bass.Bass("TRN2", target_bir_lowering=False)
        self.ctx = ExitStack()
        self.sems = {}
        self.cnt = {}
        self.dram = {}
        self.sem_cms = {}
        self.pending = []

    def sem(self, name):
        if name not in self.sems:
            cm = self.nc.semaphore(name)
            self.sems[name] = cm.__enter__()
            self.cnt[name] = 0
            self.sem_cms[name] = cm
            if name not in self.PERSISTENT:
                self.pending.append(name)
        return self.sems[name]

    def release_pending(self):
        for name in self.pending:
            self.sem_cms[name].__exit__(None, None, None)
            del self.sems[name]
            del self.cnt[name]
            del self.sem_cms[name]
        self.pending = []

    def inc(self, name, instr, k=1):
        instr.then_inc(self.sem(name), k)
        self.cnt[name] += k
        return instr

    def scratch(self, name, shape, dtype):
        if name not in self.dram:
            kind = "ExternalOutput" if name in DEBUG_OUTPUTS else "Internal"
            self.dram[name] = self.nc.dram_tensor(name, shape, dtype, kind=kind)
        return self.dram[name]

    def sbuf(self, name, shape, dtype):
        return self.ctx.enter_context(self.nc.sbuf_tensor(name, shape, dtype))


def phase_barrier(p, engines=("sync", "vector", "scalar", "tensor", "gpsimd")):
    nc = p.nc
    base = p.cnt.get("PB", 0)
    target_a = base + len(engines)
    target_b = target_a + 1
    p.sem("PB")
    to_clear = list(p.pending)
    with nc.Block() as b:
        def mk(eng_name):
            def emit(eng):
                if eng_name in ("vector", "gpsimd"):
                    p.inc("PB", eng.engine_nop())
                else:
                    p.inc("PB", eng.nop())
                eng.wait_ge(p.sem("PB"), target_a)
                if eng_name == "sync":
                    for name in to_clear:
                        eng.sem_clear(p.sems[name])
                    p.inc("PB", eng.nop())
                eng.wait_ge(p.sem("PB"), target_b)
            return emit
        for e in engines:
            getattr(b, e)(mk(e))
    p.cnt["PB"] = target_b
    p.release_pending()


def emit_ln_phase(p, src_name, src_ap, dst_T_name, dst_f32_name, idtile,
                  merge=None):
    nc = p.nc
    xnT = p.scratch(dst_T_name, [NTB, NDC, 128, 512], BF16)
    curf = p.scratch(dst_f32_name, [T, D], F32) if dst_f32_name else None
    sLD = p.sem(src_name + "_LD")
    sV0 = p.sem(src_name + "_V0")
    sS0 = p.sem(src_name + "_S0")
    sV1 = p.sem(src_name + "_V1")
    sS1 = p.sem(src_name + "_S1")
    sV2 = p.sem(src_name + "_V2")
    sT0 = p.sem(src_name + "_T0")
    sV3 = p.sem(src_name + "_V3")
    sSTC = p.sem(src_name + "_STC")
    sSTG = p.sem(src_name + "_STG")
    b_LD = p.cnt[src_name + "_LD"]
    b_V2 = p.cnt[src_name + "_V2"]
    b_S0 = p.cnt[src_name + "_S0"]
    b_V1 = p.cnt[src_name + "_V1"]
    b_S1 = p.cnt[src_name + "_S1"]
    b_V0 = p.cnt[src_name + "_V0"]
    b_T0 = p.cnt[src_name + "_T0"]
    b_V3 = p.cnt[src_name + "_V3"]
    b_STC = p.cnt[src_name + "_STC"]
    b_STG = p.cnt[src_name + "_STG"]

    if merge is not None:
        s_dram, f_dram, x1_dram = merge
        sMG = p.sem(src_name + "_MG")
        sMS = p.sem(src_name + "_MS")
        sMV = p.sem(src_name + "_MV")
        sMX = p.sem(src_name + "_MX")
        b_MG = p.cnt[src_name + "_MG"]
        b_MS = p.cnt[src_name + "_MS"]
        b_MV = p.cnt[src_name + "_MV"]
        b_MX = p.cnt[src_name + "_MX"]

    with ExitStack() as scope:
        e = scope.enter_context
        xt = [e(nc.sbuf_tensor(f"{src_name}_xt{i}", [128, D], F32)) for i in range(2)]
        if merge is not None:
            st_m = [e(nc.sbuf_tensor(f"{src_name}_st{i}", [128, D], F32))
                    for i in range(2)]
            ft_m = [e(nc.sbuf_tensor(f"{src_name}_ft{i}", [128, D], F32))
                    for i in range(2)]
            gt_m = e(nc.sbuf_tensor(src_name + "_gt", [128, D], F32))
        sq = e(nc.sbuf_tensor(src_name + "_sq", [128, D], F32))
        mneg = e(nc.sbuf_tensor(src_name + "_mneg", [128, 2], F32))
        var = e(nc.sbuf_tensor(src_name + "_var", [128, 2], F32))
        rstd = e(nc.sbuf_tensor(src_name + "_rstd", [128, 2], F32))
        xnb = e(nc.sbuf_tensor(src_name + "_xnb", [128, D], BF16))
        xnf = [e(nc.sbuf_tensor(f"{src_name}_xnf{i}", [128, D], F32)) for i in range(2)]
        stage = e(nc.sbuf_tensor(src_name + "_stg", [128, NDC * 512], BF16))
        pt = [e(nc.psum_tensor(f"{src_name}_pt{i}", [128, 128], BF16))
              for i in range(8)]

        with nc.Block() as b:
            @b.sync
            def _(s):
                for i in range(NTT):
                    if i >= 2:
                        s.wait_ge(sV2, b_V2 + i - 1)
                    p.inc(src_name + "_LD",
                          s.dma_start(out=xt[i % 2][:, :],
                                      in_=src_ap[i * 128:(i + 1) * 128, :]), 16)
                    if merge is not None:
                        s.wait_ge(p.sem(src_name + "_MV"), b_MV + i + 1)
                        p.inc(src_name + "_MX", s.dma_start(
                            out=x1_dram.ap()[i * 128:(i + 1) * 128, :],
                            in_=xt[i % 2][:, :]), 16)

            @b.vector
            def _(v):
                for i in range(NTT):
                    g = i % 4
                    xi = xt[i % 2]
                    v.wait_ge(sLD, b_LD + 16 * (i + 1))
                    if merge is not None:
                        v.wait_ge(p.sem(src_name + "_MS"), b_MS + i + 1)
                        v.tensor_tensor(gt_m[:, :], gt_m[:, :], st_m[i % 2][:, :],
                                        ALU.mult)
                        if i >= 2:
                            v.wait_ge(p.sem(src_name + "_MX"), b_MX + 16 * (i - 1))
                        p.inc(src_name + "_MV",
                              v.tensor_tensor(xi[:, :], xi[:, :], gt_m[:, :],
                                              ALU.add))
                    p.inc(src_name + "_V0a",
                          v.reduce_sum(mneg[:, g % 2:g % 2 + 1], xi[:, :],
                                       axis=AX.X))
                    v.wait_ge(p.sem(src_name + "_V0a"), p.cnt[src_name + "_V0a"])
                    p.inc(src_name + "_V0",
                          v.tensor_scalar(mneg[:, g % 2:g % 2 + 1],
                                          mneg[:, g % 2:g % 2 + 1],
                                          -1.0 / D, None, ALU.mult))
                    v.wait_ge(sS0, b_S0 + i + 1)
                    p.inc(src_name + "_V1",
                          v.reduce_sum(var[:, g % 2:g % 2 + 1], sq[:, :], axis=AX.X))
                    v.wait_ge(sS1, b_S1 + i + 1)
                    if curf is not None and i >= 2:
                        v.wait_ge(sSTC, b_STC + 16 * (i - 1))
                    v.tensor_scalar(xnb[:, :], xi[:, :],
                                    mneg[:, g % 2:g % 2 + 1],
                                    rstd[:, g % 2:g % 2 + 1], ALU.add, ALU.mult)
                    p.inc(src_name + "_V2",
                          v.tensor_scalar(xnf[i % 2][:, :], xi[:, :],
                                          mneg[:, g % 2:g % 2 + 1],
                                          rstd[:, g % 2:g % 2 + 1],
                                          ALU.add, ALU.mult))
                    v.wait_ge(sT0, b_T0 + i + 1)
                    if g == 0 and i >= 4:
                        v.wait_ge(sSTG, b_STG + 16 * (i // 4))
                    for c in range(NDC):
                        ins = v.tensor_copy(stage[:, c * 512 + g * 128:
                                                  c * 512 + g * 128 + 128],
                                            pt[c][:, :])
                        if c == NDC - 1:
                            p.inc(src_name + "_V3", ins)

            @b.scalar
            def _(sc):
                for i in range(NTT):
                    g = i % 4
                    xi = xt[i % 2]
                    if merge is not None:
                        sc.wait_ge(p.sem(src_name + "_MG"), b_MG + 32 * (i + 1))
                        if i >= 1:
                            sc.wait_ge(p.sem(src_name + "_MV"), b_MV + i)
                        p.inc(src_name + "_MS", sc.activation(
                            gt_m[:, :], ft_m[i % 2][:, :], AF.Silu))
                    sc.wait_ge(sV0, b_V0 + i + 1)
                    p.inc(src_name + "_S0",
                          sc.activation(sq[:, :], xi[:, :], AF.Square,
                                        bias=mneg[:, g % 2:g % 2 + 1]))
                    sc.wait_ge(sV1, b_V1 + i + 1)
                    sc.activation(var[:, g % 2:g % 2 + 1],
                                  var[:, g % 2:g % 2 + 1], AF.Ln,
                                  scale=1.0 / D, bias=EPS)
                    p.inc(src_name + "_S1",
                          sc.activation(rstd[:, g % 2:g % 2 + 1],
                                        var[:, g % 2:g % 2 + 1], AF.Exp,
                                        scale=-0.5))

            @b.tensor
            def _(t):
                for i in range(NTT):
                    t.wait_ge(sV2, b_V2 + i + 1)
                    if i >= 1:
                        t.wait_ge(sV3, b_V3 + i)
                    for c in range(NDC):
                        ins = t.transpose(pt[c][:, :], xnb[:, c * 128:(c + 1) * 128],
                                          idtile[:, :])
                        if c == NDC - 1:
                            p.inc(src_name + "_T0", ins)

            @b.gpsimd
            def _(g):
                for i in range(NTT):
                    if merge is not None:
                        if i >= 2:
                            g.wait_ge(sV2, b_V2 + i - 1)
                        p.inc(src_name + "_MG", g.dma_start(
                            out=st_m[i % 2][:, :],
                            in_=s_dram.ap()[i * 128:(i + 1) * 128, :]), 16)
                        p.inc(src_name + "_MG", g.dma_start(
                            out=ft_m[i % 2][:, :],
                            in_=f_dram.ap()[i * 128:(i + 1) * 128, :]), 16)
                    if curf is not None:
                        g.wait_ge(sV2, b_V2 + i + 1)
                        p.inc(src_name + "_STC",
                              g.dma_start(out=curf.ap()[i * 128:(i + 1) * 128, :],
                                          in_=xnf[i % 2][:, :]), 16)
                    if i % 4 == 3:
                        g.wait_ge(sV3, b_V3 + i + 1)
                        tb = i // 4
                        dst = bass.AP(xnT, tb * (NDC * 128 * 512),
                                      [[512, 128], [128 * 512, NDC], [1, 512]])
                        p.inc(src_name + "_STG",
                              g.dma_start(out=dst, in_=stage[:, :]), 16)
                g.wait_ge(sSTG, p.cnt[src_name + "_STG"])
                if curf is not None:
                    g.wait_ge(sSTC, p.cnt[src_name + "_STC"])

    phase_barrier(p)


def wchunks_ap(w, rows, cols, row0=0, nchunk=None, ecols=None):
    ecols = ecols if ecols is not None else cols
    nchunk = nchunk if nchunk is not None else rows // 128
    return bass.AP(w, row0 * cols, [[cols, 128], [128 * cols, nchunk], [1, ecols]])


def bias_chunks_ap(bvec, n, off=0):
    return bass.AP(bvec, off, [[1, 128], [128, n], [1, 1]])


def row_bcast_ap(bvec, n, npart=128):
    return bass.AP(bvec, 0, [[0, npart], [1, n]])


def xnT_load_ap(xnT):
    return bass.AP(xnT, 0, [[512, 128], [128 * 512, NDC], [NDC * 128 * 512, NTB],
                            [1, 512]])


def emit_attn_proj(p, nm, xnT_name, wq, bq, wk, bk, wv, bv,
                   wr=None, br=None, R_sb=None, qscale=0.125):
    nc = p.nc
    xnT = p.dram[xnT_name]
    QT = p.scratch(nm + "_QT", [H, DQ, T], BF16)
    KT = p.scratch(nm + "_KT", [H, DQ, T], BF16)
    Vs = p.scratch(nm + "_V", [NTT, 128, D], BF16)

    sG = p.sem(nm + "_G")
    sL = p.sem(nm + "_L")
    sT = p.sem(nm + "_T")
    sE = p.sem(nm + "_E")
    sEV = p.sem(nm + "_EV")
    sST = p.sem(nm + "_ST")
    bG, bL, bT, bE, bEV, bST = (p.cnt[nm + k] for k in
                                ("_G", "_L", "_T", "_E", "_EV", "_ST"))

    with ExitStack() as scope:
        e = scope.enter_context
        xnT_sb = e(nc.sbuf_tensor(nm + "_xnT", [128, NDC * T], BF16))
        Wq_sb = e(nc.sbuf_tensor(nm + "_Wq", [128, NDC * D], BF16))
        Wk_sb = e(nc.sbuf_tensor(nm + "_Wk", [128, NDC * D], BF16))
        Wv_sb = e(nc.sbuf_tensor(nm + "_Wv", [128, NDC * D], BF16))
        bq_sb = e(nc.sbuf_tensor(nm + "_bq", [128, NDC], F32))
        bk_sb = e(nc.sbuf_tensor(nm + "_bk", [128, NDC], F32)) \
            if bk is not None else None
        bv_sb = e(nc.sbuf_tensor(nm + "_bv", [128, D], F32)) \
            if bv is not None else None
        Wr_sb = e(nc.sbuf_tensor(nm + "_Wr", [128, NDC * H * NB], BF16)) \
            if wr is not None else None
        br_sb = e(nc.sbuf_tensor(nm + "_br", [128, H * NB], F32)) \
            if br is not None else None
        stg = [e(nc.sbuf_tensor(f"{nm}_stg{i}", [128, 512], BF16)) for i in range(4)]
        ps = [e(nc.psum_tensor(f"{nm}_ps{i}", [128, 512], F32)) for i in range(4)]
        psR = [e(nc.psum_tensor(f"{nm}_psR{i}", [128, H * NB], F32))
               for i in range(2)] if wr is not None else None

        jobs = []
        for qc in range(NDC):
            for tb in range(NTB):
                jobs.append(("q", qc, tb))
        for qc in range(NDC):
            for tb in range(NTB):
                jobs.append(("k", qc, tb))
        for tt in range(NTT):
            for vb in range(2):
                jobs.append(("v", tt, vb))
        njq = len(jobs)
        if wr is not None:
            for tt in range(NTT):
                jobs.append(("r", tt, 0))
        ev_engine = []
        for kind, a1, a2 in jobs:
            if kind in ("q", "k"):
                ev_engine.append("sc")
            elif kind == "v":
                ev_engine.append("v" if bv is not None else "sc")
            else:
                ev_engine.append("v")
        ev_cnt = []
        nsc = nv = 0
        for engn in ev_engine:
            if engn == "sc":
                nsc += 1
                ev_cnt.append((sE, bE + nsc))
            else:
                nv += 1
                ev_cnt.append((sEV, bEV + nv))
        nloads_g = NDC + 3 + (1 if wr is not None else 0)
        nloads_l = 1 + (bk is not None) + (bv is not None) + (br is not None)

        with nc.Block() as b:
            @b.gpsimd
            def _(g):
                for c in range(NDC):
                    src = bass.AP(xnT, c * 128 * 512,
                                  [[512, 128], [NDC * 128 * 512, NTB], [1, 512]])
                    p.inc(nm + "_G",
                          g.dma_start(out=xnT_sb[:, c * T:(c + 1) * T], in_=src),
                          16)
                p.inc(nm + "_G", g.dma_start(out=Wq_sb[:, :],
                                             in_=wchunks_ap(wq, D, D)), 16)
                p.inc(nm + "_G", g.dma_start(out=Wk_sb[:, :],
                                             in_=wchunks_ap(wk, D, D)), 16)
                p.inc(nm + "_G", g.dma_start(out=Wv_sb[:, :],
                                             in_=wchunks_ap(wv, D, D)), 16)
                if wr is not None:
                    p.inc(nm + "_G", g.dma_start(out=Wr_sb[:, :],
                                                 in_=wchunks_ap(wr, D, H * NB)), 16)

            @b.sync
            def _(s):
                p.inc(nm + "_L", s.dma_start(out=bq_sb[:, :],
                                             in_=bias_chunks_ap(bq, NDC),
                                             allow_slow_non_contiguous=True), 16)
                if bk is not None:
                    p.inc(nm + "_L", s.dma_start(out=bk_sb[:, :],
                                                 in_=bias_chunks_ap(bk, NDC),
                                                 allow_slow_non_contiguous=True), 16)
                if bv is not None:
                    p.inc(nm + "_L", s.dma_start(out=bv_sb[:, :],
                                                 in_=row_bcast_ap(bv, D)), 16)
                if br is not None:
                    p.inc(nm + "_L", s.dma_start(out=br_sb[:, :],
                                                 in_=row_bcast_ap(br, H * NB)), 16)
                for j, (kind, a1, a2) in enumerate(jobs):
                    if kind == "r":
                        continue
                    sem, cntv = ev_cnt[j]
                    s.wait_ge(sem, cntv)
                    if kind in ("q", "k"):
                        dst_t = QT if kind == "q" else KT
                        dst = bass.AP(dst_t, a1 * 128 * T + a2 * 512,
                                      [[T, 128], [1, 512]])
                    else:
                        dst = bass.AP(Vs, a1 * 128 * D + a2 * 512,
                                      [[D, 128], [1, 512]])
                    p.inc(nm + "_ST",
                          s.dma_start(out=dst, in_=stg[j % 4][:, :]), 16)
                s.wait_ge(sST, p.cnt[nm + "_ST"])

            @b.tensor
            def _(t):
                t.wait_ge(sG, bG + 16 * nloads_g)
                for j, (kind, a1, a2) in enumerate(jobs):
                    if kind == "r":
                        jr = j - njq
                        if jr >= 2:
                            sem, cntv = ev_cnt[njq + jr - 2]
                            t.wait_ge(sem, cntv)
                        pp = psR[jr % 2]
                        for dc in range(NDC):
                            ins = t.matmul(
                                pp[:, :],
                                xnT_sb[:, dc * T + a1 * 128:dc * T + a1 * 128 + 128],
                                Wr_sb[:, dc * H * NB:(dc + 1) * H * NB],
                                start=(dc == 0), stop=(dc == NDC - 1))
                        p.inc(nm + "_T", ins)
                        continue
                    if j >= 4:
                        sem, cntv = ev_cnt[j - 4]
                        t.wait_ge(sem, cntv)
                    pp = ps[j % 4]
                    for dc in range(NDC):
                        if kind == "q":
                            lhsT = Wq_sb[:, dc * D + a1 * 128:dc * D + a1 * 128 + 128]
                            rhs = xnT_sb[:, dc * T + a2 * 512:dc * T + a2 * 512 + 512]
                        elif kind == "k":
                            lhsT = Wk_sb[:, dc * D + a1 * 128:dc * D + a1 * 128 + 128]
                            rhs = xnT_sb[:, dc * T + a2 * 512:dc * T + a2 * 512 + 512]
                        else:
                            lhsT = xnT_sb[:, dc * T + a1 * 128:dc * T + a1 * 128 + 128]
                            rhs = Wv_sb[:, dc * D + a2 * 512:dc * D + a2 * 512 + 512]
                        ins = t.matmul(pp[:, :], lhsT, rhs,
                                       start=(dc == 0), stop=(dc == NDC - 1))
                    p.inc(nm + "_T", ins)

            @b.scalar
            def _(sc):
                sc.wait_ge(sL, bL + 16 * nloads_l)
                sc.wait_ge(p.sem(nm + "_Vq"), p.cnt[nm + "_Vq"])
                for j, (kind, a1, a2) in enumerate(jobs):
                    if ev_engine[j] != "sc":
                        continue
                    sc.wait_ge(sT, bT + j + 1)
                    if j >= 4:
                        sc.wait_ge(sST, bST + 16 * (j - 3))
                    if kind == "q":
                        ins = sc.activation(stg[j % 4][:, :], ps[j % 4][:, :],
                                            AF.Identity, scale=qscale,
                                            bias=bq_sb[:, a1:a1 + 1])
                    elif kind == "k" and bk is not None:
                        ins = sc.activation(stg[j % 4][:, :], ps[j % 4][:, :],
                                            AF.Identity, bias=bk_sb[:, a1:a1 + 1])
                    else:
                        ins = sc.copy(stg[j % 4][:, :], ps[j % 4][:, :])
                    p.inc(nm + "_E", ins)

            @b.vector
            def _(v):
                v.wait_ge(sL, bL + 16)
                p.inc(nm + "_Vq", v.tensor_scalar(bq_sb[:, :], bq_sb[:, :],
                                                  qscale, None, ALU.mult))
                v.wait_ge(sL, bL + 16 * nloads_l)
                for j, (kind, a1, a2) in enumerate(jobs):
                    if ev_engine[j] != "v":
                        continue
                    v.wait_ge(sT, bT + j + 1)
                    if kind == "r":
                        jr = j - njq
                        p.inc(nm + "_EV",
                              v.tensor_tensor(
                                  R_sb[:, jr * H * NB:(jr + 1) * H * NB],
                                  psR[jr % 2][:, :], br_sb[:, :], ALU.add))
                    else:
                        if j >= 4:
                            v.wait_ge(sST, bST + 16 * (j - 3))
                        p.inc(nm + "_EV",
                              v.tensor_tensor(stg[j % 4][:, :], ps[j % 4][:, :],
                                              bv_sb[:, a2 * 512:a2 * 512 + 512],
                                              ALU.add))

    phase_barrier(p)


def emit_l0_attn(p, consts, win):
    nc = p.nc
    QT = p.dram["p1_QT"]
    KT = p.dram["p1_KT"]
    Vs = p.dram["p1_V"]
    DTin = win["DT"]
    A2T = p.scratch("a0_A2T", [NDC, 128, T], BF16)
    den_sc = p.scratch("a0_den", [1, T], F32)
    id_b = consts["id_b"]
    ones_b = consts["ones_b"]
    negt_b = consts["negt_b"]
    R_sb = consts["R_sb"]

    def ptiles(tb):
        t0 = tb * 512
        lo = max(0, t0 - ML + 1) // 128
        hi = (t0 + 512) // 128
        return list(range(lo, hi))

    tiles_per_tb = [len(ptiles(tb)) for tb in range(NTB)]
    spq = []
    for h in range(H):
        spq += tiles_per_tb
    cum_s = np.cumsum([0] + spq)

    for s in ("a0_G", "a0_GV", "a0_LQ", "a0_Ts", "a0_Se", "a0_Ta", "a0_Sd",
              "a0_Sa", "a0_Gd", "a0_Vr", "a0_Va2", "a0_Ga2", "a0_Vx"):
        p.sem(s)
    B = {s: p.cnt[s] for s in p.cnt}

    def scount(q):
        return B["a0_Ts"] + int(cum_s[q])

    with ExitStack() as scope:
        e = scope.enter_context
        V_h = [e(nc.sbuf_tensor(f"a0_Vh{i}", [128, NTT * DQ], BF16))
               for i in range(2)]
        Bexp = e(nc.sbuf_tensor("a0_Bexp", [128, 9 * 11 * 128], BF16))
        QTh = [e(nc.sbuf_tensor(f"a0_QT{i}", [64, T], BF16)) for i in range(2)]
        KTh = [e(nc.sbuf_tensor(f"a0_KT{i}", [64, T], BF16)) for i in range(2)]
        Rexp = [e(nc.sbuf_tensor(f"a0_Rexp{i}", [128, 4 * 1408], BF16))
                for i in range(2)]
        P_sb = [e(nc.sbuf_tensor(f"a0_P{i}", [128, 512], BF16)) for i in range(4)]
        A_un = e(nc.sbuf_tensor("a0_Aun", [64, T], F32))
        denrow = e(nc.sbuf_tensor("a0_denrow", [1, T], F32))
        denb = e(nc.sbuf_tensor("a0_denb", [128, T], F32))
        A2st = e(nc.sbuf_tensor("a0_A2st", [64, T], BF16))
        psS = [e(nc.psum_tensor(f"a0_psS{i}", [128, 512], F32)) for i in range(4)]
        psA = e(nc.psum_tensor("a0_psA", [64, 512], F32))
        psD = e(nc.psum_tensor("a0_psD", [1, 512], F32))

        with nc.Block() as b:
            def load_vh(g, h):
                if h >= 2:
                    g.wait_ge(p.sem("a0_Ta"), B["a0_Ta"] + int(cum_s[(h - 1) * NTB]))
                vsrc = bass.AP(Vs, h * DQ, [[D, 128], [128 * D, NTT], [1, DQ]])
                p.inc("a0_GV", g.dma_start(out=V_h[h % 2][:, :], in_=vsrc), 16)

            @b.gpsimd
            def _(g):
                for k in range(9):
                    src = bass.AP(DTin, k * 11 * 128 * 128,
                                  [[128, 128], [128 * 128, 11], [1, 128]])
                    p.inc("a0_G", g.dma_start(
                        out=Bexp[:, k * 1408:(k + 1) * 1408], in_=src), 16)
                load_vh(g, 0)
                load_vh(g, 1)
                for h in range(H):
                    if h + 2 < H:
                        load_vh(g, h + 2)
                    g.wait_ge(p.sem("a0_Sd"), B["a0_Sd"] + NTB * (h + 1))
                    p.inc("a0_Gd", g.dma_start(out=den_sc.ap(), in_=denrow[:, :]), 16)
                    g.wait_ge(p.sem("a0_Gd"), B["a0_Gd"] + 32 * h + 16)
                    if h > 0:
                        g.wait_ge(p.sem("a0_Va2"), B["a0_Va2"] + h)
                    p.inc("a0_Gd", g.dma_start(
                        out=denb[:, :],
                        in_=bass.AP(den_sc, 0, [[0, 128], [1, T]])), 16)
                    g.wait_ge(p.sem("a0_Va2"), B["a0_Va2"] + h + 1)
                    dst = bass.AP(A2T, (h // 2) * 128 * T + (h % 2) * 64 * T,
                                  [[T, 64], [1, T]])
                    p.inc("a0_Ga2", g.dma_start(out=dst, in_=A2st[:, :]), 16)
                g.wait_ge(p.sem("a0_Ga2"), p.cnt["a0_Ga2"])

            @b.sync
            def _(s):
                for h in range(H):
                    if h >= 2:
                        s.wait_ge(p.sem("a0_Ts"), scount(8 * (h - 1)))
                    p.inc("a0_LQ", s.dma_start(
                        out=QTh[h % 2][:, :],
                        in_=bass.AP(QT, h * DQ * T, [[T, 64], [1, T]])), 16)
                    p.inc("a0_LQ", s.dma_start(
                        out=KTh[h % 2][:, :],
                        in_=bass.AP(KT, h * DQ * T, [[T, 64], [1, T]])), 16)

            def emit_tail(v, h):
                v.wait_ge(p.sem("a0_Gd"), B["a0_Gd"] + 32 * (h + 1))
                p.inc("a0_Vr", v.reciprocal(denb[:, :], denb[:, :]))
                v.wait_ge(p.sem("a0_Vr"), p.cnt["a0_Vr"])
                v.wait_ge(p.sem("a0_Sa"), B["a0_Sa"] + NTB * (h + 1))
                if h > 0:
                    v.wait_ge(p.sem("a0_Ga2"), B["a0_Ga2"] + 16 * h)
                p.inc("a0_Va2", v.tensor_tensor(A2st[:, :], A_un[:, :],
                                                denb[0:64, :], ALU.mult))

            @b.vector
            def _(v):
                for r in range(2):
                    for j in range(4):
                        v.tensor_copy(Rexp[r][:, j * 1408 + 1280:(j + 1) * 1408],
                                      id_b[:, :])
                for h in range(H):
                    for tb in range(NTB):
                        q = h * NTB + tb
                        if q >= 2:
                            v.wait_ge(p.sem("a0_Ts"), scount(q - 1))
                        for j in range(4):
                            qt = tb * 4 + j
                            for n in range(NB):
                                ins = v.tensor_scalar(
                                    Rexp[q % 2][:, j * 1408 + n * 128:
                                                j * 1408 + (n + 1) * 128],
                                    id_b[:, :],
                                    R_sb[:, qt * H * NB + h * NB + n:
                                         qt * H * NB + h * NB + n + 1],
                                    None, ALU.mult)
                        p.inc("a0_Vx", ins)
                        if tb == 0 and h > 0:
                            emit_tail(v, h - 1)
                emit_tail(v, H - 1)

            def emit_av(t, h, gsl, pt, first, last, q):
                t.wait_ge(p.sem("a0_Se"), B["a0_Se"] + gsl + 1)
                if first:
                    t.wait_ge(p.sem("a0_Sa"), B["a0_Sa"] + q)
                    t.wait_ge(p.sem("a0_Sd"), B["a0_Sd"] + q)
                t.matmul(psA[:, :], V_h[h % 2][:, pt * DQ:(pt + 1) * DQ],
                         P_sb[gsl % 4][:, :], start=first, stop=last)
                ins = t.matmul(psD[:, :], ones_b[:, 0:1], P_sb[gsl % 4][:, :],
                               start=first, stop=last)
                p.inc("a0_Ta", ins)

            @b.tensor
            def _(t):
                t.wait_ge(p.sem("a0_G"), B["a0_G"] + 16 * 9)
                gs = 0
                for h in range(H):
                    t.wait_ge(p.sem("a0_LQ"), B["a0_LQ"] + 32 * (h + 1))
                    t.wait_ge(p.sem("a0_GV"), B["a0_GV"] + 16 * (h + 1))
                    for tb in range(NTB):
                        q = h * NTB + tb
                        t0 = tb * 512
                        t.wait_ge(p.sem("a0_Vx"), B["a0_Vx"] + q + 1)
                        pts = ptiles(tb)
                        for pi, pt in enumerate(pts):
                            if gs >= 4:
                                t.wait_ge(p.sem("a0_Se"), B["a0_Se"] + gs - 3)
                            pp = psS[gs % 4]
                            ops = [(KTh[h % 2][:, pt * 128:(pt + 1) * 128],
                                    QTh[h % 2][:, t0:t0 + 512], None)]
                            for j in range(4):
                                dl = t0 + 128 * j - pt * 128
                                if 0 <= dl <= ML:
                                    ki = dl // 128
                                    for n in range(NB):
                                        ops.append((
                                            Bexp[:, ki * 1408 + n * 128:
                                                 ki * 1408 + (n + 1) * 128],
                                            Rexp[q % 2][:, j * 1408 + n * 128:
                                                        j * 1408 + (n + 1) * 128],
                                            j))
                                    if dl == 0 or dl == ML:
                                        ops.append((
                                            Bexp[:, ki * 1408 + NB * 128:
                                                 ki * 1408 + 1408],
                                            id_b[:, :], j))
                                else:
                                    ops.append((negt_b[:, :], id_b[:, :], j))
                            for oi, (lhsT, rhs, j) in enumerate(ops):
                                out = pp[:, :] if j is None else \
                                    pp[:, j * 128:(j + 1) * 128]
                                ins = t.matmul(out, lhsT, rhs, start=(oi == 0),
                                               stop=(oi == len(ops) - 1))
                            p.inc("a0_Ts", ins)
                            if pi >= 1:
                                emit_av(t, h, gs - 1, pts[pi - 1],
                                        first=(pi == 1), last=False, q=q)
                            gs += 1
                        emit_av(t, h, gs - 1, pts[-1], first=(len(pts) == 1),
                                last=True, q=q)
                assert gs == int(cum_s[-1])

            @b.scalar
            def _(sc):
                gs = 0
                for h in range(H):
                    for tb in range(NTB):
                        q = h * NTB + tb
                        for pi in range(len(ptiles(tb))):
                            sc.wait_ge(p.sem("a0_Ts"), B["a0_Ts"] + gs + 1)
                            if gs >= 4:
                                sc.wait_ge(p.sem("a0_Ta"), B["a0_Ta"] + gs - 3)
                            p.inc("a0_Se", sc.activation(P_sb[gs % 4][:, :],
                                                         psS[gs % 4][:, :], AF.Exp))
                            gs += 1
                        sc.wait_ge(p.sem("a0_Ta"), B["a0_Ta"] + gs)
                        if h > 0 and tb == 0:
                            sc.wait_ge(p.sem("a0_Gd"), B["a0_Gd"] + 32 * h - 16)
                            sc.wait_ge(p.sem("a0_Va2"), B["a0_Va2"] + h)
                        p.inc("a0_Sd", sc.copy(denrow[:, tb * 512:tb * 512 + 512],
                                               psD[:, :]))
                        p.inc("a0_Sa", sc.copy(A_un[:, tb * 512:tb * 512 + 512],
                                               psA[:, :]))

    phase_barrier(p)


def emit_l1_attn(p, consts, win):
    nc = p.nc
    QT = p.dram["p5_QT"]
    KT = p.dram["p5_KT"]
    Vs = p.dram["p5_V"]
    DTin = win["DT"]
    A2T = p.scratch("a1_A2T", [NDC, 128, T], BF16)
    den_sc = p.scratch("a1_den", [1, T], F32)
    id_b = consts["id_b"]
    ones_b = consts["ones_b"]
    negt_b = consts["negt_b"]

    def ptiles(tb):
        return list(range(0, tb * 4 + 4))

    tiles_per_tb = [len(ptiles(tb)) for tb in range(NTB)]
    spq = []
    for h in range(H):
        spq += tiles_per_tb
    cum_s = np.cumsum([0] + spq)

    for s in ("a1_G", "a1_GV", "a1_LQ", "a1_Ts", "a1_Se", "a1_Ta", "a1_Sd",
              "a1_Sa", "a1_Gd", "a1_Vr", "a1_Va2", "a1_Ga2"):
        p.sem(s)
    B = {s: p.cnt[s] for s in p.cnt}

    def scount(q):
        return B["a1_Ts"] + int(cum_s[q])

    with ExitStack() as scope:
        e = scope.enter_context
        V_h = [e(nc.sbuf_tensor(f"a1_Vh{i}", [128, NTT * DQ], BF16))
               for i in range(2)]
        cmask = e(nc.sbuf_tensor("a1_cmask", [128, 128], BF16))
        QTh = [e(nc.sbuf_tensor(f"a1_QT{i}", [64, T], BF16)) for i in range(2)]
        KTh = [e(nc.sbuf_tensor(f"a1_KT{i}", [64, T], BF16)) for i in range(2)]
        P_sb = [e(nc.sbuf_tensor(f"a1_P{i}", [128, 512], BF16)) for i in range(4)]
        A_un = e(nc.sbuf_tensor("a1_Aun", [64, T], F32))
        denrow = e(nc.sbuf_tensor("a1_denrow", [1, T], F32))
        denb = e(nc.sbuf_tensor("a1_denb", [128, T], F32))
        A2st = e(nc.sbuf_tensor("a1_A2st", [64, T], BF16))
        psS = [e(nc.psum_tensor(f"a1_psS{i}", [128, 512], F32)) for i in range(4)]
        psA = e(nc.psum_tensor("a1_psA", [64, 512], F32))
        psD = e(nc.psum_tensor("a1_psD", [1, 512], F32))

        with nc.Block() as b:
            def load_vh(g, h):
                if h >= 2:
                    g.wait_ge(p.sem("a1_Ta"), B["a1_Ta"] + int(cum_s[(h - 1) * NTB]))
                vsrc = bass.AP(Vs, h * DQ, [[D, 128], [128 * D, NTT], [1, DQ]])
                p.inc("a1_GV", g.dma_start(out=V_h[h % 2][:, :], in_=vsrc), 16)

            @b.gpsimd
            def _(g):
                msrc = bass.AP(DTin, NB * 128 * 128, [[128, 128], [1, 128]])
                p.inc("a1_G", g.dma_start(out=cmask[:, :], in_=msrc), 16)
                load_vh(g, 0)
                load_vh(g, 1)
                for h in range(H):
                    if h + 2 < H:
                        load_vh(g, h + 2)
                    g.wait_ge(p.sem("a1_Sd"), B["a1_Sd"] + NTB * (h + 1))
                    p.inc("a1_Gd", g.dma_start(out=den_sc.ap(), in_=denrow[:, :]), 16)
                    g.wait_ge(p.sem("a1_Gd"), B["a1_Gd"] + 32 * h + 16)
                    if h > 0:
                        g.wait_ge(p.sem("a1_Va2"), B["a1_Va2"] + h)
                    p.inc("a1_Gd", g.dma_start(
                        out=denb[:, :],
                        in_=bass.AP(den_sc, 0, [[0, 128], [1, T]])), 16)
                    g.wait_ge(p.sem("a1_Va2"), B["a1_Va2"] + h + 1)
                    dst = bass.AP(A2T, (h // 2) * 128 * T + (h % 2) * 64 * T,
                                  [[T, 64], [1, T]])
                    p.inc("a1_Ga2", g.dma_start(out=dst, in_=A2st[:, :]), 16)
                g.wait_ge(p.sem("a1_Ga2"), p.cnt["a1_Ga2"])

            @b.sync
            def _(s):
                for h in range(H):
                    if h >= 2:
                        s.wait_ge(p.sem("a1_Ts"), scount(8 * (h - 1)))
                    p.inc("a1_LQ", s.dma_start(
                        out=QTh[h % 2][:, :],
                        in_=bass.AP(QT, h * DQ * T, [[T, 64], [1, T]])), 16)
                    p.inc("a1_LQ", s.dma_start(
                        out=KTh[h % 2][:, :],
                        in_=bass.AP(KT, h * DQ * T, [[T, 64], [1, T]])), 16)

            def emit_tail(v, h):
                v.wait_ge(p.sem("a1_Gd"), B["a1_Gd"] + 32 * (h + 1))
                p.inc("a1_Vr", v.reciprocal(denb[:, :], denb[:, :]))
                v.wait_ge(p.sem("a1_Vr"), p.cnt["a1_Vr"])
                v.wait_ge(p.sem("a1_Sa"), B["a1_Sa"] + NTB * (h + 1))
                if h > 0:
                    v.wait_ge(p.sem("a1_Ga2"), B["a1_Ga2"] + 16 * h)
                p.inc("a1_Va2", v.tensor_tensor(A2st[:, :], A_un[:, :],
                                                denb[0:64, :], ALU.mult))

            @b.vector
            def _(v):
                for h in range(H):
                    emit_tail(v, h)

            def emit_av(t, h, gsl, pt, first, last, q):
                t.wait_ge(p.sem("a1_Se"), B["a1_Se"] + gsl + 1)
                if first:
                    t.wait_ge(p.sem("a1_Sa"), B["a1_Sa"] + q)
                    t.wait_ge(p.sem("a1_Sd"), B["a1_Sd"] + q)
                t.matmul(psA[:, :], V_h[h % 2][:, pt * DQ:(pt + 1) * DQ],
                         P_sb[gsl % 4][:, :], start=first, stop=last)
                ins = t.matmul(psD[:, :], ones_b[:, 0:1], P_sb[gsl % 4][:, :],
                               start=first, stop=last)
                p.inc("a1_Ta", ins)

            @b.tensor
            def _(t):
                t.wait_ge(p.sem("a1_G"), B["a1_G"] + 16)
                gs = 0
                for h in range(H):
                    t.wait_ge(p.sem("a1_LQ"), B["a1_LQ"] + 32 * (h + 1))
                    t.wait_ge(p.sem("a1_GV"), B["a1_GV"] + 16 * (h + 1))
                    for tb in range(NTB):
                        q = h * NTB + tb
                        t0 = tb * 512
                        pts = ptiles(tb)
                        for pi, pt in enumerate(pts):
                            if gs >= 4:
                                t.wait_ge(p.sem("a1_Se"), B["a1_Se"] + gs - 3)
                            pp = psS[gs % 4]
                            ops = [(KTh[h % 2][:, pt * 128:(pt + 1) * 128],
                                    QTh[h % 2][:, t0:t0 + 512], None)]
                            for j in range(4):
                                dl = t0 + 128 * j - pt * 128
                                if dl < 0:
                                    ops.append((negt_b[:, :], id_b[:, :], j))
                                elif dl == 0:
                                    ops.append((cmask[:, :], id_b[:, :], j))
                            for oi, (lhsT, rhs, j) in enumerate(ops):
                                out = pp[:, :] if j is None else \
                                    pp[:, j * 128:(j + 1) * 128]
                                ins = t.matmul(out, lhsT, rhs, start=(oi == 0),
                                               stop=(oi == len(ops) - 1))
                            p.inc("a1_Ts", ins)
                            if pi >= 1:
                                emit_av(t, h, gs - 1, pts[pi - 1],
                                        first=(pi == 1), last=False, q=q)
                            gs += 1
                        emit_av(t, h, gs - 1, pts[-1], first=(len(pts) == 1),
                                last=True, q=q)
                assert gs == int(cum_s[-1])

            @b.scalar
            def _(sc):
                gs = 0
                for h in range(H):
                    for tb in range(NTB):
                        q = h * NTB + tb
                        for pi in range(len(ptiles(tb))):
                            sc.wait_ge(p.sem("a1_Ts"), B["a1_Ts"] + gs + 1)
                            if gs >= 4:
                                sc.wait_ge(p.sem("a1_Ta"), B["a1_Ta"] + gs - 3)
                            p.inc("a1_Se", sc.activation(P_sb[gs % 4][:, :],
                                                         psS[gs % 4][:, :], AF.Exp))
                            gs += 1
                        sc.wait_ge(p.sem("a1_Ta"), B["a1_Ta"] + gs)
                        if h > 0 and tb == 0:
                            sc.wait_ge(p.sem("a1_Gd"), B["a1_Gd"] + 32 * h - 16)
                            sc.wait_ge(p.sem("a1_Va2"), B["a1_Va2"] + h)
                        p.inc("a1_Sd", sc.copy(denrow[:, tb * 512:tb * 512 + 512],
                                               psD[:, :]))
                        p.inc("a1_Sa", sc.copy(A_un[:, tb * 512:tb * 512 + 512],
                                               psA[:, :]))

    phase_barrier(p)


def emit_l1_ffn(p, consts, win):
    nc = p.nc
    xnT = p.dram["xn1T"]
    f1 = p.scratch("f1", [T, D], F32)
    wrin = win["L1_ffn_w_router"]
    wup, wgt, wdn = win["L1_ffn_w_up"], win["L1_ffn_w_gate"], win["L1_ffn_w_down"]
    wsu, wsg, wsd = win["L1_ffn_ws_up"], win["L1_ffn_ws_gate"], win["L1_ffn_ws_down"]
    Wub = p.scratch("p7_Wub", [8, NFC // 2, 128, 2, 8, 128], BF16)
    Wgb = p.scratch("p7_Wgb", [8, NFC // 2, 128, 2, 8, 128], BF16)
    Wdb = p.scratch("p7_Wdb", [8, NFC, 128, D], BF16)
    id_f = consts["id_f"]
    NU, NTB4, NSUB = 8, 4, 8

    sems = ["p7_L0", "p7_G0", "p7_LW", "p7_GW", "p7_WB", "p7_Tg", "p7_Sg",
            "p7_Tt", "p7_Vt", "p7_Vg1", "p7_Sg2", "p7_Va", "p7_Vb", "p7_Vg",
            "p7_Th", "p7_Su", "p7_Vh", "p7_Tw", "p7_Ve", "p7_SC", "p7_GX"]
    for s in sems:
        p.sem(s)
    B = {s: p.cnt[s] for s in sems}

    def uw(u):
        if u == 0:
            return (wsg, wsu, wsd, 0)
        return (wgt, wup, wdn, u - 1)

    tgt = {}
    wb_after_u = {u: B["p7_WB"] + 16 * 128 * (u + 1) for u in range(NU)}

    def gH(tb, u):
        return B["p7_Th"] + (tb * NU + u) * 128

    def gSu(tb, u):
        return B["p7_Su"] + (tb * NU + u) * 64

    def gVh(tb, u):
        return B["p7_Vh"] + (tb * NU + u) * 64

    def gW(tb, u, pas):
        return B["p7_Tw"] + ((tb * NU + u) * 2 + pas) * 32

    def gVe(tb, u):
        return B["p7_Ve"] + (tb * NU + u) * 16

    with ExitStack() as scope:
        e_ = scope.enter_context
        xnT_sb = e_(nc.sbuf_tensor("p7_xnT", [128, NDC * T], BF16))
        H_sb = e_(nc.sbuf_tensor("p7_H", [128, NFC * 1024], BF16))
        acc = e_(nc.sbuf_tensor("p7_acc", [128, NSUB * D], F32))
        Gs = [e_(nc.sbuf_tensor(f"p7_Gs{i}", [128, 512], F32)) for i in range(2)]
        Wgsl = [e_(nc.sbuf_tensor(f"p7_Wgsl{i}", [128, 2 * 8 * 128], BF16))
                for i in range(2)]
        Wusl = [e_(nc.sbuf_tensor(f"p7_Wusl{i}", [128, 2 * 8 * 128], BF16))
                for i in range(2)]
        Wdsl = [e_(nc.sbuf_tensor(f"p7_Wdsl{i}", [128, 2 * 512], BF16))
                for i in range(2)]
        wr_sb = e_(nc.sbuf_tensor("p7_wr", [128, NDC * 8], BF16))
        logT8 = [e_(nc.sbuf_tensor(f"p7_lg{i}", [8, 512], F32)) for i in range(2)]
        lgn = e_(nc.sbuf_tensor("p7_lgn", [128, NTT * 8], F32))
        wn = e_(nc.sbuf_tensor("p7_wn", [128, NTT * 8], F32))
        wk = e_(nc.sbuf_tensor("p7_wk", [128, 8 * 32], F32))
        wk2 = e_(nc.sbuf_tensor("p7_wk2", [128, 16 * 32], F32))
        exio = e_(nc.sbuf_tensor("p7_exio", [128, 18 * 32], F32))
        exchk = e_(nc.sbuf_tensor("p7_exchk", [128, 32], F32))
        ps = [e_(nc.psum_tensor(f"p7_ps{i}", [128, 512], F32)) for i in range(8)]

        def W(i):
            return wk[:, i * 32:(i + 1) * 32]

        def lgn_e(t, eix):
            return bass.AP(t, eix, [[NTT * 8, 128], [8, NTT]])

        with nc.Block() as b:
            @b.sync
            def _(s):
                for tb in range(NTB4):
                    for u in range(NU):
                        if tb == 0:
                            continue
                        if tb == 1:
                            s.wait_ge(p.sem("p7_WB"), wb_after_u[u])
                        for fb in range(NFC // 2):
                            for kind, Wb_t, sl in (("wg", Wgb, Wgsl),
                                                   ("wu", Wub, Wusl)):
                                tv = gH(tb, u) + 8 * (fb - 1)
                                if tv > B["p7_Th"]:
                                    s.wait_ge(p.sem("p7_Th"), tv)
                                srcw = bass.AP(Wb_t, (u * 16 + fb) * 128 * 2048,
                                               [[2048, 128], [1, 2048]])
                                ins = s.dma_start(out=sl[fb % 2][:, :], in_=srcw)
                                p.inc("p7_LW", ins, 16)
                                for fc in range(2 * fb, 2 * fb + 2):
                                    tgt[(tb, u, kind, fc)] =                                         ("p7_LW", p.cnt["p7_LW"])
                        for pas in range(2):
                            for fb in range(NFC // 2):
                                tv = gW(tb, u, pas) + 2 * (fb - 1)
                                if tv > B["p7_Tw"]:
                                    s.wait_ge(p.sem("p7_Tw"), tv)
                                srcw = bass.AP(
                                    Wdb, ((u * NFC) + 2 * fb) * 128 * D + pas * 512,
                                    [[D, 128], [128 * D, 2], [1, 512]])
                                ins = s.dma_start(out=Wdsl[fb % 2][:, :], in_=srcw)
                                p.inc("p7_LW", ins, 16)
                                for fc in range(2 * fb, 2 * fb + 2):
                                    tgt[(tb, u, "wd", fc, pas)] =                                         ("p7_LW", p.cnt["p7_LW"])

            @b.gpsimd
            def _(g):
                for c in range(NDC):
                    srcx = bass.AP(xnT, c * 128 * 512,
                                   [[512, 128], [NDC * 128 * 512, NTB], [1, 512]])
                    p.inc("p7_G0", g.dma_start(out=xnT_sb[:, c * T:(c + 1) * T],
                                               in_=srcx), 16)
                p.inc("p7_G0", g.dma_start(
                    out=wr_sb[:, :],
                    in_=bass.AP(wrin, 0, [[8, 128], [1024, NDC], [1, 8]]),
                    allow_slow_non_contiguous=True), 16)
                tb = 0
                for u in range(NU):
                    wg_d, wu_d, wd_d, ei = uw(u)
                    for fc in range(NFC):
                        fb, fq = fc // 2, fc % 2
                        for kind, w_d, Wb_t, sl in (("wg", wg_d, Wgb, Wgsl),
                                                    ("wu", wu_d, Wub, Wusl)):
                            tv = gH(tb, u) + 8 * (fb - 1)
                            if tv > B["p7_Th"]:
                                g.wait_ge(p.sem("p7_Th"), tv)
                            srcw = bass.AP(w_d, ei * D * FF + fc * 128,
                                           [[FF, 128], [128 * FF, 8], [1, 128]])
                            dst = sl[fb % 2][:, fq * 1024:(fq + 1) * 1024]
                            p.inc("p7_GW", g.dma_start(out=dst, in_=srcw), 16)
                            tgt[(tb, u, kind, fc)] = ("p7_GW", p.cnt["p7_GW"])
                            g.wait_ge(p.sem("p7_GW"), p.cnt["p7_GW"])
                            dstw = bass.AP(Wb_t,
                                           (u * 16 + fb) * 128 * 2048 + fq * 1024,
                                           [[2048, 128], [128, 8], [1, 128]])
                            p.inc("p7_WB", g.dma_start(out=dstw, in_=dst), 16)
                    for pas in range(2):
                        for fc in range(NFC):
                            fb, fq = fc // 2, fc % 2
                            tv = gW(tb, u, pas) + 2 * (fb - 1)
                            if tv > B["p7_Tw"]:
                                g.wait_ge(p.sem("p7_Tw"), tv)
                            srcw = bass.AP(wd_d, ei * FF * D + fc * 128 * D +
                                           pas * 512, [[D, 128], [1, 512]])
                            dst = Wdsl[fb % 2][:, fq * 512:(fq + 1) * 512]
                            p.inc("p7_GW", g.dma_start(out=dst, in_=srcw), 16)
                            tgt[(tb, u, "wd", fc, pas)] =                                 ("p7_GW", p.cnt["p7_GW"])
                            g.wait_ge(p.sem("p7_GW"), p.cnt["p7_GW"])
                            dstw = bass.AP(Wdb, ((u * NFC) + fc) * 128 * D +
                                           pas * 512, [[D, 128], [1, 512]])
                            p.inc("p7_WB", g.dma_start(out=dstw, in_=dst), 16)
                for tb in range(NTB4):
                    g.wait_ge(p.sem("p7_Ve"), gVe(tb, NU - 1) + 16)
                    dstc = bass.AP(f1, tb * 1024 * D,
                                   [[D, 128], [128 * D, NSUB], [1, D]])
                    p.inc("p7_SC", g.dma_start(out=dstc, in_=acc[:, :]), 16)
                g.wait_ge(p.sem("p7_SC"), p.cnt["p7_SC"])
                g.wait_ge(p.sem("p7_WB"), p.cnt["p7_WB"])

            @b.scalar
            def _(sc):
                for tb8 in range(NTB):
                    sc.wait_ge(p.sem("p7_Tg"), B["p7_Tg"] + tb8 + 1)
                    if tb8 >= 2:
                        sc.wait_ge(p.sem("p7_Tt"), B["p7_Tt"] + tb8 - 1)
                    p.inc("p7_Sg", sc.copy(logT8[tb8 % 2][:, :], ps[0][0:8, :]))
                sc.wait_ge(p.sem("p7_Vg1"), B["p7_Vg1"] + 1)
                sc.activation(exio[:, 288:576], exio[:, 0:288], AF.Exp)
                p.inc("p7_Sg2", sc.copy(exchk[:, :], exio[:, 544:576]))
                for tb in range(NTB4):
                    for u in range(NU):
                        for fc in range(NFC):
                            for th in range(2):
                                k = fc * 2 + th
                                sc.wait_ge(p.sem("p7_Th"), gH(tb, u) + 2 * k + 1)
                                tv = gVh(tb, u) + k - 1
                                if tv > B["p7_Vh"]:
                                    sc.wait_ge(p.sem("p7_Vh"), tv)
                                p.inc("p7_Su", sc.activation(
                                    Gs[k % 2][:, :], ps[(2 * k) % 4][:, :],
                                    AF.Silu))

            @b.tensor
            def _(t):
                t.wait_ge(p.sem("p7_G0"), B["p7_G0"] + 16 * (NDC + 1))
                for tb8 in range(NTB):
                    if tb8 >= 1:
                        t.wait_ge(p.sem("p7_Sg"), B["p7_Sg"] + tb8)
                    for dc in range(NDC):
                        ins = t.matmul(
                            ps[0][0:8, :],
                            wr_sb[:, dc * 8:(dc + 1) * 8],
                            xnT_sb[:, dc * T + tb8 * 512:dc * T + tb8 * 512 + 512],
                            start=(dc == 0), stop=(dc == NDC - 1))
                    p.inc("p7_Tg", ins)
                    t.wait_ge(p.sem("p7_Sg"), B["p7_Sg"] + tb8 + 1)
                    if tb8 >= 1:
                        t.wait_ge(p.sem("p7_Vt"), B["p7_Vt"] + tb8)
                    for c4 in range(4):
                        ins = t.transpose(ps[1][:, c4 * 8:c4 * 8 + 8],
                                          logT8[tb8 % 2][0:8,
                                                         c4 * 128:(c4 + 1) * 128],
                                          id_f[0:8, 0:8])
                    p.inc("p7_Tt", ins)
                for tb in range(NTB4):
                    for u in range(NU):
                        for fc in range(NFC):
                            for kind in ("wg", "wu"):
                                sem, val = tgt[(tb, u, kind, fc)]
                                t.wait_ge(p.sem(sem), val)
                            for th in range(2):
                                k = fc * 2 + th
                                for ki, sl in enumerate((Wgsl, Wusl)):
                                    pp = ps[(2 * k + ki) % 4]
                                    gk = 2 * k + ki
                                    if gk <= 3:
                                        if (tb, u) == (0, 0):
                                            if gk <= 1:
                                                t.wait_ge(p.sem("p7_Vt"),
                                                          B["p7_Vt"] + NTB)
                                        else:
                                            t.wait_ge(p.sem("p7_Ve"),
                                                      gVe(tb, u))
                                    if ki == 0 and k >= 2:
                                        t.wait_ge(p.sem("p7_Su"),
                                                  gSu(tb, u) + k - 1)
                                    if ki == 1 and k >= 2:
                                        t.wait_ge(p.sem("p7_Vh"),
                                                  gVh(tb, u) + k - 1)
                                    fb, fq = fc // 2, fc % 2
                                    for dc in range(NDC):
                                        ins = t.matmul(
                                            pp[:, :],
                                            sl[fb % 2][:, fq * 1024 + dc * 128:
                                                       fq * 1024 + (dc + 1) * 128],
                                            xnT_sb[:, dc * T + tb * 1024 +
                                                   th * 512:dc * T + tb * 1024 +
                                                   th * 512 + 512],
                                            start=(dc == 0), stop=(dc == NDC - 1))
                                    p.inc("p7_Th", ins)
                        t.wait_ge(p.sem("p7_Vh"), gVh(tb, u) + 64)
                        if u == 0:
                            t.wait_ge(p.sem("p7_Vg"), B["p7_Vg"] + 1)
                        for pas in range(2):
                            t.wait_ge(p.sem("p7_Ve"), gVe(tb, u) + 8 * pas)
                            for fc in range(NFC):
                                sem, val = tgt[(tb, u, "wd", fc, pas)]
                                t.wait_ge(p.sem(sem), val)
                                fb, fq = fc // 2, fc % 2
                                for tt in range(NSUB):
                                    ins = t.matmul(
                                        ps[tt][:, :],
                                        H_sb[:, fc * 1024 + tt * 128:
                                             fc * 1024 + (tt + 1) * 128],
                                        Wdsl[fb % 2][:, fq * 512:(fq + 1) * 512],
                                        start=(fc == 0), stop=(fc == NFC - 1))
                                p.inc("p7_Tw", ins)

            @b.vector
            def _(v):
                for tb8 in range(NTB):
                    v.wait_ge(p.sem("p7_Tt"), B["p7_Tt"] + tb8 + 1)
                    p.inc("p7_Vt", v.tensor_copy(
                        lgn[:, tb8 * 32:(tb8 + 1) * 32], ps[1][:, 0:32]))
                c = [lgn_e(lgn, e) for e in range(8)]
                for i in range(4):
                    v.tensor_tensor(W(i), c[2 * i], c[2 * i + 1], ALU.max)
                v.tensor_tensor(W(4), W(0), W(1), ALU.max)
                v.tensor_tensor(W(5), W(2), W(3), ALU.max)
                v.tensor_tensor(W(6), W(4), W(5), ALU.max)
                for e in range(8):
                    v.tensor_tensor(exio[:, (e + 1) * 32:(e + 2) * 32],
                                    c[e], W(6), ALU.subtract)
                v.tensor_copy(exio[:, 0:32], W(6))
                p.inc("p7_Vg1", v.tensor_scalar(W(7), exio[:, 256:288], 1.0,
                                                None, ALU.mult))
                v.wait_ge(p.sem("p7_Sg2"), B["p7_Sg2"] + 1)
                for i in range(4):
                    v.tensor_tensor(wk2[:, i * 32:(i + 1) * 32],
                                    exio[:, (10 + 2 * i) * 32:(11 + 2 * i) * 32],
                                    exio[:, (11 + 2 * i) * 32:(12 + 2 * i) * 32],
                                    ALU.add)
                v.tensor_tensor(wk2[:, 4 * 32:5 * 32], wk2[:, 0:32],
                                wk2[:, 32:64], ALU.add)
                v.tensor_tensor(wk2[:, 5 * 32:6 * 32], wk2[:, 64:96],
                                wk2[:, 96:128], ALU.add)
                p.inc("p7_Va", v.tensor_tensor(wk2[:, 6 * 32:7 * 32],
                                               wk2[:, 4 * 32:5 * 32],
                                               wk2[:, 5 * 32:6 * 32], ALU.add))
                v.wait_ge(p.sem("p7_Va"), p.cnt["p7_Va"])
                p.inc("p7_Vb", v.reciprocal(wk2[:, 7 * 32:8 * 32],
                                            wk2[:, 6 * 32:7 * 32]))
                v.wait_ge(p.sem("p7_Vb"), p.cnt["p7_Vb"])
                v.tensor_scalar(wk2[:, 8 * 32:9 * 32], wk2[:, 7 * 32:8 * 32],
                                2.0, None, ALU.mult)
                for e in range(7):
                    v.tensor_tensor(wk2[:, (9 + e) * 32:(10 + e) * 32],
                                    c[e], W(6), ALU.is_equal)
                for e in range(7):
                    ins = v.tensor_tensor(lgn_e(wn, e),
                                          wk2[:, (9 + e) * 32:(10 + e) * 32],
                                          wk2[:, 8 * 32:9 * 32], ALU.mult)
                p.inc("p7_Vg", ins)
                for tb in range(NTB4):
                    for u in range(NU):
                        for fc in range(NFC):
                            for th in range(2):
                                k = fc * 2 + th
                                v.wait_ge(p.sem("p7_Th"), gH(tb, u) + 2 * k + 2)
                                v.wait_ge(p.sem("p7_Su"), gSu(tb, u) + k + 1)
                                if k <= 1 and (tb, u) != (0, 0):
                                    v.wait_ge(p.sem("p7_Tw"), gW(tb, u, 0))
                                p.inc("p7_Vh", v.tensor_tensor(
                                    H_sb[:, fc * 1024 + th * 512:
                                         fc * 1024 + th * 512 + 512],
                                    ps[(2 * k + 1) % 4][:, :], Gs[k % 2][:, :],
                                    ALU.mult))
                        for pas in range(2):
                            v.wait_ge(p.sem("p7_Tw"), gW(tb, u, pas) + NFC)
                            for tt in range(NSUB):
                                sl = acc[:, tt * D + pas * 512:
                                         tt * D + pas * 512 + 512]
                                if u == 0:
                                    if (tt == 0 and pas == 0 and tb > 0):
                                        v.wait_ge(p.sem("p7_SC"),
                                                  B["p7_SC"] + 16 * tb)
                                    ins = v.tensor_copy(sl, ps[tt][:, :])
                                else:
                                    wcol = wn[:, (tb * NSUB + tt) * 8 + (u - 1):
                                              (tb * NSUB + tt) * 8 + u]
                                    ins = v.scalar_tensor_tensor(
                                        sl, ps[tt][:, :], wcol, sl,
                                        ALU.mult, ALU.add)
                                p.inc("p7_Ve", ins)

    phase_barrier(p)


def emit_final(p, y_out):
    nc = p.nc
    x1 = p.dram["x1"]
    s1 = p.dram["s1"]
    f1 = p.dram["f1"]
    for s in ("p8_L", "p8_S", "p8_V", "p8_O"):
        p.sem(s)
    B = {s: p.cnt[s] for s in ("p8_L", "p8_S", "p8_V", "p8_O")}
    with ExitStack() as scope:
        e = scope.enter_context
        xt = [e(nc.sbuf_tensor(f"p8_x{i}", [128, D], F32)) for i in range(2)]
        st = [e(nc.sbuf_tensor(f"p8_s{i}", [128, D], F32)) for i in range(2)]
        ft = [e(nc.sbuf_tensor(f"p8_f{i}", [128, D], F32)) for i in range(2)]
        gt = e(nc.sbuf_tensor("p8_g", [128, D], F32))
        ot = [e(nc.sbuf_tensor(f"p8_o{i}", [128, D], F32)) for i in range(2)]
        with nc.Block() as b:
            @b.sync
            def _(s):
                for i in range(NTT):
                    if i >= 2:
                        s.wait_ge(p.sem("p8_V"), B["p8_V"] + i - 1)
                    for src, dstt in ((x1, xt), (s1, st), (f1, ft)):
                        p.inc("p8_L", s.dma_start(
                            out=dstt[i % 2][:, :],
                            in_=src.ap()[i * 128:(i + 1) * 128, :]), 16)

            @b.scalar
            def _(sc):
                for i in range(NTT):
                    sc.wait_ge(p.sem("p8_L"), B["p8_L"] + 48 * (i + 1))
                    if i >= 1:
                        sc.wait_ge(p.sem("p8_V"), B["p8_V"] + i)
                    p.inc("p8_S", sc.activation(gt[:, :], ft[i % 2][:, :],
                                                AF.Silu))

            @b.vector
            def _(v):
                for i in range(NTT):
                    v.wait_ge(p.sem("p8_S"), B["p8_S"] + i + 1)
                    v.tensor_tensor(gt[:, :], gt[:, :], st[i % 2][:, :], ALU.mult)
                    if i >= 2:
                        v.wait_ge(p.sem("p8_O"), B["p8_O"] + 16 * (i - 1))
                    p.inc("p8_V", v.tensor_tensor(ot[i % 2][:, :], xt[i % 2][:, :],
                                                  gt[:, :], ALU.add))

            @b.gpsimd
            def _(g):
                for i in range(NTT):
                    g.wait_ge(p.sem("p8_V"), B["p8_V"] + i + 1)
                    p.inc("p8_O", g.dma_start(
                        out=y_out.ap()[i * 128:(i + 1) * 128, :],
                        in_=ot[i % 2][:, :]), 16)
                g.wait_ge(p.sem("p8_O"), p.cnt["p8_O"])


def emit_proj_nat(p, nm, inT_name, w, brow, out_name, post_ln=None,
                  inT_layout="chunks"):
    nc = p.nc
    inT = p.scratch(inT_name, [NDC, 128, T], BF16)
    out = p.scratch(out_name, [T, D], F32)
    for s in (nm + "_G", nm + "_L", nm + "_T", nm + "_E", nm + "_ST",
              nm + "_V0", nm + "_S0", nm + "_V1", nm + "_S1", nm + "_V0a",
              nm + "_V0b"):
        p.sem(s)
    B = {s: p.cnt[s] for s in p.cnt}

    with ExitStack() as scope:
        e = scope.enter_context
        inT_sb = e(nc.sbuf_tensor(nm + "_inT", [128, NDC * T], BF16))
        W_sb = e(nc.sbuf_tensor(nm + "_W", [128, NDC * D], BF16))
        brow_sb = e(nc.sbuf_tensor(nm + "_brow", [128, D], F32))
        yt = [e(nc.sbuf_tensor(f"{nm}_yt{i}", [128, D], F32)) for i in range(2)]
        ps = [e(nc.psum_tensor(f"{nm}_ps{i}", [128, 512], F32)) for i in range(4)]
        if post_ln is not None:
            gam_sb = e(nc.sbuf_tensor(nm + "_gam", [128, D], F32))
            bet_sb = e(nc.sbuf_tensor(nm + "_bet", [128, D], F32))
            sq = e(nc.sbuf_tensor(nm + "_sq", [128, D], F32))
            mneg = e(nc.sbuf_tensor(nm + "_mneg", [128, 2], F32))
            var = e(nc.sbuf_tensor(nm + "_var", [128, 2], F32))
            rstd = e(nc.sbuf_tensor(nm + "_rstd", [128, 2], F32))

        with nc.Block() as b:
            @b.gpsimd
            def _(g):
                for c in range(NDC):
                    if inT_layout == "chunks":
                        src = bass.AP(inT, c * 128 * T, [[T, 128], [1, T]])
                    else:
                        src = bass.AP(inT, c * 128 * 512,
                                      [[512, 128], [NDC * 128 * 512, NTB],
                                       [1, 512]])
                    p.inc(nm + "_G",
                          g.dma_start(out=inT_sb[:, c * T:(c + 1) * T], in_=src), 16)
                p.inc(nm + "_G", g.dma_start(out=W_sb[:, :],
                                             in_=wchunks_ap(w, D, D)), 16)

            @b.sync
            def _(s):
                p.inc(nm + "_L", s.dma_start(out=brow_sb[:, :],
                                             in_=row_bcast_ap(brow, D)), 16)
                if post_ln is not None:
                    gamma, beta = post_ln
                    p.inc(nm + "_L", s.dma_start(out=gam_sb[:, :],
                                                 in_=row_bcast_ap(gamma, D)), 16)
                    p.inc(nm + "_L", s.dma_start(out=bet_sb[:, :],
                                                 in_=row_bcast_ap(beta, D)), 16)
                for tt in range(NTT):
                    s.wait_ge(p.sem(nm + "_V1" if post_ln else nm + "_E"),
                              B[nm + "_V1" if post_ln else nm + "_E"] +
                              ((tt + 1) if post_ln else 2 * (tt + 1)))
                    p.inc(nm + "_ST", s.dma_start(
                        out=out.ap()[tt * 128:(tt + 1) * 128, :],
                        in_=yt[tt % 2][:, :]), 16)
                s.wait_ge(p.sem(nm + "_ST"), p.cnt[nm + "_ST"])

            @b.tensor
            def _(t):
                t.wait_ge(p.sem(nm + "_G"), B[nm + "_G"] + 16 * (NDC + 1))
                t.wait_ge(p.sem(nm + "_L"),
                          B[nm + "_L"] + 16 * (3 if post_ln is not None else 1))
                for tt in range(NTT):
                    for eb in range(2):
                        jj = tt * 2 + eb
                        if jj >= 4:
                            t.wait_ge(p.sem(nm + "_E"), B[nm + "_E"] + jj - 3)
                        for dc in range(NDC):
                            ins = t.matmul(
                                ps[jj % 4][:, :],
                                inT_sb[:, dc * T + tt * 128:dc * T + tt * 128 + 128],
                                W_sb[:, dc * D + eb * 512:dc * D + eb * 512 + 512],
                                start=(dc == 0), stop=(dc == NDC - 1))
                        p.inc(nm + "_T", ins)

            @b.vector
            def _(v):
                for tt in range(NTT):
                    if tt >= 2:
                        v.wait_ge(p.sem(nm + "_ST"), B[nm + "_ST"] + 16 * (tt - 1))
                    for eb in range(2):
                        jj = tt * 2 + eb
                        v.wait_ge(p.sem(nm + "_T"), B[nm + "_T"] + jj + 1)
                        p.inc(nm + "_E", v.tensor_tensor(
                            yt[tt % 2][:, eb * 512:eb * 512 + 512],
                            ps[jj % 4][:, :], brow_sb[:, eb * 512:eb * 512 + 512],
                            ALU.add))
                    if post_ln is not None:
                        yv = yt[tt % 2]
                        g2 = tt % 2
                        p.inc(nm + "_V0a", v.reduce_sum(mneg[:, g2:g2 + 1],
                                                        yv[:, :], axis=AX.X))
                        v.wait_ge(p.sem(nm + "_V0a"), p.cnt[nm + "_V0a"])
                        p.inc(nm + "_V0",
                              v.tensor_scalar(mneg[:, g2:g2 + 1],
                                              mneg[:, g2:g2 + 1], -1.0 / D,
                                              None, ALU.mult))
                        v.wait_ge(p.sem(nm + "_S0"), B[nm + "_S0"] + tt + 1)
                        p.inc(nm + "_V0b", v.reduce_sum(var[:, g2:g2 + 1],
                                                        sq[:, :], axis=AX.X))
                        v.wait_ge(p.sem(nm + "_S1"), B[nm + "_S1"] + tt + 1)
                        v.tensor_scalar(yv[:, :], yv[:, :], mneg[:, g2:g2 + 1],
                                        rstd[:, g2:g2 + 1], ALU.add, ALU.mult)
                        v.tensor_tensor(yv[:, :], yv[:, :], gam_sb[:, :], ALU.mult)
                        p.inc(nm + "_V1", v.tensor_tensor(yv[:, :], yv[:, :],
                                                          bet_sb[:, :], ALU.add))

            if post_ln is not None:
                @b.scalar
                def _(sc):
                    for tt in range(NTT):
                        g2 = tt % 2
                        sc.wait_ge(p.sem(nm + "_V0"), B[nm + "_V0"] + tt + 1)
                        p.inc(nm + "_S0",
                              sc.activation(sq[:, :], yt[tt % 2][:, :], AF.Square,
                                            bias=mneg[:, g2:g2 + 1]))
                        sc.wait_ge(p.sem(nm + "_V0b"), B[nm + "_V0b"] + tt + 1)
                        sc.activation(var[:, g2:g2 + 1], var[:, g2:g2 + 1],
                                      AF.Ln, scale=1.0 / D, bias=EPS)
                        p.inc(nm + "_S1",
                              sc.activation(rstd[:, g2:g2 + 1], var[:, g2:g2 + 1],
                                            AF.Exp, scale=-0.5))

    phase_barrier(p)


def emit_l0_ffn(p, consts, win):
    nc = p.nc
    xnT = p.dram["xnT"]
    curf = p.dram["cur_f32"]
    W1in, b1in = win["L0_ffn_W1"], win["L0_ffn_b1"]
    W2in, b2in = win["L0_ffn_W2"], win["L0_ffn_b2"]
    Wgin, bgin = win["L0_ffn_Wg"], win["L0_ffn_bg"]
    W1b = p.scratch("f3_W1b", [4, NFC // 4, 128, 4, 8, 128], BF16)
    W2b = p.scratch("f3_W2b", [4, NFC, 128, D], BF16)
    id_f = consts["id_f"]
    ones_b = consts["ones_b"]
    NR, NE, NTB4 = F3_NR, F3_NE, 4
    NSUB = 8

    sems = ["f3_L0", "f3_G0", "f3_LW", "f3_GW", "f3_WB", "f3_Th", "f3_Se",
            "f3_Tw", "f3_Ve", "f3_Tg", "f3_Sg", "f3_Tt", "f3_Vt", "f3_Vg1",
            "f3_Sg2", "f3_Va", "f3_Vb", "f3_Vg", "f3_Lc", "f3_SC", "f3_Tx",
            "f3_Vx"]
    for s in sems:
        p.sem(s)
    B = {s: p.cnt[s] for s in sems}

    tgt = {}
    wb_after_e = {e: B["f3_WB"] + 16 * 96 * (e + 1) for e in range(NE)}

    def gH(r, tb, e):
        return B["f3_Th"] + ((r * NTB4 + tb) * NE + e) * 64

    def gW(r, tb, e, pas):
        return B["f3_Tw"] + (((r * NTB4 + tb) * NE + e) * 2 + pas) * 32

    def gVe(r, tb, e):
        return B["f3_Ve"] + ((r * NTB4 + tb) * NE + e) * 16

    def gSe(r, tb, e):
        return B["f3_Se"] + ((r * NTB4 + tb) * NE + e) * 64

    def gTx(r, tb):
        return B["f3_Tx"] + (r * NTB4 + tb) * 64

    with ExitStack() as scope:
        e_ = scope.enter_context
        curT = e_(nc.sbuf_tensor("f3_curT", [128, NDC * T], BF16))
        H_sb = e_(nc.sbuf_tensor("f3_H", [128, NFC * 1024], BF16))
        acc = e_(nc.sbuf_tensor("f3_acc", [128, NSUB * D], F32))
        W1sl = [e_(nc.sbuf_tensor(f"f3_W1sl{i}", [128, 4 * 8 * 128], BF16))
                for i in range(2)]
        W2sl = [e_(nc.sbuf_tensor(f"f3_W2sl{i}", [128, 4 * 512], BF16))
                for i in range(2)]
        Wg_sb = e_(nc.sbuf_tensor("f3_Wg", [128, NR * NDC * 4], BF16))
        bg_sb = e_(nc.sbuf_tensor("f3_bg", [4, NR], F32))
        b1_sb = e_(nc.sbuf_tensor("f3_b1", [128, NE * NFC], F32))
        b2row = e_(nc.sbuf_tensor("f3_b2r", [1, NE * D], BF16))
        logT4 = [e_(nc.sbuf_tensor(f"f3_lg{i}", [4, 512], F32)) for i in range(2)]
        lgn = e_(nc.sbuf_tensor("f3_lgn", [128, NTT * 4], F32))
        cwn = e_(nc.sbuf_tensor("f3_cwn", [128, NTT * 4], F32))
        wk = e_(nc.sbuf_tensor("f3_wk", [128, 8 * 32], F32))
        exio = e_(nc.sbuf_tensor("f3_exio", [128, 10 * 32], F32))
        exchk = e_(nc.sbuf_tensor("f3_exchk", [128, 32], F32))
        wk2 = e_(nc.sbuf_tensor("f3_wk2", [128, 8 * 32], F32))
        ps = [e_(nc.psum_tensor(f"f3_ps{i}", [128, 512], F32)) for i in range(8)]

        def W(i):
            return wk[:, i * 32:(i + 1) * 32]

        def lgn_e(t, eix):
            return bass.AP(t, eix, [[NTT * 4, 128], [4, NTT]])

        with nc.Block() as b:
            @b.sync
            def _(s):
                for r in range(NR):
                    p.inc("f3_L0", s.dma_start(
                        out=bg_sb[:, r:r + 1],
                        in_=bass.AP(bgin, r * 4, [[1, 4], [1, 1]]),
                        allow_slow_non_contiguous=True), 16)
                for e in range(NE):
                    p.inc("f3_L0", s.dma_start(
                        out=b1_sb[:, e * NFC:(e + 1) * NFC],
                        in_=bias_chunks_ap(b1in, NFC, off=e * FF),
                        allow_slow_non_contiguous=True), 16)
                for r in range(NR):
                    for tb in range(NTB4):
                        idx = r * NTB4 + tb
                        if idx > 0:
                            s.wait_ge(p.sem("f3_SC"), B["f3_SC"] + 16 * idx)
                            prev_r, prev_tb = (r, tb - 1) if tb > 0 else \
                                (r - 1, NTB4 - 1)
                            if prev_r < 3:
                                s.wait_ge(p.sem("f3_Tx"),
                                          gTx(prev_r, prev_tb) + 64)
                        srcc = bass.AP(curf, tb * 1024 * D,
                                       [[D, 128], [128 * D, NSUB], [1, D]])
                        p.inc("f3_Lc", s.dma_start(out=acc[:, :], in_=srcc), 16)
                        for e in range(NE):
                            if r == 0 and tb == 0:
                                continue
                            if r == 0 and tb == 1:
                                s.wait_ge(p.sem("f3_WB"), wb_after_e[e])
                            for fb in range(NFC // 4):
                                tv = gH(r, tb, e) + 8 * (fb - 1)
                                if tv > B["f3_Th"]:
                                    s.wait_ge(p.sem("f3_Th"), tv)
                                srcw = bass.AP(W1b,
                                               (e * 8 + fb) * 128 * 4096,
                                               [[4096, 128], [1, 4096]])
                                ins = s.dma_start(out=W1sl[fb % 2][:, :], in_=srcw)
                                p.inc("f3_LW", ins, 16)
                                for fc in range(4 * fb, 4 * fb + 4):
                                    tgt[(r, tb, e, "w1", fc)] =                                         ("f3_LW", p.cnt["f3_LW"])
                            for pas in range(2):
                                for fb in range(NFC // 4):
                                    tv = gW(r, tb, e, pas) + 4 * (fb - 1)
                                    if tv > B["f3_Tw"]:
                                        s.wait_ge(p.sem("f3_Tw"), tv)
                                    srcw = bass.AP(
                                        W2b,
                                        ((e * NFC) + 4 * fb) * 128 * D + pas * 512,
                                        [[D, 128], [128 * D, 4], [1, 512]])
                                    ins = s.dma_start(out=W2sl[fb % 2][:, :],
                                                      in_=srcw)
                                    p.inc("f3_LW", ins, 16)
                                    for fc in range(4 * fb, 4 * fb + 4):
                                        tgt[(r, tb, e, "w2", fc, pas)] =                                             ("f3_LW", p.cnt["f3_LW"])

            @b.gpsimd
            def _(g):
                for c in range(NDC):
                    srcx = bass.AP(xnT, c * 128 * 512,
                                   [[512, 128], [NDC * 128 * 512, NTB], [1, 512]])
                    p.inc("f3_G0",
                          g.dma_start(out=curT[:, c * T:(c + 1) * T], in_=srcx), 16)
                p.inc("f3_G0", g.dma_start(
                    out=b2row[:, :],
                    in_=bass.AP(b2in, 0, [[NE * D, 1], [1, NE * D]])), 16)
                for r in range(NR):
                    srcw = bass.AP(Wgin, r * D * 4, [[4, 128], [512, NDC], [1, 4]])
                    p.inc("f3_G0", g.dma_start(
                        out=Wg_sb[:, r * NDC * 4:(r + 1) * NDC * 4], in_=srcw,
                        allow_slow_non_contiguous=True), 16)
                r, tb = 0, 0
                for e in range(NE):
                    for fc in range(NFC):
                        fb, fq = fc // 4, fc % 4
                        tv = gH(r, tb, e) + 8 * (fb - 1)
                        if tv > B["f3_Th"]:
                            g.wait_ge(p.sem("f3_Th"), tv)
                        srcw = bass.AP(W1in, e * D * FF + fc * 128,
                                       [[FF, 128], [128 * FF, 8], [1, 128]])
                        dst = W1sl[fb % 2][:, fq * 1024:(fq + 1) * 1024]
                        ins = g.dma_start(out=dst, in_=srcw)
                        p.inc("f3_GW", ins, 16)
                        tgt[(r, tb, e, "w1", fc)] = ("f3_GW", p.cnt["f3_GW"])
                        g.wait_ge(p.sem("f3_GW"), p.cnt["f3_GW"])
                        dstw = bass.AP(W1b, (e * 8 + fb) * 128 * 4096 + fq * 1024,
                                       [[4096, 128], [128, 8], [1, 128]])
                        p.inc("f3_WB", g.dma_start(out=dstw, in_=dst), 16)
                    for pas in range(2):
                        for fc in range(NFC):
                            fb, fq = fc // 4, fc % 4
                            tv = gW(r, tb, e, pas) + 4 * (fb - 1)
                            if tv > B["f3_Tw"]:
                                g.wait_ge(p.sem("f3_Tw"), tv)
                            srcw = bass.AP(W2in,
                                           e * FF * D + fc * 128 * D + pas * 512,
                                           [[D, 128], [1, 512]])
                            dst = W2sl[fb % 2][:, fq * 512:(fq + 1) * 512]
                            ins = g.dma_start(out=dst, in_=srcw)
                            p.inc("f3_GW", ins, 16)
                            tgt[(r, tb, e, "w2", fc, pas)] =                                 ("f3_GW", p.cnt["f3_GW"])
                            g.wait_ge(p.sem("f3_GW"), p.cnt["f3_GW"])
                            dstw = bass.AP(W2b,
                                           ((e * NFC) + fc) * 128 * D + pas * 512,
                                           [[D, 128], [1, 512]])
                            p.inc("f3_WB", g.dma_start(out=dstw, in_=dst), 16)
                for r in range(NR):
                    for tb in range(NTB4):
                        g.wait_ge(p.sem("f3_Ve"), gVe(r, tb, NE - 1) + 16)
                        dstc = bass.AP(curf, tb * 1024 * D,
                                       [[D, 128], [128 * D, NSUB], [1, D]])
                        p.inc("f3_SC", g.dma_start(out=dstc, in_=acc[:, :]), 16)
                if "f3_dbg" in DEBUG_OUTPUTS:
                    dbg = p.scratch("f3_dbg", [2, 128, NTT * 4], F32)
                    g.wait_ge(p.sem("f3_Vg"), B["f3_Vg"] + 1)
                    p.inc("f3_SC", g.dma_start(
                        out=bass.AP(dbg, 0, [[NTT * 4, 128], [1, NTT * 4]]),
                        in_=lgn[:, :]), 16)
                    p.inc("f3_SC", g.dma_start(
                        out=bass.AP(dbg, 128 * NTT * 4,
                                    [[NTT * 4, 128], [1, NTT * 4]]),
                        in_=cwn[:, :]), 16)
                g.wait_ge(p.sem("f3_SC"), p.cnt["f3_SC"])
                g.wait_ge(p.sem("f3_WB"), p.cnt["f3_WB"])

            @b.scalar
            def _(sc):
                sc.wait_ge(p.sem("f3_L0"), B["f3_L0"] + 16 * (NR + NE))
                for r in range(NR):
                    for tb8 in range(NTB):
                        gi = r * NTB + tb8
                        sc.wait_ge(p.sem("f3_Tg"), B["f3_Tg"] + gi + 1)
                        if gi >= 2:
                            sc.wait_ge(p.sem("f3_Tt"), B["f3_Tt"] + gi - 1)
                        p.inc("f3_Sg", sc.activation(
                            logT4[tb8 % 2][:, :], ps[0][0:4, :], AF.Identity,
                            bias=bg_sb[:, r:r + 1]))
                    sc.wait_ge(p.sem("f3_Vg1"), B["f3_Vg1"] + r + 1)
                    sc.activation(exio[:, 160:320], exio[:, 0:160], AF.Exp)
                    p.inc("f3_Sg2", sc.copy(exchk[:, :], exio[:, 288:320]))
                    for tb in range(NTB4):
                        for e in range(NE):
                            for fc in range(NFC):
                                for th in range(2):
                                    k = fc * 2 + th
                                    sc.wait_ge(p.sem("f3_Th"),
                                               gH(r, tb, e) + k + 1)
                                    if k == 0 and (r, tb, e) != (0, 0, 0):
                                        sc.wait_ge(p.sem("f3_Tw"),
                                                   gW(r, tb, e, 0))
                                    p.inc("f3_Se", sc.activation(
                                        H_sb[:, fc * 1024 + th * 512:
                                             fc * 1024 + th * 512 + 512],
                                        ps[k % 2][:, :], AF.Gelu,
                                        bias=b1_sb[:, e * NFC + fc:
                                                   e * NFC + fc + 1]))

            @b.tensor
            def _(t):
                t.wait_ge(p.sem("f3_G0"), B["f3_G0"] + 16 * (NDC + 1 + NR))
                t.wait_ge(p.sem("f3_L0"), B["f3_L0"] + 16 * (NR + NE))
                for r in range(NR):
                    if r > 0:
                        t.wait_ge(p.sem("f3_Vx"), B["f3_Vx"] + r * NTB4 * 64)
                    for tb8 in range(NTB):
                        gi = r * NTB + tb8
                        if gi >= 1:
                            t.wait_ge(p.sem("f3_Sg"), B["f3_Sg"] + gi)
                        for dc in range(NDC):
                            ins = t.matmul(
                                ps[0][0:4, :],
                                Wg_sb[:, (r * NDC + dc) * 4:
                                      (r * NDC + dc) * 4 + 4],
                                curT[:, dc * T + tb8 * 512:
                                     dc * T + tb8 * 512 + 512],
                                start=(dc == 0), stop=(dc == NDC - 1))
                        p.inc("f3_Tg", ins)
                        t.wait_ge(p.sem("f3_Sg"), B["f3_Sg"] + gi + 1)
                        if gi >= 1:
                            t.wait_ge(p.sem("f3_Vt"), B["f3_Vt"] + gi)
                        for c4 in range(4):
                            ins = t.transpose(
                                ps[1][:, c4 * 4:c4 * 4 + 4],
                                logT4[tb8 % 2][0:4, c4 * 128:(c4 + 1) * 128],
                                id_f[0:4, 0:4])
                        p.inc("f3_Tt", ins)
                    for tb in range(NTB4):
                        for e in range(NE):
                            for fc in range(NFC):
                                sem, val = tgt[(r, tb, e, "w1", fc)]
                                t.wait_ge(p.sem(sem), val)
                                for th in range(2):
                                    k = fc * 2 + th
                                    if k == 0:
                                        t.wait_ge(p.sem("f3_Ve"), gVe(r, tb, e))
                                        if e == 0 and tb == 0:
                                            t.wait_ge(p.sem("f3_Vt"),
                                                      B["f3_Vt"] + NTB * (r + 1))
                                        if e == 0:
                                            vxb = ((r * NTB4 + tb) if r < 3
                                                   else 3 * NTB4) * 64
                                            t.wait_ge(p.sem("f3_Vx"),
                                                      B["f3_Vx"] + vxb)
                                    if k >= 2:
                                        t.wait_ge(p.sem("f3_Se"),
                                                  gSe(r, tb, e) + k - 1)
                                    fb, fq = fc // 4, fc % 4
                                    for dc in range(NDC):
                                        ins = t.matmul(
                                            ps[k % 2][:, :],
                                            W1sl[fb % 2][:, fq * 1024 + dc * 128:
                                                         fq * 1024 +
                                                         (dc + 1) * 128],
                                            curT[:, dc * T + tb * 1024 + th * 512:
                                                 dc * T + tb * 1024 +
                                                 th * 512 + 512],
                                            start=(dc == 0), stop=(dc == NDC - 1))
                                    p.inc("f3_Th", ins)
                            t.wait_ge(p.sem("f3_Se"), gSe(r, tb, e) + 64)
                            t.wait_ge(p.sem("f3_Vg"), B["f3_Vg"] + r + 1)
                            for pas in range(2):
                                t.wait_ge(p.sem("f3_Ve"), gVe(r, tb, e) + 8 * pas)
                                for tt in range(NSUB):
                                    t.matmul(ps[tt][:, :], ones_b[0:1, 0:128],
                                             b2row[:, e * D + pas * 512:
                                                   e * D + pas * 512 + 512],
                                             start=True, stop=False)
                                for fc in range(NFC):
                                    sem, val = tgt[(r, tb, e, "w2", fc, pas)]
                                    t.wait_ge(p.sem(sem), val)
                                    fb, fq = fc // 4, fc % 4
                                    for tt in range(NSUB):
                                        ins = t.matmul(
                                            ps[tt][:, :],
                                            H_sb[:, fc * 1024 + tt * 128:
                                                 fc * 1024 + (tt + 1) * 128],
                                            W2sl[fb % 2][:, fq * 512:
                                                         (fq + 1) * 512],
                                            start=False, stop=(fc == NFC - 1))
                                    p.inc("f3_Tw", ins)
                        if r < 3:
                            t.wait_ge(p.sem("f3_Ve"), gVe(r, tb, NE - 1) + 16)
                            for tt in range(NSUB):
                                for dc in range(NDC):
                                    k = tt * NDC + dc
                                    if k >= 2:
                                        t.wait_ge(p.sem("f3_Vx"),
                                                  B["f3_Vx"] + gTx(r, tb) -
                                                  B["f3_Tx"] + k - 1)
                                    ins = t.transpose(
                                        ps[k % 2][:, 0:128],
                                        acc[:, tt * D + dc * 128:
                                            tt * D + (dc + 1) * 128],
                                        id_f[:, :])
                                    p.inc("f3_Tx", ins)

            @b.vector
            def _(v):
                for r in range(NR):
                    for tb8 in range(NTB):
                        gi = r * NTB + tb8
                        v.wait_ge(p.sem("f3_Tt"), B["f3_Tt"] + gi + 1)
                        p.inc("f3_Vt", v.tensor_copy(
                            lgn[:, tb8 * 16:(tb8 + 1) * 16], ps[1][:, 0:16]))
                    c = [lgn_e(lgn, e) for e in range(4)]
                    v.tensor_tensor(W(0), c[0], c[1], ALU.max)
                    v.tensor_tensor(W(1), c[2], c[3], ALU.max)
                    v.tensor_tensor(W(2), c[0], c[1], ALU.min)
                    v.tensor_tensor(W(3), c[2], c[3], ALU.min)
                    v.tensor_tensor(W(4), W(0), W(1), ALU.max)
                    v.tensor_tensor(W(5), W(0), W(1), ALU.min)
                    v.tensor_tensor(W(6), W(2), W(3), ALU.max)
                    v.tensor_tensor(W(5), W(5), W(6), ALU.max)
                    v.tensor_tensor(exio[:, 0:32], W(5), W(4), ALU.subtract)
                    for e in range(4):
                        v.tensor_tensor(exio[:, (e + 1) * 32:(e + 2) * 32],
                                        c[e], W(4), ALU.subtract)
                    p.inc("f3_Vg1", v.tensor_scalar(W(7), exio[:, 128:160], 1.0,
                                                    None, ALU.mult))
                    v.wait_ge(p.sem("f3_Sg2"), B["f3_Sg2"] + r + 1)
                    p.inc("f3_Va", v.tensor_scalar(W(6), exio[:, 160:192], 1.0,
                                                   None, ALU.add))
                    v.wait_ge(p.sem("f3_Va"), p.cnt["f3_Va"])
                    p.inc("f3_Vb", v.reciprocal(W(6), W(6)))
                    v.wait_ge(p.sem("f3_Vb"), p.cnt["f3_Vb"])
                    for e in range(4):
                        v.tensor_tensor(wk2[:, (2 * e) * 32:(2 * e + 1) * 32],
                                        c[e], W(5), ALU.is_ge)
                    for e in range(4):
                        v.tensor_tensor(wk2[:, (2 * e + 1) * 32:(2 * e + 2) * 32],
                                        wk2[:, (2 * e) * 32:(2 * e + 1) * 32],
                                        exio[:, (6 + e) * 32:(7 + e) * 32],
                                        ALU.mult)
                    for e in range(4):
                        ins = v.tensor_tensor(
                            lgn_e(cwn, e),
                            wk2[:, (2 * e + 1) * 32:(2 * e + 2) * 32],
                            W(6), ALU.mult)
                    p.inc("f3_Vg", ins)
                    for tb in range(NTB4):
                        idx = r * NTB4 + tb
                        for e in range(NE):
                            for pas in range(2):
                                v.wait_ge(p.sem("f3_Tw"),
                                          gW(r, tb, e, pas) + NFC)
                                if e == 0 and pas == 0:
                                    v.wait_ge(p.sem("f3_Lc"),
                                              B["f3_Lc"] + 16 * (idx + 1))
                                for tt in range(NSUB):
                                    cw = cwn[:, (tb * NSUB + tt) * 4 + e:
                                             (tb * NSUB + tt) * 4 + e + 1]
                                    p.inc("f3_Ve", v.scalar_tensor_tensor(
                                        acc[:, tt * D + pas * 512:
                                            tt * D + pas * 512 + 512],
                                        ps[tt][:, :], cw,
                                        acc[:, tt * D + pas * 512:
                                            tt * D + pas * 512 + 512],
                                        ALU.mult, ALU.add))
                        if r < 3:
                            for tt in range(NSUB):
                                for dc in range(NDC):
                                    k = tt * NDC + dc
                                    v.wait_ge(p.sem("f3_Tx"),
                                              gTx(r, tb) + k + 1)
                                    ins = v.tensor_copy(
                                        curT[:, dc * T + tb * 1024 + tt * 128:
                                             dc * T + tb * 1024 +
                                             (tt + 1) * 128],
                                        ps[k % 2][:, 0:128])
                                    if k < NSUB * NDC - 1:
                                        p.inc("f3_Vx", ins)
                            p.inc("f3_Vx", v.tensor_copy(
                                exchk[:, :],
                                curT[:, (NDC - 1) * T + tb * 1024 +
                                     (NSUB - 1) * 128 + 96:
                                     (NDC - 1) * T + tb * 1024 + NSUB * 128]))

    phase_barrier(p)


def kernel(**inputs):
    x_full = np.ascontiguousarray(np.asarray(inputs["L0_attn_X"], dtype=np.float32))
    b_nd = np.asarray(inputs["L0_attn_b_nd"], dtype=np.float32)

    ii = np.arange(128)
    dgrid = ii[:, None] - ii[None, :]
    DT = np.empty((9, 11, 128, 128), dtype=np.float32)
    for k in range(9):
        dk = 128 * k + dgrid
        idx = np.clip(dk, 0, ML - 1)
        DT[k, :NB] = b_nd[:, idx]
        DT[k, NB] = np.where((dk < 0) | (dk >= ML), NEG, 0.0).astype(np.float32)

    weight_names = [k for k in inputs if k != "L0_attn_X"]
    common = {k: np.ascontiguousarray(np.asarray(inputs[k], dtype=np.float32))
              for k in weight_names}
    common["DT"] = DT

    p = build_program()
    in_maps = [dict(common, x=x_full[b]) for b in range(8)]
    res = run_bass_kernel_spmd(p.nc, in_maps, core_ids=list(range(8)))
    out = np.stack([np.asarray(res.results[b]["y"]) for b in range(8)], axis=0)
    return out.astype(np.float32)


def build_program():
    p = Prog()
    nc = p.nc
    x_in = nc.dram_tensor("x", [T, D], F32, kind="ExternalInput")
    y_out = nc.dram_tensor("y", [T, D], F32, kind="ExternalOutput")
    win = {}
    for nm, shape in [
        ("L0_attn_Wq", [D, D]), ("L0_attn_bq", [D]), ("L0_attn_Wk", [D, D]),
        ("L0_attn_Wv", [D, D]), ("L0_attn_Wproj", [D, D]), ("L0_attn_bproj", [D]),
        ("L0_attn_Wr", [D, H * NB]), ("L0_attn_br", [H * NB]),
        ("L0_attn_b_nd", [NB, ML]), ("DT", [9, 11, 128, 128]),
        ("L0_ffn_Wg", [4, D, 4]), ("L0_ffn_bg", [4, 4]),
        ("L0_ffn_W1", [4, D, FF]), ("L0_ffn_b1", [4, FF]),
        ("L0_ffn_W2", [4, FF, D]), ("L0_ffn_b2", [4, D]),
        ("L1_attn_Wq", [D, D]), ("L1_attn_bq", [D]), ("L1_attn_Wk", [D, D]),
        ("L1_attn_bk", [D]), ("L1_attn_Wv", [D, D]), ("L1_attn_bv", [D]),
        ("L1_attn_Wo", [D, D]), ("L1_attn_bo", [D]),
        ("L1_attn_gamma", [D]), ("L1_attn_beta", [D]),
        ("L1_ffn_w_router", [D, 8]), ("L1_ffn_w_up", [7, D, FF]),
        ("L1_ffn_w_gate", [7, D, FF]), ("L1_ffn_w_down", [7, FF, D]),
        ("L1_ffn_ws_up", [D, FF]), ("L1_ffn_ws_gate", [D, FF]),
        ("L1_ffn_ws_down", [FF, D]),
    ]:
        win[nm] = nc.dram_tensor(nm, shape, F32, kind="ExternalInput")

    id_b = p.sbuf("id_b", [128, 128], BF16)
    ones_b = p.sbuf("ones_b", [128, 128], BF16)
    negt_b = p.sbuf("negt_b", [128, 128], BF16)
    iota_f = p.sbuf("iota_f", [128, 128], F32)
    iota_p = p.sbuf("iota_p", [128, 1], F32)
    id_f = p.sbuf("id_f", [128, 128], F32)

    const_vals = [EPS]
    const_ts = []
    for i, val in enumerate(const_vals):
        t = p.sbuf(f"cst{i}", [128, 1], F32)
        nc.const_aps.register(F32, val, t[:, 0:1])
        const_ts.append((t, val))

    with nc.Block() as b:
        @b.gpsimd
        def _(g):
            g.iota(iota_f[:, :], [[1, 128]], channel_multiplier=0,
                   allow_small_or_imprecise_dtypes=True)
            p.inc("INIT_G", g.iota(iota_p[:, :], [[1, 1]], channel_multiplier=1,
                                   allow_small_or_imprecise_dtypes=True))

        @b.vector
        def _(v):
            v.wait_ge(p.sem("INIT_G"), 1)
            v.tensor_scalar(id_f[:, :], iota_f[:, :], iota_p[:, 0:1], None,
                            ALU.subtract)
            v.tensor_single_scalar(id_f[:, :], id_f[:, :], 0.0, ALU.is_equal)
            v.tensor_copy(id_b[:, :], id_f[:, :])
            v.memset(ones_b[:, :], 1.0)
            for t, val in const_ts:
                v.memset(t[:, :], val)
            p.inc("INIT_V", v.memset(negt_b[:, :], NEG))
    phase_barrier(p)

    consts = dict(id_b=id_b, ones_b=ones_b, negt_b=negt_b, id_f=id_f)
    r_scope = ExitStack()
    R_sb = r_scope.enter_context(nc.sbuf_tensor("R_sb", [128, NTT * H * NB], F32))
    consts["R_sb"] = R_sb

    phases = [
        lambda: emit_ln_phase(p, "p0", x_in.ap(), "xnT", "cur_f32", id_b),
        lambda: emit_attn_proj(p, "p1", "xnT",
                               win["L0_attn_Wq"], win["L0_attn_bq"],
                               win["L0_attn_Wk"], None,
                               win["L0_attn_Wv"], None,
                               wr=win["L0_attn_Wr"], br=win["L0_attn_br"],
                               R_sb=R_sb),
        lambda: (emit_l0_attn(p, consts, win), r_scope.close()),
        lambda: emit_proj_nat(p, "p2b", "a0_A2T", win["L0_attn_Wproj"],
                              win["L0_attn_bproj"], "s0"),
        lambda: emit_l0_ffn(p, consts, win),
        lambda: emit_ln_phase(p, "p4", x_in.ap(), "xn1T", None, id_b,
                              merge=(p.scratch("s0", [T, D], F32),
                                     p.scratch("cur_f32", [T, D], F32),
                                     p.scratch("x1", [T, D], F32))),
        lambda: emit_attn_proj(p, "p5", "xn1T",
                               win["L1_attn_Wq"], win["L1_attn_bq"],
                               win["L1_attn_Wk"], win["L1_attn_bk"],
                               win["L1_attn_Wv"], win["L1_attn_bv"]),
        lambda: emit_l1_attn(p, consts, win),
        lambda: emit_proj_nat(p, "p6b", "a1_A2T", win["L1_attn_Wo"],
                              win["L1_attn_bo"], "s1",
                              post_ln=(win["L1_attn_gamma"],
                                       win["L1_attn_beta"])),
        lambda: emit_l1_ffn(p, consts, win),
        lambda: emit_final(p, y_out),
    ]
    for i, ph in enumerate(phases):
        if PHASES_SELECT is not None and i not in PHASES_SELECT:
            continue
        if PHASE_LIMIT is not None and i >= PHASE_LIMIT:
            break
        ph()

    if "R_dump" in DEBUG_OUTPUTS:
        rd = p.scratch("R_dump", [128, NTT * H * NB], BF16)
        with nc.Block() as b:
            @b.gpsimd
            def _(g):
                p.inc("RD", g.dma_start(out=rd.ap(), in_=R_sb[:, :]), 16)
                g.wait_ge(p.sem("RD"), p.cnt["RD"])
        phase_barrier(p)

    if PHASE_LIMIT is not None or PHASES_SELECT is not None:
        finalize(p, y_out, consts, win)
    p.ctx.close()
    return p


def finalize(p, y_out, consts, win):
    nc = p.nc
    with ExitStack() as scope:
        z = scope.enter_context(nc.sbuf_tensor("zf", [128, D], F32))
        with nc.Block() as b:
            @b.vector
            def _(v):
                p.inc("FINV", v.memset(z[:, :], 0.0))

            @b.gpsimd
            def _(g):
                g.wait_ge(p.sem("FINV"), p.cnt["FINV"])
                for i in range(NTT):
                    p.inc("FIN", g.dma_start(
                        out=y_out.ap()[i * 128:(i + 1) * 128, :], in_=z[:, :]), 16)
                g.wait_ge(p.sem("FIN"), p.cnt["FIN"])


# version 03
# speedup vs baseline: 1.1256x; 1.1256x over previous
from contextlib import ExitStack

import numpy as np

import concourse.bass as bass
import concourse.mybir as mybir
from concourse.bass_utils import run_bass_kernel_spmd

F32 = mybir.dt.float32
BF16 = mybir.dt.float16
AF = mybir.ActivationFunctionType
ALU = mybir.AluOpType
AX = mybir.AxisListType

T = 4096
D = 1024
H = 16
DQ = 64
NB = 10
ML = 1024
FF = 4096
NEG = -30000.0
EPS = 1e-5
NTT = T // 128
NDC = D // 128
NFC = FF // 128
NTB = T // 512

DEBUG_OUTPUTS = ()
PHASE_LIMIT = None
F3_NR = 4
F3_NE = 4
PHASES_SELECT = None


class Prog:

    PERSISTENT = {"PB"}

    def __init__(self):
        self.nc = bass.Bass("TRN2", target_bir_lowering=False)
        self.ctx = ExitStack()
        self.sems = {}
        self.cnt = {}
        self.dram = {}
        self.sem_cms = {}
        self.pending = []

    def sem(self, name):
        if name not in self.sems:
            cm = self.nc.semaphore(name)
            self.sems[name] = cm.__enter__()
            self.cnt[name] = 0
            self.sem_cms[name] = cm
            if name not in self.PERSISTENT:
                self.pending.append(name)
        return self.sems[name]

    def release_pending(self):
        for name in self.pending:
            self.sem_cms[name].__exit__(None, None, None)
            del self.sems[name]
            del self.cnt[name]
            del self.sem_cms[name]
        self.pending = []

    def inc(self, name, instr, k=1):
        instr.then_inc(self.sem(name), k)
        self.cnt[name] += k
        return instr

    def scratch(self, name, shape, dtype):
        if name not in self.dram:
            kind = "ExternalOutput" if name in DEBUG_OUTPUTS else "Internal"
            self.dram[name] = self.nc.dram_tensor(name, shape, dtype, kind=kind)
        return self.dram[name]

    def sbuf(self, name, shape, dtype):
        return self.ctx.enter_context(self.nc.sbuf_tensor(name, shape, dtype))


def phase_barrier(p, engines=("sync", "vector", "scalar", "tensor", "gpsimd")):
    nc = p.nc
    base = p.cnt.get("PB", 0)
    target_a = base + len(engines)
    target_b = target_a + 1
    p.sem("PB")
    to_clear = list(p.pending)
    with nc.Block() as b:
        def mk(eng_name):
            def emit(eng):
                if eng_name in ("vector", "gpsimd"):
                    p.inc("PB", eng.engine_nop())
                else:
                    p.inc("PB", eng.nop())
                eng.wait_ge(p.sem("PB"), target_a)
                if eng_name == "sync":
                    for name in to_clear:
                        eng.sem_clear(p.sems[name])
                    p.inc("PB", eng.nop())
                eng.wait_ge(p.sem("PB"), target_b)
            return emit
        for e in engines:
            getattr(b, e)(mk(e))
    p.cnt["PB"] = target_b
    p.release_pending()


def emit_ln_phase(p, src_name, src_ap, dst_T_name, dst_f32_name, idtile,
                  merge=None):
    nc = p.nc
    xnT = p.scratch(dst_T_name, [NTB, NDC, 128, 512], BF16)
    curf = p.scratch(dst_f32_name, [T, D], F32) if dst_f32_name else None
    sLD = p.sem(src_name + "_LD")
    sV0 = p.sem(src_name + "_V0")
    sS0 = p.sem(src_name + "_S0")
    sV1 = p.sem(src_name + "_V1")
    sS1 = p.sem(src_name + "_S1")
    sV2 = p.sem(src_name + "_V2")
    sT0 = p.sem(src_name + "_T0")
    sV3 = p.sem(src_name + "_V3")
    sSTC = p.sem(src_name + "_STC")
    sSTG = p.sem(src_name + "_STG")
    b_LD = p.cnt[src_name + "_LD"]
    b_V2 = p.cnt[src_name + "_V2"]
    b_S0 = p.cnt[src_name + "_S0"]
    b_V1 = p.cnt[src_name + "_V1"]
    b_S1 = p.cnt[src_name + "_S1"]
    b_V0 = p.cnt[src_name + "_V0"]
    b_T0 = p.cnt[src_name + "_T0"]
    b_V3 = p.cnt[src_name + "_V3"]
    b_STC = p.cnt[src_name + "_STC"]
    b_STG = p.cnt[src_name + "_STG"]

    if merge is not None:
        s_dram, f_dram, x1_dram = merge
        sMG = p.sem(src_name + "_MG")
        sMS = p.sem(src_name + "_MS")
        sMV = p.sem(src_name + "_MV")
        sMX = p.sem(src_name + "_MX")
        b_MG = p.cnt[src_name + "_MG"]
        b_MS = p.cnt[src_name + "_MS"]
        b_MV = p.cnt[src_name + "_MV"]
        b_MX = p.cnt[src_name + "_MX"]

    with ExitStack() as scope:
        e = scope.enter_context
        xt = [e(nc.sbuf_tensor(f"{src_name}_xt{i}", [128, D], F32)) for i in range(2)]
        if merge is not None:
            st_m = [e(nc.sbuf_tensor(f"{src_name}_st{i}", [128, D], F32))
                    for i in range(2)]
            ft_m = [e(nc.sbuf_tensor(f"{src_name}_ft{i}", [128, D], F32))
                    for i in range(2)]
            gt_m = e(nc.sbuf_tensor(src_name + "_gt", [128, D], F32))
        sq = e(nc.sbuf_tensor(src_name + "_sq", [128, D], F32))
        mneg = e(nc.sbuf_tensor(src_name + "_mneg", [128, 2], F32))
        var = e(nc.sbuf_tensor(src_name + "_var", [128, 2], F32))
        rstd = e(nc.sbuf_tensor(src_name + "_rstd", [128, 2], F32))
        xnb = e(nc.sbuf_tensor(src_name + "_xnb", [128, D], BF16))
        xnf = [e(nc.sbuf_tensor(f"{src_name}_xnf{i}", [128, D], F32)) for i in range(2)]
        stage = e(nc.sbuf_tensor(src_name + "_stg", [128, NDC * 512], BF16))
        pt = [e(nc.psum_tensor(f"{src_name}_pt{i}", [128, 128], BF16))
              for i in range(8)]

        with nc.Block() as b:
            @b.sync
            def _(s):
                for i in range(NTT):
                    if i >= 2:
                        s.wait_ge(sV2, b_V2 + i - 1)
                    p.inc(src_name + "_LD",
                          s.dma_start(out=xt[i % 2][:, :],
                                      in_=src_ap[i * 128:(i + 1) * 128, :]), 16)
                    if merge is not None:
                        s.wait_ge(p.sem(src_name + "_MV"), b_MV + i + 1)
                        p.inc(src_name + "_MX", s.dma_start(
                            out=x1_dram.ap()[i * 128:(i + 1) * 128, :],
                            in_=xt[i % 2][:, :]), 16)

            @b.vector
            def _(v):
                for i in range(NTT):
                    g = i % 4
                    xi = xt[i % 2]
                    v.wait_ge(sLD, b_LD + 16 * (i + 1))
                    if merge is not None:
                        v.wait_ge(p.sem(src_name + "_MS"), b_MS + i + 1)
                        v.tensor_tensor(gt_m[:, :], gt_m[:, :], st_m[i % 2][:, :],
                                        ALU.mult)
                        if i >= 2:
                            v.wait_ge(p.sem(src_name + "_MX"), b_MX + 16 * (i - 1))
                        p.inc(src_name + "_MV",
                              v.tensor_tensor(xi[:, :], xi[:, :], gt_m[:, :],
                                              ALU.add))
                    p.inc(src_name + "_V0a",
                          v.reduce_sum(mneg[:, g % 2:g % 2 + 1], xi[:, :],
                                       axis=AX.X))
                    v.wait_ge(p.sem(src_name + "_V0a"), p.cnt[src_name + "_V0a"])
                    p.inc(src_name + "_V0",
                          v.tensor_scalar(mneg[:, g % 2:g % 2 + 1],
                                          mneg[:, g % 2:g % 2 + 1],
                                          -1.0 / D, None, ALU.mult))
                    v.wait_ge(sS0, b_S0 + i + 1)
                    p.inc(src_name + "_V1",
                          v.reduce_sum(var[:, g % 2:g % 2 + 1], sq[:, :], axis=AX.X))
                    v.wait_ge(sS1, b_S1 + i + 1)
                    if curf is not None and i >= 2:
                        v.wait_ge(sSTC, b_STC + 16 * (i - 1))
                    v.tensor_scalar(xnb[:, :], xi[:, :],
                                    mneg[:, g % 2:g % 2 + 1],
                                    rstd[:, g % 2:g % 2 + 1], ALU.add, ALU.mult)
                    p.inc(src_name + "_V2",
                          v.tensor_scalar(xnf[i % 2][:, :], xi[:, :],
                                          mneg[:, g % 2:g % 2 + 1],
                                          rstd[:, g % 2:g % 2 + 1],
                                          ALU.add, ALU.mult))
                    v.wait_ge(sT0, b_T0 + i + 1)
                    if g == 0 and i >= 4:
                        v.wait_ge(sSTG, b_STG + 16 * (i // 4))
                    for c in range(NDC):
                        ins = v.tensor_copy(stage[:, c * 512 + g * 128:
                                                  c * 512 + g * 128 + 128],
                                            pt[c][:, :])
                        if c == NDC - 1:
                            p.inc(src_name + "_V3", ins)

            @b.scalar
            def _(sc):
                for i in range(NTT):
                    g = i % 4
                    xi = xt[i % 2]
                    if merge is not None:
                        sc.wait_ge(p.sem(src_name + "_MG"), b_MG + 32 * (i + 1))
                        if i >= 1:
                            sc.wait_ge(p.sem(src_name + "_MV"), b_MV + i)
                        p.inc(src_name + "_MS", sc.activation(
                            gt_m[:, :], ft_m[i % 2][:, :], AF.Silu))
                    sc.wait_ge(sV0, b_V0 + i + 1)
                    p.inc(src_name + "_S0",
                          sc.activation(sq[:, :], xi[:, :], AF.Square,
                                        bias=mneg[:, g % 2:g % 2 + 1]))
                    sc.wait_ge(sV1, b_V1 + i + 1)
                    sc.activation(var[:, g % 2:g % 2 + 1],
                                  var[:, g % 2:g % 2 + 1], AF.Ln,
                                  scale=1.0 / D, bias=EPS)
                    p.inc(src_name + "_S1",
                          sc.activation(rstd[:, g % 2:g % 2 + 1],
                                        var[:, g % 2:g % 2 + 1], AF.Exp,
                                        scale=-0.5))

            @b.tensor
            def _(t):
                for i in range(NTT):
                    t.wait_ge(sV2, b_V2 + i + 1)
                    if i >= 1:
                        t.wait_ge(sV3, b_V3 + i)
                    for c in range(NDC):
                        ins = t.transpose(pt[c][:, :], xnb[:, c * 128:(c + 1) * 128],
                                          idtile[:, :])
                        if c == NDC - 1:
                            p.inc(src_name + "_T0", ins)

            @b.gpsimd
            def _(g):
                for i in range(NTT):
                    if merge is not None:
                        if i >= 2:
                            g.wait_ge(sV2, b_V2 + i - 1)
                        p.inc(src_name + "_MG", g.dma_start(
                            out=st_m[i % 2][:, :],
                            in_=s_dram.ap()[i * 128:(i + 1) * 128, :]), 16)
                        p.inc(src_name + "_MG", g.dma_start(
                            out=ft_m[i % 2][:, :],
                            in_=f_dram.ap()[i * 128:(i + 1) * 128, :]), 16)
                    if curf is not None:
                        g.wait_ge(sV2, b_V2 + i + 1)
                        p.inc(src_name + "_STC",
                              g.dma_start(out=curf.ap()[i * 128:(i + 1) * 128, :],
                                          in_=xnf[i % 2][:, :]), 16)
                    if i % 4 == 3:
                        g.wait_ge(sV3, b_V3 + i + 1)
                        tb = i // 4
                        dst = bass.AP(xnT, tb * (NDC * 128 * 512),
                                      [[512, 128], [128 * 512, NDC], [1, 512]])
                        p.inc(src_name + "_STG",
                              g.dma_start(out=dst, in_=stage[:, :]), 16)
                g.wait_ge(sSTG, p.cnt[src_name + "_STG"])
                if curf is not None:
                    g.wait_ge(sSTC, p.cnt[src_name + "_STC"])

    phase_barrier(p)


def wchunks_ap(w, rows, cols, row0=0, nchunk=None, ecols=None):
    ecols = ecols if ecols is not None else cols
    nchunk = nchunk if nchunk is not None else rows // 128
    return bass.AP(w, row0 * cols, [[cols, 128], [128 * cols, nchunk], [1, ecols]])


def bias_chunks_ap(bvec, n, off=0):
    return bass.AP(bvec, off, [[1, 128], [128, n], [1, 1]])


def row_bcast_ap(bvec, n, npart=128):
    return bass.AP(bvec, 0, [[0, npart], [1, n]])


def xnT_load_ap(xnT):
    return bass.AP(xnT, 0, [[512, 128], [128 * 512, NDC], [NDC * 128 * 512, NTB],
                            [1, 512]])


def emit_attn_proj(p, nm, xnT_name, wq, bq, wk, bk, wv, bv,
                   wr=None, br=None, R_sb=None, qscale=0.125):
    nc = p.nc
    xnT = p.dram[xnT_name]
    QT = p.scratch(nm + "_QT", [H, DQ, T], BF16)
    KT = p.scratch(nm + "_KT", [H, DQ, T], BF16)
    Vs = p.scratch(nm + "_V", [NTT, 128, D], BF16)

    sG = p.sem(nm + "_G")
    sL = p.sem(nm + "_L")
    sT = p.sem(nm + "_T")
    sE = p.sem(nm + "_E")
    sEV = p.sem(nm + "_EV")
    sST = p.sem(nm + "_ST")
    bG, bL, bT, bE, bEV, bST = (p.cnt[nm + k] for k in
                                ("_G", "_L", "_T", "_E", "_EV", "_ST"))

    with ExitStack() as scope:
        e = scope.enter_context
        xnT_sb = e(nc.sbuf_tensor(nm + "_xnT", [128, NDC * T], BF16))
        Wq_sb = e(nc.sbuf_tensor(nm + "_Wq", [128, NDC * D], BF16))
        Wk_sb = e(nc.sbuf_tensor(nm + "_Wk", [128, NDC * D], BF16))
        Wv_sb = e(nc.sbuf_tensor(nm + "_Wv", [128, NDC * D], BF16))
        bq_sb = e(nc.sbuf_tensor(nm + "_bq", [128, NDC], F32))
        bk_sb = e(nc.sbuf_tensor(nm + "_bk", [128, NDC], F32)) \
            if bk is not None else None
        bv_sb = e(nc.sbuf_tensor(nm + "_bv", [128, D], F32)) \
            if bv is not None else None
        Wr_sb = e(nc.sbuf_tensor(nm + "_Wr", [128, NDC * H * NB], BF16)) \
            if wr is not None else None
        br_sb = e(nc.sbuf_tensor(nm + "_br", [128, H * NB], F32)) \
            if br is not None else None
        stg = [e(nc.sbuf_tensor(f"{nm}_stg{i}", [128, 512], BF16)) for i in range(4)]
        ps = [e(nc.psum_tensor(f"{nm}_ps{i}", [128, 512], F32)) for i in range(4)]
        psR = [e(nc.psum_tensor(f"{nm}_psR{i}", [128, H * NB], F32))
               for i in range(2)] if wr is not None else None

        jobs = []
        for qc in range(NDC):
            for tb in range(NTB):
                jobs.append(("q", qc, tb))
        for qc in range(NDC):
            for tb in range(NTB):
                jobs.append(("k", qc, tb))
        for tt in range(NTT):
            for vb in range(2):
                jobs.append(("v", tt, vb))
        njq = len(jobs)
        if wr is not None:
            for tt in range(NTT):
                jobs.append(("r", tt, 0))
        ev_engine = []
        for kind, a1, a2 in jobs:
            if kind in ("q", "k"):
                ev_engine.append("sc")
            elif kind == "v":
                ev_engine.append("v" if bv is not None else "sc")
            else:
                ev_engine.append("v")
        ev_cnt = []
        nsc = nv = 0
        for engn in ev_engine:
            if engn == "sc":
                nsc += 1
                ev_cnt.append((sE, bE + nsc))
            else:
                nv += 1
                ev_cnt.append((sEV, bEV + nv))
        nloads_g = NDC + 3 + (1 if wr is not None else 0)
        nloads_l = 1 + (bk is not None) + (bv is not None) + (br is not None)

        with nc.Block() as b:
            @b.gpsimd
            def _(g):
                for c in range(NDC):
                    src = bass.AP(xnT, c * 128 * 512,
                                  [[512, 128], [NDC * 128 * 512, NTB], [1, 512]])
                    p.inc(nm + "_G",
                          g.dma_start(out=xnT_sb[:, c * T:(c + 1) * T], in_=src),
                          16)
                p.inc(nm + "_G", g.dma_start(out=Wq_sb[:, :],
                                             in_=wchunks_ap(wq, D, D)), 16)
                p.inc(nm + "_G", g.dma_start(out=Wk_sb[:, :],
                                             in_=wchunks_ap(wk, D, D)), 16)
                p.inc(nm + "_G", g.dma_start(out=Wv_sb[:, :],
                                             in_=wchunks_ap(wv, D, D)), 16)
                if wr is not None:
                    p.inc(nm + "_G", g.dma_start(out=Wr_sb[:, :],
                                                 in_=wchunks_ap(wr, D, H * NB)), 16)

            @b.sync
            def _(s):
                p.inc(nm + "_L", s.dma_start(out=bq_sb[:, :],
                                             in_=bias_chunks_ap(bq, NDC),
                                             allow_slow_non_contiguous=True), 16)
                if bk is not None:
                    p.inc(nm + "_L", s.dma_start(out=bk_sb[:, :],
                                                 in_=bias_chunks_ap(bk, NDC),
                                                 allow_slow_non_contiguous=True), 16)
                if bv is not None:
                    p.inc(nm + "_L", s.dma_start(out=bv_sb[:, :],
                                                 in_=row_bcast_ap(bv, D)), 16)
                if br is not None:
                    p.inc(nm + "_L", s.dma_start(out=br_sb[:, :],
                                                 in_=row_bcast_ap(br, H * NB)), 16)
                for j, (kind, a1, a2) in enumerate(jobs):
                    if kind == "r":
                        continue
                    sem, cntv = ev_cnt[j]
                    s.wait_ge(sem, cntv)
                    if kind in ("q", "k"):
                        dst_t = QT if kind == "q" else KT
                        dst = bass.AP(dst_t, a1 * 128 * T + a2 * 512,
                                      [[T, 128], [1, 512]])
                    else:
                        dst = bass.AP(Vs, a1 * 128 * D + a2 * 512,
                                      [[D, 128], [1, 512]])
                    p.inc(nm + "_ST",
                          s.dma_start(out=dst, in_=stg[j % 4][:, :]), 16)
                s.wait_ge(sST, p.cnt[nm + "_ST"])

            @b.tensor
            def _(t):
                t.wait_ge(sG, bG + 16 * nloads_g)
                for j, (kind, a1, a2) in enumerate(jobs):
                    if kind == "r":
                        jr = j - njq
                        if jr >= 2:
                            sem, cntv = ev_cnt[njq + jr - 2]
                            t.wait_ge(sem, cntv)
                        pp = psR[jr % 2]
                        for dc in range(NDC):
                            ins = t.matmul(
                                pp[:, :],
                                xnT_sb[:, dc * T + a1 * 128:dc * T + a1 * 128 + 128],
                                Wr_sb[:, dc * H * NB:(dc + 1) * H * NB],
                                start=(dc == 0), stop=(dc == NDC - 1))
                        p.inc(nm + "_T", ins)
                        continue
                    if j >= 4:
                        sem, cntv = ev_cnt[j - 4]
                        t.wait_ge(sem, cntv)
                    pp = ps[j % 4]
                    for dc in range(NDC):
                        if kind == "q":
                            lhsT = Wq_sb[:, dc * D + a1 * 128:dc * D + a1 * 128 + 128]
                            rhs = xnT_sb[:, dc * T + a2 * 512:dc * T + a2 * 512 + 512]
                        elif kind == "k":
                            lhsT = Wk_sb[:, dc * D + a1 * 128:dc * D + a1 * 128 + 128]
                            rhs = xnT_sb[:, dc * T + a2 * 512:dc * T + a2 * 512 + 512]
                        else:
                            lhsT = xnT_sb[:, dc * T + a1 * 128:dc * T + a1 * 128 + 128]
                            rhs = Wv_sb[:, dc * D + a2 * 512:dc * D + a2 * 512 + 512]
                        ins = t.matmul(pp[:, :], lhsT, rhs,
                                       start=(dc == 0), stop=(dc == NDC - 1))
                    p.inc(nm + "_T", ins)

            @b.scalar
            def _(sc):
                sc.wait_ge(sL, bL + 16 * nloads_l)
                sc.wait_ge(p.sem(nm + "_Vq"), p.cnt[nm + "_Vq"])
                for j, (kind, a1, a2) in enumerate(jobs):
                    if ev_engine[j] != "sc":
                        continue
                    sc.wait_ge(sT, bT + j + 1)
                    if j >= 4:
                        sc.wait_ge(sST, bST + 16 * (j - 3))
                    if kind == "q":
                        ins = sc.activation(stg[j % 4][:, :], ps[j % 4][:, :],
                                            AF.Identity, scale=qscale,
                                            bias=bq_sb[:, a1:a1 + 1])
                    elif kind == "k" and bk is not None:
                        ins = sc.activation(stg[j % 4][:, :], ps[j % 4][:, :],
                                            AF.Identity, bias=bk_sb[:, a1:a1 + 1])
                    else:
                        ins = sc.copy(stg[j % 4][:, :], ps[j % 4][:, :])
                    p.inc(nm + "_E", ins)

            @b.vector
            def _(v):
                v.wait_ge(sL, bL + 16)
                p.inc(nm + "_Vq", v.tensor_scalar(bq_sb[:, :], bq_sb[:, :],
                                                  qscale, None, ALU.mult))
                v.wait_ge(sL, bL + 16 * nloads_l)
                for j, (kind, a1, a2) in enumerate(jobs):
                    if ev_engine[j] != "v":
                        continue
                    v.wait_ge(sT, bT + j + 1)
                    if kind == "r":
                        jr = j - njq
                        p.inc(nm + "_EV",
                              v.tensor_tensor(
                                  R_sb[:, jr * H * NB:(jr + 1) * H * NB],
                                  psR[jr % 2][:, :], br_sb[:, :], ALU.add))
                    else:
                        if j >= 4:
                            v.wait_ge(sST, bST + 16 * (j - 3))
                        p.inc(nm + "_EV",
                              v.tensor_tensor(stg[j % 4][:, :], ps[j % 4][:, :],
                                              bv_sb[:, a2 * 512:a2 * 512 + 512],
                                              ALU.add))

    phase_barrier(p)


def emit_l0_attn(p, consts, win):
    nc = p.nc
    QT = p.dram["p1_QT"]
    KT = p.dram["p1_KT"]
    Vs = p.dram["p1_V"]
    DTin = win["DT"]
    A2T = p.scratch("a0_A2T", [NDC, 128, T], BF16)
    den_sc = p.scratch("a0_den", [1, T], F32)
    id_b = consts["id_b"]
    ones_b = consts["ones_b"]
    negt_b = consts["negt_b"]
    R_sb = consts["R_sb"]

    def ptiles(tb):
        t0 = tb * 512
        lo = max(0, t0 - ML + 1) // 128
        hi = (t0 + 512) // 128
        return list(range(lo, hi))

    tiles_per_tb = [len(ptiles(tb)) for tb in range(NTB)]
    spq = []
    for h in range(H):
        spq += tiles_per_tb
    cum_s = np.cumsum([0] + spq)

    for s in ("a0_G", "a0_GV", "a0_LQ", "a0_Ts", "a0_Se", "a0_Ta", "a0_Sd",
              "a0_Sa", "a0_Gd", "a0_Vr", "a0_Va2", "a0_Ga2", "a0_Vx"):
        p.sem(s)
    B = {s: p.cnt[s] for s in p.cnt}

    def scount(q):
        return B["a0_Ts"] + int(cum_s[q])

    with ExitStack() as scope:
        e = scope.enter_context
        V_h = [e(nc.sbuf_tensor(f"a0_Vh{i}", [128, NTT * DQ], BF16))
               for i in range(2)]
        Bexp = e(nc.sbuf_tensor("a0_Bexp", [128, 9 * 11 * 128], BF16))
        QTh = [e(nc.sbuf_tensor(f"a0_QT{i}", [64, T], BF16)) for i in range(2)]
        KTh = [e(nc.sbuf_tensor(f"a0_KT{i}", [64, T], BF16)) for i in range(2)]
        Rexp = [e(nc.sbuf_tensor(f"a0_Rexp{i}", [128, 4 * 1408], BF16))
                for i in range(2)]
        P_sb = [e(nc.sbuf_tensor(f"a0_P{i}", [128, 512], BF16)) for i in range(4)]
        A_un = e(nc.sbuf_tensor("a0_Aun", [64, T], F32))
        denrow = e(nc.sbuf_tensor("a0_denrow", [1, T], F32))
        denb = e(nc.sbuf_tensor("a0_denb", [128, T], F32))
        A2st = e(nc.sbuf_tensor("a0_A2st", [64, T], BF16))
        psS = [e(nc.psum_tensor(f"a0_psS{i}", [128, 512], F32)) for i in range(4)]
        psA = [e(nc.psum_tensor(f"a0_psA{i}", [64, 512], F32))
               for i in range(2)]
        psD = [e(nc.psum_tensor(f"a0_psD{i}", [1, 512], F32))
               for i in range(2)]

        with nc.Block() as b:
            def load_vh(g, h):
                if h >= 2:
                    g.wait_ge(p.sem("a0_Ta"), B["a0_Ta"] + int(cum_s[(h - 1) * NTB]))
                vsrc = bass.AP(Vs, h * DQ, [[D, 128], [128 * D, NTT], [1, DQ]])
                p.inc("a0_GV", g.dma_start(out=V_h[h % 2][:, :], in_=vsrc), 16)

            @b.gpsimd
            def _(g):
                for k in range(9):
                    src = bass.AP(DTin, k * 11 * 128 * 128,
                                  [[128, 128], [128 * 128, 11], [1, 128]])
                    p.inc("a0_G", g.dma_start(
                        out=Bexp[:, k * 1408:(k + 1) * 1408], in_=src), 16)
                load_vh(g, 0)
                load_vh(g, 1)
                for h in range(H):
                    if h + 2 < H:
                        load_vh(g, h + 2)
                    g.wait_ge(p.sem("a0_Sd"), B["a0_Sd"] + NTB * (h + 1))
                    p.inc("a0_Gd", g.dma_start(out=den_sc.ap(), in_=denrow[:, :]), 16)
                    g.wait_ge(p.sem("a0_Gd"), B["a0_Gd"] + 32 * h + 16)
                    if h > 0:
                        g.wait_ge(p.sem("a0_Va2"), B["a0_Va2"] + h)
                    p.inc("a0_Gd", g.dma_start(
                        out=denb[:, :],
                        in_=bass.AP(den_sc, 0, [[0, 128], [1, T]])), 16)
                    g.wait_ge(p.sem("a0_Va2"), B["a0_Va2"] + h + 1)
                    dst = bass.AP(A2T, (h // 2) * 128 * T + (h % 2) * 64 * T,
                                  [[T, 64], [1, T]])
                    p.inc("a0_Ga2", g.dma_start(out=dst, in_=A2st[:, :]), 16)
                g.wait_ge(p.sem("a0_Ga2"), p.cnt["a0_Ga2"])

            @b.sync
            def _(s):
                for h in range(H):
                    if h >= 2:
                        s.wait_ge(p.sem("a0_Ts"), scount(8 * (h - 1)))
                    p.inc("a0_LQ", s.dma_start(
                        out=QTh[h % 2][:, :],
                        in_=bass.AP(QT, h * DQ * T, [[T, 64], [1, T]])), 16)
                    p.inc("a0_LQ", s.dma_start(
                        out=KTh[h % 2][:, :],
                        in_=bass.AP(KT, h * DQ * T, [[T, 64], [1, T]])), 16)

            def emit_tail(v, h):
                v.wait_ge(p.sem("a0_Gd"), B["a0_Gd"] + 32 * (h + 1))
                p.inc("a0_Vr", v.reciprocal(denb[:, :], denb[:, :]))
                v.wait_ge(p.sem("a0_Vr"), p.cnt["a0_Vr"])
                v.wait_ge(p.sem("a0_Sa"), B["a0_Sa"] + NTB * (h + 1))
                if h > 0:
                    v.wait_ge(p.sem("a0_Ga2"), B["a0_Ga2"] + 16 * h)
                p.inc("a0_Va2", v.tensor_tensor(A2st[:, :], A_un[:, :],
                                                denb[0:64, :], ALU.mult))

            @b.vector
            def _(v):
                for r in range(2):
                    for j in range(4):
                        v.tensor_copy(Rexp[r][:, j * 1408 + 1280:(j + 1) * 1408],
                                      id_b[:, :])
                for h in range(H):
                    for tb in range(NTB):
                        q = h * NTB + tb
                        if q >= 2:
                            v.wait_ge(p.sem("a0_Ts"), scount(q - 1))
                        for j in range(4):
                            qt = tb * 4 + j
                            for n in range(NB):
                                ins = v.tensor_scalar(
                                    Rexp[q % 2][:, j * 1408 + n * 128:
                                                j * 1408 + (n + 1) * 128],
                                    id_b[:, :],
                                    R_sb[:, qt * H * NB + h * NB + n:
                                         qt * H * NB + h * NB + n + 1],
                                    None, ALU.mult)
                        p.inc("a0_Vx", ins)
                        if tb == 0 and h > 0:
                            emit_tail(v, h - 1)
                emit_tail(v, H - 1)

            def emit_av(t, h, gsl, pt, first, last, q):
                t.wait_ge(p.sem("a0_Se"), B["a0_Se"] + gsl + 1)
                if first and q >= 2:
                    t.wait_ge(p.sem("a0_Sa"), B["a0_Sa"] + q - 1)
                    t.wait_ge(p.sem("a0_Sd"), B["a0_Sd"] + q - 1)
                t.matmul(psA[q % 2][:, :], V_h[h % 2][:, pt * DQ:(pt + 1) * DQ],
                         P_sb[gsl % 4][:, :], start=first, stop=last)
                ins = t.matmul(psD[q % 2][:, :], ones_b[:, 0:1],
                               P_sb[gsl % 4][:, :], start=first, stop=last)
                p.inc("a0_Ta", ins)

            @b.tensor
            def _(t):
                t.wait_ge(p.sem("a0_G"), B["a0_G"] + 16 * 9)
                gs = 0
                for h in range(H):
                    t.wait_ge(p.sem("a0_LQ"), B["a0_LQ"] + 32 * (h + 1))
                    t.wait_ge(p.sem("a0_GV"), B["a0_GV"] + 16 * (h + 1))
                    for tb in range(NTB):
                        q = h * NTB + tb
                        t0 = tb * 512
                        t.wait_ge(p.sem("a0_Vx"), B["a0_Vx"] + q + 1)
                        pts = ptiles(tb)
                        for pi, pt in enumerate(pts):
                            if gs >= 4:
                                t.wait_ge(p.sem("a0_Se"), B["a0_Se"] + gs - 3)
                            pp = psS[gs % 4]
                            ops = [(KTh[h % 2][:, pt * 128:(pt + 1) * 128],
                                    QTh[h % 2][:, t0:t0 + 512], None)]
                            for j in range(4):
                                dl = t0 + 128 * j - pt * 128
                                if 0 <= dl <= ML:
                                    ki = dl // 128
                                    for n in range(NB):
                                        ops.append((
                                            Bexp[:, ki * 1408 + n * 128:
                                                 ki * 1408 + (n + 1) * 128],
                                            Rexp[q % 2][:, j * 1408 + n * 128:
                                                        j * 1408 + (n + 1) * 128],
                                            j))
                                    if dl == 0 or dl == ML:
                                        ops.append((
                                            Bexp[:, ki * 1408 + NB * 128:
                                                 ki * 1408 + 1408],
                                            id_b[:, :], j))
                                else:
                                    ops.append((negt_b[:, :], id_b[:, :], j))
                            for oi, (lhsT, rhs, j) in enumerate(ops):
                                out = pp[:, :] if j is None else \
                                    pp[:, j * 128:(j + 1) * 128]
                                ins = t.matmul(out, lhsT, rhs, start=(oi == 0),
                                               stop=(oi == len(ops) - 1))
                            p.inc("a0_Ts", ins)
                            if pi >= 1:
                                emit_av(t, h, gs - 1, pts[pi - 1],
                                        first=(pi == 1), last=False, q=q)
                            gs += 1
                        emit_av(t, h, gs - 1, pts[-1], first=(len(pts) == 1),
                                last=True, q=q)
                assert gs == int(cum_s[-1])

            @b.scalar
            def _(sc):
                gs = 0
                for h in range(H):
                    for tb in range(NTB):
                        q = h * NTB + tb
                        for pi in range(len(ptiles(tb))):
                            sc.wait_ge(p.sem("a0_Ts"), B["a0_Ts"] + gs + 1)
                            if gs >= 4:
                                sc.wait_ge(p.sem("a0_Ta"), B["a0_Ta"] + gs - 3)
                            p.inc("a0_Se", sc.activation(P_sb[gs % 4][:, :],
                                                         psS[gs % 4][:, :], AF.Exp))
                            gs += 1
                        sc.wait_ge(p.sem("a0_Ta"), B["a0_Ta"] + gs)
                        if h > 0 and tb == 0:
                            sc.wait_ge(p.sem("a0_Gd"), B["a0_Gd"] + 32 * h - 16)
                            sc.wait_ge(p.sem("a0_Va2"), B["a0_Va2"] + h)
                        p.inc("a0_Sd", sc.copy(denrow[:, tb * 512:tb * 512 + 512],
                                               psD[q % 2][:, :]))
                        p.inc("a0_Sa", sc.copy(A_un[:, tb * 512:tb * 512 + 512],
                                               psA[q % 2][:, :]))

    phase_barrier(p)


def emit_l1_attn(p, consts, win):
    nc = p.nc
    QT = p.dram["p5_QT"]
    KT = p.dram["p5_KT"]
    Vs = p.dram["p5_V"]
    DTin = win["DT"]
    A2T = p.scratch("a1_A2T", [NDC, 128, T], BF16)
    den_sc = p.scratch("a1_den", [1, T], F32)
    id_b = consts["id_b"]
    ones_b = consts["ones_b"]
    negt_b = consts["negt_b"]

    def ptiles(tb):
        return list(range(0, tb * 4 + 4))

    tiles_per_tb = [len(ptiles(tb)) for tb in range(NTB)]
    spq = []
    for h in range(H):
        spq += tiles_per_tb
    cum_s = np.cumsum([0] + spq)

    for s in ("a1_G", "a1_GV", "a1_LQ", "a1_Ts", "a1_Se", "a1_Ta", "a1_Sd",
              "a1_Sa", "a1_Gd", "a1_Vr", "a1_Va2", "a1_Ga2"):
        p.sem(s)
    B = {s: p.cnt[s] for s in p.cnt}

    def scount(q):
        return B["a1_Ts"] + int(cum_s[q])

    with ExitStack() as scope:
        e = scope.enter_context
        V_h = [e(nc.sbuf_tensor(f"a1_Vh{i}", [128, NTT * DQ], BF16))
               for i in range(2)]
        cmask = e(nc.sbuf_tensor("a1_cmask", [128, 128], BF16))
        QTh = [e(nc.sbuf_tensor(f"a1_QT{i}", [64, T], BF16)) for i in range(2)]
        KTh = [e(nc.sbuf_tensor(f"a1_KT{i}", [64, T], BF16)) for i in range(2)]
        P_sb = [e(nc.sbuf_tensor(f"a1_P{i}", [128, 512], BF16)) for i in range(4)]
        A_un = e(nc.sbuf_tensor("a1_Aun", [64, T], F32))
        denrow = e(nc.sbuf_tensor("a1_denrow", [1, T], F32))
        denb = e(nc.sbuf_tensor("a1_denb", [128, T], F32))
        A2st = e(nc.sbuf_tensor("a1_A2st", [64, T], BF16))
        psS = [e(nc.psum_tensor(f"a1_psS{i}", [128, 512], F32)) for i in range(4)]
        psA = [e(nc.psum_tensor(f"a1_psA{i}", [64, 512], F32))
               for i in range(2)]
        psD = [e(nc.psum_tensor(f"a1_psD{i}", [1, 512], F32))
               for i in range(2)]

        with nc.Block() as b:
            def load_vh(g, h):
                if h >= 2:
                    g.wait_ge(p.sem("a1_Ta"), B["a1_Ta"] + int(cum_s[(h - 1) * NTB]))
                vsrc = bass.AP(Vs, h * DQ, [[D, 128], [128 * D, NTT], [1, DQ]])
                p.inc("a1_GV", g.dma_start(out=V_h[h % 2][:, :], in_=vsrc), 16)

            @b.gpsimd
            def _(g):
                msrc = bass.AP(DTin, NB * 128 * 128, [[128, 128], [1, 128]])
                p.inc("a1_G", g.dma_start(out=cmask[:, :], in_=msrc), 16)
                load_vh(g, 0)
                load_vh(g, 1)
                for h in range(H):
                    if h + 2 < H:
                        load_vh(g, h + 2)
                    g.wait_ge(p.sem("a1_Sd"), B["a1_Sd"] + NTB * (h + 1))
                    p.inc("a1_Gd", g.dma_start(out=den_sc.ap(), in_=denrow[:, :]), 16)
                    g.wait_ge(p.sem("a1_Gd"), B["a1_Gd"] + 32 * h + 16)
                    if h > 0:
                        g.wait_ge(p.sem("a1_Va2"), B["a1_Va2"] + h)
                    p.inc("a1_Gd", g.dma_start(
                        out=denb[:, :],
                        in_=bass.AP(den_sc, 0, [[0, 128], [1, T]])), 16)
                    g.wait_ge(p.sem("a1_Va2"), B["a1_Va2"] + h + 1)
                    dst = bass.AP(A2T, (h // 2) * 128 * T + (h % 2) * 64 * T,
                                  [[T, 64], [1, T]])
                    p.inc("a1_Ga2", g.dma_start(out=dst, in_=A2st[:, :]), 16)
                g.wait_ge(p.sem("a1_Ga2"), p.cnt["a1_Ga2"])

            @b.sync
            def _(s):
                for h in range(H):
                    if h >= 2:
                        s.wait_ge(p.sem("a1_Ts"), scount(8 * (h - 1)))
                    p.inc("a1_LQ", s.dma_start(
                        out=QTh[h % 2][:, :],
                        in_=bass.AP(QT, h * DQ * T, [[T, 64], [1, T]])), 16)
                    p.inc("a1_LQ", s.dma_start(
                        out=KTh[h % 2][:, :],
                        in_=bass.AP(KT, h * DQ * T, [[T, 64], [1, T]])), 16)

            def emit_tail(v, h):
                v.wait_ge(p.sem("a1_Gd"), B["a1_Gd"] + 32 * (h + 1))
                p.inc("a1_Vr", v.reciprocal(denb[:, :], denb[:, :]))
                v.wait_ge(p.sem("a1_Vr"), p.cnt["a1_Vr"])
                v.wait_ge(p.sem("a1_Sa"), B["a1_Sa"] + NTB * (h + 1))
                if h > 0:
                    v.wait_ge(p.sem("a1_Ga2"), B["a1_Ga2"] + 16 * h)
                p.inc("a1_Va2", v.tensor_tensor(A2st[:, :], A_un[:, :],
                                                denb[0:64, :], ALU.mult))

            @b.vector
            def _(v):
                for h in range(H):
                    emit_tail(v, h)

            def emit_av(t, h, gsl, pt, first, last, q):
                t.wait_ge(p.sem("a1_Se"), B["a1_Se"] + gsl + 1)
                if first and q >= 2:
                    t.wait_ge(p.sem("a1_Sa"), B["a1_Sa"] + q - 1)
                    t.wait_ge(p.sem("a1_Sd"), B["a1_Sd"] + q - 1)
                t.matmul(psA[q % 2][:, :], V_h[h % 2][:, pt * DQ:(pt + 1) * DQ],
                         P_sb[gsl % 4][:, :], start=first, stop=last)
                ins = t.matmul(psD[q % 2][:, :], ones_b[:, 0:1],
                               P_sb[gsl % 4][:, :], start=first, stop=last)
                p.inc("a1_Ta", ins)

            @b.tensor
            def _(t):
                t.wait_ge(p.sem("a1_G"), B["a1_G"] + 16)
                gs = 0
                for h in range(H):
                    t.wait_ge(p.sem("a1_LQ"), B["a1_LQ"] + 32 * (h + 1))
                    t.wait_ge(p.sem("a1_GV"), B["a1_GV"] + 16 * (h + 1))
                    for tb in range(NTB):
                        q = h * NTB + tb
                        t0 = tb * 512
                        pts = ptiles(tb)
                        for pi, pt in enumerate(pts):
                            if gs >= 4:
                                t.wait_ge(p.sem("a1_Se"), B["a1_Se"] + gs - 3)
                            pp = psS[gs % 4]
                            ops = [(KTh[h % 2][:, pt * 128:(pt + 1) * 128],
                                    QTh[h % 2][:, t0:t0 + 512], None)]
                            for j in range(4):
                                dl = t0 + 128 * j - pt * 128
                                if dl < 0:
                                    ops.append((negt_b[:, :], id_b[:, :], j))
                                elif dl == 0:
                                    ops.append((cmask[:, :], id_b[:, :], j))
                            for oi, (lhsT, rhs, j) in enumerate(ops):
                                out = pp[:, :] if j is None else \
                                    pp[:, j * 128:(j + 1) * 128]
                                ins = t.matmul(out, lhsT, rhs, start=(oi == 0),
                                               stop=(oi == len(ops) - 1))
                            p.inc("a1_Ts", ins)
                            if pi >= 1:
                                emit_av(t, h, gs - 1, pts[pi - 1],
                                        first=(pi == 1), last=False, q=q)
                            gs += 1
                        emit_av(t, h, gs - 1, pts[-1], first=(len(pts) == 1),
                                last=True, q=q)
                assert gs == int(cum_s[-1])

            @b.scalar
            def _(sc):
                gs = 0
                for h in range(H):
                    for tb in range(NTB):
                        q = h * NTB + tb
                        for pi in range(len(ptiles(tb))):
                            sc.wait_ge(p.sem("a1_Ts"), B["a1_Ts"] + gs + 1)
                            if gs >= 4:
                                sc.wait_ge(p.sem("a1_Ta"), B["a1_Ta"] + gs - 3)
                            p.inc("a1_Se", sc.activation(P_sb[gs % 4][:, :],
                                                         psS[gs % 4][:, :], AF.Exp))
                            gs += 1
                        sc.wait_ge(p.sem("a1_Ta"), B["a1_Ta"] + gs)
                        if h > 0 and tb == 0:
                            sc.wait_ge(p.sem("a1_Gd"), B["a1_Gd"] + 32 * h - 16)
                            sc.wait_ge(p.sem("a1_Va2"), B["a1_Va2"] + h)
                        p.inc("a1_Sd", sc.copy(denrow[:, tb * 512:tb * 512 + 512],
                                               psD[q % 2][:, :]))
                        p.inc("a1_Sa", sc.copy(A_un[:, tb * 512:tb * 512 + 512],
                                               psA[q % 2][:, :]))

    phase_barrier(p)


def emit_l1_ffn(p, consts, win):
    nc = p.nc
    xnT = p.dram["xn1T"]
    f1 = p.scratch("f1", [T, D], F32)
    wrin = win["L1_ffn_w_router"]
    wup, wgt, wdn = win["L1_ffn_w_up"], win["L1_ffn_w_gate"], win["L1_ffn_w_down"]
    wsu, wsg, wsd = win["L1_ffn_ws_up"], win["L1_ffn_ws_gate"], win["L1_ffn_ws_down"]
    Wub = p.scratch("p7_Wub", [8, NFC // 2, 128, 2, 8, 128], BF16)
    Wgb = p.scratch("p7_Wgb", [8, NFC // 2, 128, 2, 8, 128], BF16)
    Wdb = p.scratch("p7_Wdb", [8, NFC, 128, D], BF16)
    id_f = consts["id_f"]
    NU, NTB4, NSUB = 8, 4, 8

    sems = ["p7_L0", "p7_G0", "p7_LW", "p7_GW", "p7_WB", "p7_Tg", "p7_Sg",
            "p7_Tt", "p7_Vt", "p7_Vg1", "p7_Sg2", "p7_Va", "p7_Vb", "p7_Vg",
            "p7_Th", "p7_Su", "p7_Vh", "p7_Tw", "p7_Ve", "p7_SC", "p7_GX"]
    for s in sems:
        p.sem(s)
    B = {s: p.cnt[s] for s in sems}

    def uw(u):
        if u == 0:
            return (wsg, wsu, wsd, 0)
        return (wgt, wup, wdn, u - 1)

    tgt = {}
    wb_after_u = {u: B["p7_WB"] + 16 * 96 * (u + 1) for u in range(NU)}

    def gH(tb, u):
        return B["p7_Th"] + (tb * NU + u) * 128

    def gSu(tb, u):
        return B["p7_Su"] + (tb * NU + u) * 64

    def gVh(tb, u):
        return B["p7_Vh"] + (tb * NU + u) * 64

    def gW(tb, u, pas):
        return B["p7_Tw"] + ((tb * NU + u) * 2 + pas) * 32

    def gVe(tb, u):
        return B["p7_Ve"] + (tb * NU + u) * 16

    with ExitStack() as scope:
        e_ = scope.enter_context
        xnT_sb = e_(nc.sbuf_tensor("p7_xnT", [128, NDC * T], BF16))
        H_sb = e_(nc.sbuf_tensor("p7_H", [128, NFC * 1024], BF16))
        acc = e_(nc.sbuf_tensor("p7_acc", [128, NSUB * D], F32))
        Gs = [e_(nc.sbuf_tensor(f"p7_Gs{i}", [128, 512], F32)) for i in range(2)]
        Wgsl = [e_(nc.sbuf_tensor(f"p7_Wgsl{i}", [128, 2 * 8 * 128], BF16))
                for i in range(2)]
        Wusl = [e_(nc.sbuf_tensor(f"p7_Wusl{i}", [128, 2 * 8 * 128], BF16))
                for i in range(2)]
        Wdsl = [e_(nc.sbuf_tensor(f"p7_Wdsl{i}", [128, 2 * 512], BF16))
                for i in range(2)]
        wr_sb = e_(nc.sbuf_tensor("p7_wr", [128, NDC * 8], BF16))
        logT8 = [e_(nc.sbuf_tensor(f"p7_lg{i}", [8, 512], F32)) for i in range(2)]
        lgn = e_(nc.sbuf_tensor("p7_lgn", [128, NTT * 8], F32))
        wn = e_(nc.sbuf_tensor("p7_wn", [128, NTT * 8], F32))
        wk = e_(nc.sbuf_tensor("p7_wk", [128, 8 * 32], F32))
        wk2 = e_(nc.sbuf_tensor("p7_wk2", [128, 16 * 32], F32))
        exio = e_(nc.sbuf_tensor("p7_exio", [128, 18 * 32], F32))
        exchk = e_(nc.sbuf_tensor("p7_exchk", [128, 32], F32))
        ps = [e_(nc.psum_tensor(f"p7_ps{i}", [128, 512], F32)) for i in range(8)]

        def W(i):
            return wk[:, i * 32:(i + 1) * 32]

        def lgn_e(t, eix):
            return bass.AP(t, eix, [[NTT * 8, 128], [8, NTT]])

        with nc.Block() as b:
            @b.sync
            def _(s):
                for tb in range(NTB4):
                    for u in range(NU):
                        if tb == 0:
                            continue
                        if tb == 1:
                            s.wait_ge(p.sem("p7_WB"), wb_after_u[u])
                        for fb in range(NFC // 2):
                            for kind, Wb_t, sl in (("wg", Wgb, Wgsl),
                                                   ("wu", Wub, Wusl)):
                                tv = gH(tb, u) + 8 * (fb - 1)
                                if tv > B["p7_Th"]:
                                    s.wait_ge(p.sem("p7_Th"), tv)
                                srcw = bass.AP(Wb_t, (u * 16 + fb) * 128 * 2048,
                                               [[2048, 128], [1, 2048]])
                                ins = s.dma_start(out=sl[fb % 2][:, :], in_=srcw)
                                p.inc("p7_LW", ins, 16)
                                for fc in range(2 * fb, 2 * fb + 2):
                                    tgt[(tb, u, kind, fc)] =                                         ("p7_LW", p.cnt["p7_LW"])
                        for pas in range(2):
                            for fb in range(NFC // 2):
                                tv = gW(tb, u, pas) + 2 * (fb - 1)
                                if tv > B["p7_Tw"]:
                                    s.wait_ge(p.sem("p7_Tw"), tv)
                                srcw = bass.AP(
                                    Wdb, ((u * NFC) + 2 * fb) * 128 * D + pas * 512,
                                    [[D, 128], [128 * D, 2], [1, 512]])
                                ins = s.dma_start(out=Wdsl[fb % 2][:, :], in_=srcw)
                                p.inc("p7_LW", ins, 16)
                                for fc in range(2 * fb, 2 * fb + 2):
                                    tgt[(tb, u, "wd", fc, pas)] =                                         ("p7_LW", p.cnt["p7_LW"])

            @b.gpsimd
            def _(g):
                for c in range(NDC):
                    srcx = bass.AP(xnT, c * 128 * 512,
                                   [[512, 128], [NDC * 128 * 512, NTB], [1, 512]])
                    p.inc("p7_G0", g.dma_start(out=xnT_sb[:, c * T:(c + 1) * T],
                                               in_=srcx), 16)
                p.inc("p7_G0", g.dma_start(
                    out=wr_sb[:, :],
                    in_=bass.AP(wrin, 0, [[8, 128], [1024, NDC], [1, 8]]),
                    allow_slow_non_contiguous=True), 16)
                tb = 0
                for u in range(NU):
                    wg_d, wu_d, wd_d, ei = uw(u)
                    for fc in range(NFC):
                        fb, fq = fc // 2, fc % 2
                        for w_d, Wb_t in ((wg_d, Wgb), (wu_d, Wub)):
                            srcc = bass.AP(w_d, ei * D * FF + fc * 128,
                                           [[FF, 128], [128 * FF, 8], [1, 128]])
                            dstc = bass.AP(Wb_t,
                                           (u * 16 + fb) * 128 * 2048 + fq * 1024,
                                           [[2048, 128], [128, 8], [1, 128]])
                            p.inc("p7_WB", g.dma_start(out=dstc, in_=srcc), 16)
                        srcc = bass.AP(wd_d, ei * FF * D + fc * 128 * D,
                                       [[D, 128], [1, D]])
                        dstc = bass.AP(Wdb, ((u * NFC) + fc) * 128 * D,
                                       [[D, 128], [1, D]])
                        p.inc("p7_WB", g.dma_start(out=dstc, in_=srcc), 16)
                    for fc in range(NFC):
                        fb, fq = fc // 2, fc % 2
                        for kind, w_d, sl in (("wg", wg_d, Wgsl),
                                              ("wu", wu_d, Wusl)):
                            tv = gH(tb, u) + 8 * (fb - 1)
                            if tv > B["p7_Th"]:
                                g.wait_ge(p.sem("p7_Th"), tv)
                            srcw = bass.AP(w_d, ei * D * FF + fc * 128,
                                           [[FF, 128], [128 * FF, 8], [1, 128]])
                            dst = sl[fb % 2][:, fq * 1024:(fq + 1) * 1024]
                            p.inc("p7_GW", g.dma_start(out=dst, in_=srcw), 16)
                            tgt[(tb, u, kind, fc)] = ("p7_GW", p.cnt["p7_GW"])
                    for pas in range(2):
                        for fc in range(NFC):
                            fb, fq = fc // 2, fc % 2
                            tv = gW(tb, u, pas) + 2 * (fb - 1)
                            if tv > B["p7_Tw"]:
                                g.wait_ge(p.sem("p7_Tw"), tv)
                            srcw = bass.AP(wd_d, ei * FF * D + fc * 128 * D +
                                           pas * 512, [[D, 128], [1, 512]])
                            dst = Wdsl[fb % 2][:, fq * 512:(fq + 1) * 512]
                            p.inc("p7_GW", g.dma_start(out=dst, in_=srcw), 16)
                            tgt[(tb, u, "wd", fc, pas)] =                                 ("p7_GW", p.cnt["p7_GW"])
                for tb in range(NTB4):
                    g.wait_ge(p.sem("p7_Ve"), gVe(tb, NU - 1) + 16)
                    dstc = bass.AP(f1, tb * 1024 * D,
                                   [[D, 128], [128 * D, NSUB], [1, D]])
                    p.inc("p7_SC", g.dma_start(out=dstc, in_=acc[:, :]), 16)
                g.wait_ge(p.sem("p7_SC"), p.cnt["p7_SC"])
                g.wait_ge(p.sem("p7_WB"), p.cnt["p7_WB"])

            @b.scalar
            def _(sc):
                for tb8 in range(NTB):
                    sc.wait_ge(p.sem("p7_Tg"), B["p7_Tg"] + tb8 + 1)
                    if tb8 >= 2:
                        sc.wait_ge(p.sem("p7_Tt"), B["p7_Tt"] + tb8 - 1)
                    p.inc("p7_Sg", sc.copy(logT8[tb8 % 2][:, :], ps[0][0:8, :]))
                sc.wait_ge(p.sem("p7_Vg1"), B["p7_Vg1"] + 1)
                sc.activation(exio[:, 288:576], exio[:, 0:288], AF.Exp)
                p.inc("p7_Sg2", sc.copy(exchk[:, :], exio[:, 544:576]))
                for tb in range(NTB4):
                    for u in range(NU):
                        for fc in range(NFC):
                            for th in range(2):
                                k = fc * 2 + th
                                sc.wait_ge(p.sem("p7_Th"), gH(tb, u) + 2 * k + 1)
                                tv = gVh(tb, u) + k - 1
                                if tv > B["p7_Vh"]:
                                    sc.wait_ge(p.sem("p7_Vh"), tv)
                                p.inc("p7_Su", sc.activation(
                                    Gs[k % 2][:, :], ps[(2 * k) % 4][:, :],
                                    AF.Silu))

            @b.tensor
            def _(t):
                t.wait_ge(p.sem("p7_G0"), B["p7_G0"] + 16 * (NDC + 1))
                for tb8 in range(NTB):
                    if tb8 >= 1:
                        t.wait_ge(p.sem("p7_Sg"), B["p7_Sg"] + tb8)
                    for dc in range(NDC):
                        ins = t.matmul(
                            ps[0][0:8, :],
                            wr_sb[:, dc * 8:(dc + 1) * 8],
                            xnT_sb[:, dc * T + tb8 * 512:dc * T + tb8 * 512 + 512],
                            start=(dc == 0), stop=(dc == NDC - 1))
                    p.inc("p7_Tg", ins)
                    t.wait_ge(p.sem("p7_Sg"), B["p7_Sg"] + tb8 + 1)
                    if tb8 >= 1:
                        t.wait_ge(p.sem("p7_Vt"), B["p7_Vt"] + tb8)
                    for c4 in range(4):
                        ins = t.transpose(ps[1][:, c4 * 8:c4 * 8 + 8],
                                          logT8[tb8 % 2][0:8,
                                                         c4 * 128:(c4 + 1) * 128],
                                          id_f[0:8, 0:8])
                    p.inc("p7_Tt", ins)
                for tb in range(NTB4):
                    for u in range(NU):
                        for fc in range(NFC):
                            for kind in ("wg", "wu"):
                                sem, val = tgt[(tb, u, kind, fc)]
                                t.wait_ge(p.sem(sem), val)
                            for th in range(2):
                                k = fc * 2 + th
                                for ki, sl in enumerate((Wgsl, Wusl)):
                                    pp = ps[(2 * k + ki) % 4]
                                    gk = 2 * k + ki
                                    if gk <= 3:
                                        if (tb, u) == (0, 0):
                                            if gk <= 1:
                                                t.wait_ge(p.sem("p7_Vt"),
                                                          B["p7_Vt"] + NTB)
                                        else:
                                            t.wait_ge(p.sem("p7_Ve"),
                                                      gVe(tb, u))
                                    if ki == 0 and k >= 2:
                                        t.wait_ge(p.sem("p7_Su"),
                                                  gSu(tb, u) + k - 1)
                                    if ki == 1 and k >= 2:
                                        t.wait_ge(p.sem("p7_Vh"),
                                                  gVh(tb, u) + k - 1)
                                    fb, fq = fc // 2, fc % 2
                                    for dc in range(NDC):
                                        ins = t.matmul(
                                            pp[:, :],
                                            sl[fb % 2][:, fq * 1024 + dc * 128:
                                                       fq * 1024 + (dc + 1) * 128],
                                            xnT_sb[:, dc * T + tb * 1024 +
                                                   th * 512:dc * T + tb * 1024 +
                                                   th * 512 + 512],
                                            start=(dc == 0), stop=(dc == NDC - 1))
                                    p.inc("p7_Th", ins)
                        t.wait_ge(p.sem("p7_Vh"), gVh(tb, u) + 64)
                        if u == 0:
                            t.wait_ge(p.sem("p7_Vg"), B["p7_Vg"] + 1)
                        for pas in range(2):
                            t.wait_ge(p.sem("p7_Ve"), gVe(tb, u) + 8 * pas)
                            for fc in range(NFC):
                                sem, val = tgt[(tb, u, "wd", fc, pas)]
                                t.wait_ge(p.sem(sem), val)
                                fb, fq = fc // 2, fc % 2
                                for tt in range(NSUB):
                                    ins = t.matmul(
                                        ps[tt][:, :],
                                        H_sb[:, fc * 1024 + tt * 128:
                                             fc * 1024 + (tt + 1) * 128],
                                        Wdsl[fb % 2][:, fq * 512:(fq + 1) * 512],
                                        start=(fc == 0), stop=(fc == NFC - 1))
                                p.inc("p7_Tw", ins)

            @b.vector
            def _(v):
                for tb8 in range(NTB):
                    v.wait_ge(p.sem("p7_Tt"), B["p7_Tt"] + tb8 + 1)
                    p.inc("p7_Vt", v.tensor_copy(
                        lgn[:, tb8 * 32:(tb8 + 1) * 32], ps[1][:, 0:32]))
                c = [lgn_e(lgn, e) for e in range(8)]
                for i in range(4):
                    v.tensor_tensor(W(i), c[2 * i], c[2 * i + 1], ALU.max)
                v.tensor_tensor(W(4), W(0), W(1), ALU.max)
                v.tensor_tensor(W(5), W(2), W(3), ALU.max)
                v.tensor_tensor(W(6), W(4), W(5), ALU.max)
                for e in range(8):
                    v.tensor_tensor(exio[:, (e + 1) * 32:(e + 2) * 32],
                                    c[e], W(6), ALU.subtract)
                v.tensor_copy(exio[:, 0:32], W(6))
                p.inc("p7_Vg1", v.tensor_scalar(W(7), exio[:, 256:288], 1.0,
                                                None, ALU.mult))
                v.wait_ge(p.sem("p7_Sg2"), B["p7_Sg2"] + 1)
                for i in range(4):
                    v.tensor_tensor(wk2[:, i * 32:(i + 1) * 32],
                                    exio[:, (10 + 2 * i) * 32:(11 + 2 * i) * 32],
                                    exio[:, (11 + 2 * i) * 32:(12 + 2 * i) * 32],
                                    ALU.add)
                v.tensor_tensor(wk2[:, 4 * 32:5 * 32], wk2[:, 0:32],
                                wk2[:, 32:64], ALU.add)
                v.tensor_tensor(wk2[:, 5 * 32:6 * 32], wk2[:, 64:96],
                                wk2[:, 96:128], ALU.add)
                p.inc("p7_Va", v.tensor_tensor(wk2[:, 6 * 32:7 * 32],
                                               wk2[:, 4 * 32:5 * 32],
                                               wk2[:, 5 * 32:6 * 32], ALU.add))
                v.wait_ge(p.sem("p7_Va"), p.cnt["p7_Va"])
                p.inc("p7_Vb", v.reciprocal(wk2[:, 7 * 32:8 * 32],
                                            wk2[:, 6 * 32:7 * 32]))
                v.wait_ge(p.sem("p7_Vb"), p.cnt["p7_Vb"])
                v.tensor_scalar(wk2[:, 8 * 32:9 * 32], wk2[:, 7 * 32:8 * 32],
                                2.0, None, ALU.mult)
                for e in range(7):
                    v.tensor_tensor(wk2[:, (9 + e) * 32:(10 + e) * 32],
                                    c[e], W(6), ALU.is_equal)
                for e in range(7):
                    ins = v.tensor_tensor(lgn_e(wn, e),
                                          wk2[:, (9 + e) * 32:(10 + e) * 32],
                                          wk2[:, 8 * 32:9 * 32], ALU.mult)
                p.inc("p7_Vg", ins)
                for tb in range(NTB4):
                    for u in range(NU):
                        for fc in range(NFC):
                            for th in range(2):
                                k = fc * 2 + th
                                v.wait_ge(p.sem("p7_Th"), gH(tb, u) + 2 * k + 2)
                                v.wait_ge(p.sem("p7_Su"), gSu(tb, u) + k + 1)
                                if k <= 1 and (tb, u) != (0, 0):
                                    v.wait_ge(p.sem("p7_Tw"), gW(tb, u, 0))
                                p.inc("p7_Vh", v.tensor_tensor(
                                    H_sb[:, fc * 1024 + th * 512:
                                         fc * 1024 + th * 512 + 512],
                                    ps[(2 * k + 1) % 4][:, :], Gs[k % 2][:, :],
                                    ALU.mult))
                        for pas in range(2):
                            v.wait_ge(p.sem("p7_Tw"), gW(tb, u, pas) + NFC)
                            for tt in range(NSUB):
                                sl = acc[:, tt * D + pas * 512:
                                         tt * D + pas * 512 + 512]
                                if u == 0:
                                    if (tt == 0 and pas == 0 and tb > 0):
                                        v.wait_ge(p.sem("p7_SC"),
                                                  B["p7_SC"] + 16 * tb)
                                    ins = v.tensor_copy(sl, ps[tt][:, :])
                                else:
                                    wcol = wn[:, (tb * NSUB + tt) * 8 + (u - 1):
                                              (tb * NSUB + tt) * 8 + u]
                                    ins = v.scalar_tensor_tensor(
                                        sl, ps[tt][:, :], wcol, sl,
                                        ALU.mult, ALU.add)
                                p.inc("p7_Ve", ins)

    phase_barrier(p)


def emit_final(p, y_out):
    nc = p.nc
    x1 = p.dram["x1"]
    s1 = p.dram["s1"]
    f1 = p.dram["f1"]
    for s in ("p8_L", "p8_S", "p8_V", "p8_O"):
        p.sem(s)
    B = {s: p.cnt[s] for s in ("p8_L", "p8_S", "p8_V", "p8_O")}
    with ExitStack() as scope:
        e = scope.enter_context
        xt = [e(nc.sbuf_tensor(f"p8_x{i}", [128, D], F32)) for i in range(2)]
        st = [e(nc.sbuf_tensor(f"p8_s{i}", [128, D], F32)) for i in range(2)]
        ft = [e(nc.sbuf_tensor(f"p8_f{i}", [128, D], F32)) for i in range(2)]
        gt = e(nc.sbuf_tensor("p8_g", [128, D], F32))
        ot = [e(nc.sbuf_tensor(f"p8_o{i}", [128, D], F32)) for i in range(2)]
        with nc.Block() as b:
            @b.sync
            def _(s):
                for i in range(NTT):
                    if i >= 2:
                        s.wait_ge(p.sem("p8_V"), B["p8_V"] + i - 1)
                    for src, dstt in ((x1, xt), (s1, st), (f1, ft)):
                        p.inc("p8_L", s.dma_start(
                            out=dstt[i % 2][:, :],
                            in_=src.ap()[i * 128:(i + 1) * 128, :]), 16)

            @b.scalar
            def _(sc):
                for i in range(NTT):
                    sc.wait_ge(p.sem("p8_L"), B["p8_L"] + 48 * (i + 1))
                    if i >= 1:
                        sc.wait_ge(p.sem("p8_V"), B["p8_V"] + i)
                    p.inc("p8_S", sc.activation(gt[:, :], ft[i % 2][:, :],
                                                AF.Silu))

            @b.vector
            def _(v):
                for i in range(NTT):
                    v.wait_ge(p.sem("p8_S"), B["p8_S"] + i + 1)
                    v.tensor_tensor(gt[:, :], gt[:, :], st[i % 2][:, :], ALU.mult)
                    if i >= 2:
                        v.wait_ge(p.sem("p8_O"), B["p8_O"] + 16 * (i - 1))
                    p.inc("p8_V", v.tensor_tensor(ot[i % 2][:, :], xt[i % 2][:, :],
                                                  gt[:, :], ALU.add))

            @b.gpsimd
            def _(g):
                for i in range(NTT):
                    g.wait_ge(p.sem("p8_V"), B["p8_V"] + i + 1)
                    p.inc("p8_O", g.dma_start(
                        out=y_out.ap()[i * 128:(i + 1) * 128, :],
                        in_=ot[i % 2][:, :]), 16)
                g.wait_ge(p.sem("p8_O"), p.cnt["p8_O"])


def emit_proj_nat(p, nm, inT_name, w, brow, out_name, post_ln=None,
                  inT_layout="chunks"):
    nc = p.nc
    inT = p.scratch(inT_name, [NDC, 128, T], BF16)
    out = p.scratch(out_name, [T, D], F32)
    for s in (nm + "_G", nm + "_L", nm + "_T", nm + "_E", nm + "_ST",
              nm + "_V0", nm + "_S0", nm + "_V1", nm + "_S1", nm + "_V0a",
              nm + "_V0b"):
        p.sem(s)
    B = {s: p.cnt[s] for s in p.cnt}

    with ExitStack() as scope:
        e = scope.enter_context
        inT_sb = e(nc.sbuf_tensor(nm + "_inT", [128, NDC * T], BF16))
        W_sb = e(nc.sbuf_tensor(nm + "_W", [128, NDC * D], BF16))
        brow_sb = e(nc.sbuf_tensor(nm + "_brow", [128, D], F32))
        yt = [e(nc.sbuf_tensor(f"{nm}_yt{i}", [128, D], F32)) for i in range(2)]
        ps = [e(nc.psum_tensor(f"{nm}_ps{i}", [128, 512], F32)) for i in range(4)]
        if post_ln is not None:
            gam_sb = e(nc.sbuf_tensor(nm + "_gam", [128, D], F32))
            bet_sb = e(nc.sbuf_tensor(nm + "_bet", [128, D], F32))
            sq = e(nc.sbuf_tensor(nm + "_sq", [128, D], F32))
            mneg = e(nc.sbuf_tensor(nm + "_mneg", [128, 2], F32))
            var = e(nc.sbuf_tensor(nm + "_var", [128, 2], F32))
            rstd = e(nc.sbuf_tensor(nm + "_rstd", [128, 2], F32))

        with nc.Block() as b:
            @b.gpsimd
            def _(g):
                for c in range(NDC):
                    if inT_layout == "chunks":
                        src = bass.AP(inT, c * 128 * T, [[T, 128], [1, T]])
                    else:
                        src = bass.AP(inT, c * 128 * 512,
                                      [[512, 128], [NDC * 128 * 512, NTB],
                                       [1, 512]])
                    p.inc(nm + "_G",
                          g.dma_start(out=inT_sb[:, c * T:(c + 1) * T], in_=src), 16)
                p.inc(nm + "_G", g.dma_start(out=W_sb[:, :],
                                             in_=wchunks_ap(w, D, D)), 16)

            @b.sync
            def _(s):
                p.inc(nm + "_L", s.dma_start(out=brow_sb[:, :],
                                             in_=row_bcast_ap(brow, D)), 16)
                if post_ln is not None:
                    gamma, beta = post_ln
                    p.inc(nm + "_L", s.dma_start(out=gam_sb[:, :],
                                                 in_=row_bcast_ap(gamma, D)), 16)
                    p.inc(nm + "_L", s.dma_start(out=bet_sb[:, :],
                                                 in_=row_bcast_ap(beta, D)), 16)
                for tt in range(NTT):
                    s.wait_ge(p.sem(nm + "_V1" if post_ln else nm + "_E"),
                              B[nm + "_V1" if post_ln else nm + "_E"] +
                              ((tt + 1) if post_ln else 2 * (tt + 1)))
                    p.inc(nm + "_ST", s.dma_start(
                        out=out.ap()[tt * 128:(tt + 1) * 128, :],
                        in_=yt[tt % 2][:, :]), 16)
                s.wait_ge(p.sem(nm + "_ST"), p.cnt[nm + "_ST"])

            @b.tensor
            def _(t):
                t.wait_ge(p.sem(nm + "_G"), B[nm + "_G"] + 16 * (NDC + 1))
                t.wait_ge(p.sem(nm + "_L"),
                          B[nm + "_L"] + 16 * (3 if post_ln is not None else 1))
                for tt in range(NTT):
                    for eb in range(2):
                        jj = tt * 2 + eb
                        if jj >= 4:
                            t.wait_ge(p.sem(nm + "_E"), B[nm + "_E"] + jj - 3)
                        for dc in range(NDC):
                            ins = t.matmul(
                                ps[jj % 4][:, :],
                                inT_sb[:, dc * T + tt * 128:dc * T + tt * 128 + 128],
                                W_sb[:, dc * D + eb * 512:dc * D + eb * 512 + 512],
                                start=(dc == 0), stop=(dc == NDC - 1))
                        p.inc(nm + "_T", ins)

            @b.vector
            def _(v):
                for tt in range(NTT):
                    if tt >= 2:
                        v.wait_ge(p.sem(nm + "_ST"), B[nm + "_ST"] + 16 * (tt - 1))
                    for eb in range(2):
                        jj = tt * 2 + eb
                        v.wait_ge(p.sem(nm + "_T"), B[nm + "_T"] + jj + 1)
                        p.inc(nm + "_E", v.tensor_tensor(
                            yt[tt % 2][:, eb * 512:eb * 512 + 512],
                            ps[jj % 4][:, :], brow_sb[:, eb * 512:eb * 512 + 512],
                            ALU.add))
                    if post_ln is not None:
                        yv = yt[tt % 2]
                        g2 = tt % 2
                        p.inc(nm + "_V0a", v.reduce_sum(mneg[:, g2:g2 + 1],
                                                        yv[:, :], axis=AX.X))
                        v.wait_ge(p.sem(nm + "_V0a"), p.cnt[nm + "_V0a"])
                        p.inc(nm + "_V0",
                              v.tensor_scalar(mneg[:, g2:g2 + 1],
                                              mneg[:, g2:g2 + 1], -1.0 / D,
                                              None, ALU.mult))
                        v.wait_ge(p.sem(nm + "_S0"), B[nm + "_S0"] + tt + 1)
                        p.inc(nm + "_V0b", v.reduce_sum(var[:, g2:g2 + 1],
                                                        sq[:, :], axis=AX.X))
                        v.wait_ge(p.sem(nm + "_S1"), B[nm + "_S1"] + tt + 1)
                        v.tensor_scalar(yv[:, :], yv[:, :], mneg[:, g2:g2 + 1],
                                        rstd[:, g2:g2 + 1], ALU.add, ALU.mult)
                        v.tensor_tensor(yv[:, :], yv[:, :], gam_sb[:, :], ALU.mult)
                        p.inc(nm + "_V1", v.tensor_tensor(yv[:, :], yv[:, :],
                                                          bet_sb[:, :], ALU.add))

            if post_ln is not None:
                @b.scalar
                def _(sc):
                    for tt in range(NTT):
                        g2 = tt % 2
                        sc.wait_ge(p.sem(nm + "_V0"), B[nm + "_V0"] + tt + 1)
                        p.inc(nm + "_S0",
                              sc.activation(sq[:, :], yt[tt % 2][:, :], AF.Square,
                                            bias=mneg[:, g2:g2 + 1]))
                        sc.wait_ge(p.sem(nm + "_V0b"), B[nm + "_V0b"] + tt + 1)
                        sc.activation(var[:, g2:g2 + 1], var[:, g2:g2 + 1],
                                      AF.Ln, scale=1.0 / D, bias=EPS)
                        p.inc(nm + "_S1",
                              sc.activation(rstd[:, g2:g2 + 1], var[:, g2:g2 + 1],
                                            AF.Exp, scale=-0.5))

    phase_barrier(p)


def emit_l0_ffn(p, consts, win):
    nc = p.nc
    xnT = p.dram["xnT"]
    curf = p.dram["cur_f32"]
    W1in, b1in = win["L0_ffn_W1"], win["L0_ffn_b1"]
    W2in, b2in = win["L0_ffn_W2"], win["L0_ffn_b2"]
    Wgin, bgin = win["L0_ffn_Wg"], win["L0_ffn_bg"]
    W1b = p.scratch("f3_W1b", [4, NFC // 4, 128, 4, 8, 128], BF16)
    W2b = p.scratch("f3_W2b", [4, NFC, 128, D], BF16)
    id_f = consts["id_f"]
    ones_b = consts["ones_b"]
    NR, NE, NTB4 = F3_NR, F3_NE, 4
    NSUB = 8

    sems = ["f3_L0", "f3_G0", "f3_LW", "f3_GW", "f3_WB", "f3_Th", "f3_Se",
            "f3_Tw", "f3_Ve", "f3_Tg", "f3_Sg", "f3_Tt", "f3_Vt", "f3_Vg1",
            "f3_Sg2", "f3_Va", "f3_Vb", "f3_Vg", "f3_Lc", "f3_SC", "f3_Tx",
            "f3_Vx"]
    for s in sems:
        p.sem(s)
    B = {s: p.cnt[s] for s in sems}

    tgt = {}
    wb_after_e = {e: B["f3_WB"] + 16 * 64 * (e + 1) for e in range(NE)}

    def gH(r, tb, e):
        return B["f3_Th"] + ((r * NTB4 + tb) * NE + e) * 64

    def gW(r, tb, e, pas):
        return B["f3_Tw"] + (((r * NTB4 + tb) * NE + e) * 2 + pas) * 32

    def gVe(r, tb, e):
        return B["f3_Ve"] + ((r * NTB4 + tb) * NE + e) * 16

    def gSe(r, tb, e):
        return B["f3_Se"] + ((r * NTB4 + tb) * NE + e) * 64

    def gTx(r, tb):
        return B["f3_Tx"] + (r * NTB4 + tb) * 64

    with ExitStack() as scope:
        e_ = scope.enter_context
        curT = e_(nc.sbuf_tensor("f3_curT", [128, NDC * T], BF16))
        H_sb = e_(nc.sbuf_tensor("f3_H", [128, NFC * 1024], BF16))
        acc = e_(nc.sbuf_tensor("f3_acc", [128, NSUB * D], F32))
        W1sl = [e_(nc.sbuf_tensor(f"f3_W1sl{i}", [128, 4 * 8 * 128], BF16))
                for i in range(2)]
        W2sl = [e_(nc.sbuf_tensor(f"f3_W2sl{i}", [128, 4 * 512], BF16))
                for i in range(2)]
        Wg_sb = e_(nc.sbuf_tensor("f3_Wg", [128, NR * NDC * 4], BF16))
        bg_sb = e_(nc.sbuf_tensor("f3_bg", [4, NR], F32))
        b1_sb = e_(nc.sbuf_tensor("f3_b1", [128, NE * NFC], F32))
        b2row = e_(nc.sbuf_tensor("f3_b2r", [1, NE * D], BF16))
        logT4 = [e_(nc.sbuf_tensor(f"f3_lg{i}", [4, 512], F32)) for i in range(2)]
        lgn = e_(nc.sbuf_tensor("f3_lgn", [128, NTT * 4], F32))
        cwn = e_(nc.sbuf_tensor("f3_cwn", [128, NTT * 4], F32))
        wk = e_(nc.sbuf_tensor("f3_wk", [128, 8 * 32], F32))
        exio = e_(nc.sbuf_tensor("f3_exio", [128, 10 * 32], F32))
        exchk = e_(nc.sbuf_tensor("f3_exchk", [128, 32], F32))
        wk2 = e_(nc.sbuf_tensor("f3_wk2", [128, 8 * 32], F32))
        ps = [e_(nc.psum_tensor(f"f3_ps{i}", [128, 512], F32)) for i in range(8)]

        def W(i):
            return wk[:, i * 32:(i + 1) * 32]

        def lgn_e(t, eix):
            return bass.AP(t, eix, [[NTT * 4, 128], [4, NTT]])

        with nc.Block() as b:
            @b.sync
            def _(s):
                for r in range(NR):
                    p.inc("f3_L0", s.dma_start(
                        out=bg_sb[:, r:r + 1],
                        in_=bass.AP(bgin, r * 4, [[1, 4], [1, 1]]),
                        allow_slow_non_contiguous=True), 16)
                for e in range(NE):
                    p.inc("f3_L0", s.dma_start(
                        out=b1_sb[:, e * NFC:(e + 1) * NFC],
                        in_=bias_chunks_ap(b1in, NFC, off=e * FF),
                        allow_slow_non_contiguous=True), 16)
                for r in range(NR):
                    for tb in range(NTB4):
                        idx = r * NTB4 + tb
                        if idx > 0:
                            s.wait_ge(p.sem("f3_SC"), B["f3_SC"] + 16 * idx)
                            prev_r, prev_tb = (r, tb - 1) if tb > 0 else \
                                (r - 1, NTB4 - 1)
                            if prev_r < 3:
                                s.wait_ge(p.sem("f3_Tx"),
                                          gTx(prev_r, prev_tb) + 64)
                        srcc = bass.AP(curf, tb * 1024 * D,
                                       [[D, 128], [128 * D, NSUB], [1, D]])
                        p.inc("f3_Lc", s.dma_start(out=acc[:, :], in_=srcc), 16)
                        for e in range(NE):
                            if r == 0 and tb == 0:
                                continue
                            if r == 0 and tb == 1:
                                s.wait_ge(p.sem("f3_WB"), wb_after_e[e])
                            for fb in range(NFC // 4):
                                tv = gH(r, tb, e) + 8 * (fb - 1)
                                if tv > B["f3_Th"]:
                                    s.wait_ge(p.sem("f3_Th"), tv)
                                srcw = bass.AP(W1b,
                                               (e * 8 + fb) * 128 * 4096,
                                               [[4096, 128], [1, 4096]])
                                ins = s.dma_start(out=W1sl[fb % 2][:, :], in_=srcw)
                                p.inc("f3_LW", ins, 16)
                                for fc in range(4 * fb, 4 * fb + 4):
                                    tgt[(r, tb, e, "w1", fc)] =                                         ("f3_LW", p.cnt["f3_LW"])
                            for pas in range(2):
                                for fb in range(NFC // 4):
                                    tv = gW(r, tb, e, pas) + 4 * (fb - 1)
                                    if tv > B["f3_Tw"]:
                                        s.wait_ge(p.sem("f3_Tw"), tv)
                                    srcw = bass.AP(
                                        W2b,
                                        ((e * NFC) + 4 * fb) * 128 * D + pas * 512,
                                        [[D, 128], [128 * D, 4], [1, 512]])
                                    ins = s.dma_start(out=W2sl[fb % 2][:, :],
                                                      in_=srcw)
                                    p.inc("f3_LW", ins, 16)
                                    for fc in range(4 * fb, 4 * fb + 4):
                                        tgt[(r, tb, e, "w2", fc, pas)] =                                             ("f3_LW", p.cnt["f3_LW"])

            @b.gpsimd
            def _(g):
                for c in range(NDC):
                    srcx = bass.AP(xnT, c * 128 * 512,
                                   [[512, 128], [NDC * 128 * 512, NTB], [1, 512]])
                    p.inc("f3_G0",
                          g.dma_start(out=curT[:, c * T:(c + 1) * T], in_=srcx), 16)
                p.inc("f3_G0", g.dma_start(
                    out=b2row[:, :],
                    in_=bass.AP(b2in, 0, [[NE * D, 1], [1, NE * D]])), 16)
                for r in range(NR):
                    srcw = bass.AP(Wgin, r * D * 4, [[4, 128], [512, NDC], [1, 4]])
                    p.inc("f3_G0", g.dma_start(
                        out=Wg_sb[:, r * NDC * 4:(r + 1) * NDC * 4], in_=srcw,
                        allow_slow_non_contiguous=True), 16)
                r, tb = 0, 0
                for e in range(NE):
                    for fc in range(NFC):
                        fb, fq = fc // 4, fc % 4
                        for q4 in range(1):
                            srcc = bass.AP(W1in, e * D * FF + fc * 128,
                                           [[FF, 128], [128 * FF, 8], [1, 128]])
                            dstc = bass.AP(W1b,
                                           (e * 8 + fb) * 128 * 4096 + fq * 1024,
                                           [[4096, 128], [128, 8], [1, 128]])
                            p.inc("f3_WB", g.dma_start(out=dstc, in_=srcc), 16)
                        srcc = bass.AP(W2in, e * FF * D + fc * 128 * D,
                                       [[D, 128], [1, D]])
                        dstc = bass.AP(W2b, ((e * NFC) + fc) * 128 * D,
                                       [[D, 128], [1, D]])
                        p.inc("f3_WB", g.dma_start(out=dstc, in_=srcc), 16)
                    for fc in range(NFC):
                        fb, fq = fc // 4, fc % 4
                        tv = gH(r, tb, e) + 8 * (fb - 1)
                        if tv > B["f3_Th"]:
                            g.wait_ge(p.sem("f3_Th"), tv)
                        srcw = bass.AP(W1in, e * D * FF + fc * 128,
                                       [[FF, 128], [128 * FF, 8], [1, 128]])
                        dst = W1sl[fb % 2][:, fq * 1024:(fq + 1) * 1024]
                        p.inc("f3_GW", g.dma_start(out=dst, in_=srcw), 16)
                        tgt[(r, tb, e, "w1", fc)] = ("f3_GW", p.cnt["f3_GW"])
                    for pas in range(2):
                        for fc in range(NFC):
                            fb, fq = fc // 4, fc % 4
                            tv = gW(r, tb, e, pas) + 4 * (fb - 1)
                            if tv > B["f3_Tw"]:
                                g.wait_ge(p.sem("f3_Tw"), tv)
                            srcw = bass.AP(W2in,
                                           e * FF * D + fc * 128 * D + pas * 512,
                                           [[D, 128], [1, 512]])
                            dst = W2sl[fb % 2][:, fq * 512:(fq + 1) * 512]
                            p.inc("f3_GW", g.dma_start(out=dst, in_=srcw), 16)
                            tgt[(r, tb, e, "w2", fc, pas)] =                                 ("f3_GW", p.cnt["f3_GW"])
                for r in range(NR):
                    for tb in range(NTB4):
                        g.wait_ge(p.sem("f3_Ve"), gVe(r, tb, NE - 1) + 16)
                        dstc = bass.AP(curf, tb * 1024 * D,
                                       [[D, 128], [128 * D, NSUB], [1, D]])
                        p.inc("f3_SC", g.dma_start(out=dstc, in_=acc[:, :]), 16)
                if "f3_dbg" in DEBUG_OUTPUTS:
                    dbg = p.scratch("f3_dbg", [2, 128, NTT * 4], F32)
                    g.wait_ge(p.sem("f3_Vg"), B["f3_Vg"] + 1)
                    p.inc("f3_SC", g.dma_start(
                        out=bass.AP(dbg, 0, [[NTT * 4, 128], [1, NTT * 4]]),
                        in_=lgn[:, :]), 16)
                    p.inc("f3_SC", g.dma_start(
                        out=bass.AP(dbg, 128 * NTT * 4,
                                    [[NTT * 4, 128], [1, NTT * 4]]),
                        in_=cwn[:, :]), 16)
                g.wait_ge(p.sem("f3_SC"), p.cnt["f3_SC"])
                g.wait_ge(p.sem("f3_WB"), p.cnt["f3_WB"])

            @b.scalar
            def _(sc):
                sc.wait_ge(p.sem("f3_L0"), B["f3_L0"] + 16 * (NR + NE))
                for r in range(NR):
                    for tb8 in range(NTB):
                        gi = r * NTB + tb8
                        sc.wait_ge(p.sem("f3_Tg"), B["f3_Tg"] + gi + 1)
                        if gi >= 2:
                            sc.wait_ge(p.sem("f3_Tt"), B["f3_Tt"] + gi - 1)
                        p.inc("f3_Sg", sc.activation(
                            logT4[tb8 % 2][:, :], ps[0][0:4, :], AF.Identity,
                            bias=bg_sb[:, r:r + 1]))
                    sc.wait_ge(p.sem("f3_Vg1"), B["f3_Vg1"] + r + 1)
                    sc.activation(exio[:, 160:320], exio[:, 0:160], AF.Exp)
                    p.inc("f3_Sg2", sc.copy(exchk[:, :], exio[:, 288:320]))
                    for tb in range(NTB4):
                        for e in range(NE):
                            for fc in range(NFC):
                                for th in range(2):
                                    k = fc * 2 + th
                                    sc.wait_ge(p.sem("f3_Th"),
                                               gH(r, tb, e) + k + 1)
                                    if k == 0 and (r, tb, e) != (0, 0, 0):
                                        sc.wait_ge(p.sem("f3_Tw"),
                                                   gW(r, tb, e, 0))
                                    p.inc("f3_Se", sc.activation(
                                        H_sb[:, fc * 1024 + th * 512:
                                             fc * 1024 + th * 512 + 512],
                                        ps[k % 2][:, :], AF.Gelu,
                                        bias=b1_sb[:, e * NFC + fc:
                                                   e * NFC + fc + 1]))

            @b.tensor
            def _(t):
                t.wait_ge(p.sem("f3_G0"), B["f3_G0"] + 16 * (NDC + 1 + NR))
                t.wait_ge(p.sem("f3_L0"), B["f3_L0"] + 16 * (NR + NE))
                for r in range(NR):
                    if r > 0:
                        t.wait_ge(p.sem("f3_Vx"), B["f3_Vx"] + r * NTB4 * 64)
                    for tb8 in range(NTB):
                        gi = r * NTB + tb8
                        if gi >= 1:
                            t.wait_ge(p.sem("f3_Sg"), B["f3_Sg"] + gi)
                        for dc in range(NDC):
                            ins = t.matmul(
                                ps[0][0:4, :],
                                Wg_sb[:, (r * NDC + dc) * 4:
                                      (r * NDC + dc) * 4 + 4],
                                curT[:, dc * T + tb8 * 512:
                                     dc * T + tb8 * 512 + 512],
                                start=(dc == 0), stop=(dc == NDC - 1))
                        p.inc("f3_Tg", ins)
                        t.wait_ge(p.sem("f3_Sg"), B["f3_Sg"] + gi + 1)
                        if gi >= 1:
                            t.wait_ge(p.sem("f3_Vt"), B["f3_Vt"] + gi)
                        for c4 in range(4):
                            ins = t.transpose(
                                ps[1][:, c4 * 4:c4 * 4 + 4],
                                logT4[tb8 % 2][0:4, c4 * 128:(c4 + 1) * 128],
                                id_f[0:4, 0:4])
                        p.inc("f3_Tt", ins)
                    for tb in range(NTB4):
                        for e in range(NE):
                            for fc in range(NFC):
                                sem, val = tgt[(r, tb, e, "w1", fc)]
                                t.wait_ge(p.sem(sem), val)
                                for th in range(2):
                                    k = fc * 2 + th
                                    if k == 0:
                                        t.wait_ge(p.sem("f3_Ve"), gVe(r, tb, e))
                                        if e == 0 and tb == 0:
                                            t.wait_ge(p.sem("f3_Vt"),
                                                      B["f3_Vt"] + NTB * (r + 1))
                                        if e == 0:
                                            vxb = ((r * NTB4 + tb) if r < 3
                                                   else 3 * NTB4) * 64
                                            t.wait_ge(p.sem("f3_Vx"),
                                                      B["f3_Vx"] + vxb)
                                    if k >= 2:
                                        t.wait_ge(p.sem("f3_Se"),
                                                  gSe(r, tb, e) + k - 1)
                                    fb, fq = fc // 4, fc % 4
                                    for dc in range(NDC):
                                        ins = t.matmul(
                                            ps[k % 2][:, :],
                                            W1sl[fb % 2][:, fq * 1024 + dc * 128:
                                                         fq * 1024 +
                                                         (dc + 1) * 128],
                                            curT[:, dc * T + tb * 1024 + th * 512:
                                                 dc * T + tb * 1024 +
                                                 th * 512 + 512],
                                            start=(dc == 0), stop=(dc == NDC - 1))
                                    p.inc("f3_Th", ins)
                            t.wait_ge(p.sem("f3_Se"), gSe(r, tb, e) + 64)
                            t.wait_ge(p.sem("f3_Vg"), B["f3_Vg"] + r + 1)
                            for pas in range(2):
                                t.wait_ge(p.sem("f3_Ve"), gVe(r, tb, e) + 8 * pas)
                                for tt in range(NSUB):
                                    t.matmul(ps[tt][:, :], ones_b[0:1, 0:128],
                                             b2row[:, e * D + pas * 512:
                                                   e * D + pas * 512 + 512],
                                             start=True, stop=False)
                                for fc in range(NFC):
                                    sem, val = tgt[(r, tb, e, "w2", fc, pas)]
                                    t.wait_ge(p.sem(sem), val)
                                    fb, fq = fc // 4, fc % 4
                                    for tt in range(NSUB):
                                        ins = t.matmul(
                                            ps[tt][:, :],
                                            H_sb[:, fc * 1024 + tt * 128:
                                                 fc * 1024 + (tt + 1) * 128],
                                            W2sl[fb % 2][:, fq * 512:
                                                         (fq + 1) * 512],
                                            start=False, stop=(fc == NFC - 1))
                                    p.inc("f3_Tw", ins)
                        if r < 3:
                            t.wait_ge(p.sem("f3_Ve"), gVe(r, tb, NE - 1) + 16)
                            for tt in range(NSUB):
                                for dc in range(NDC):
                                    k = tt * NDC + dc
                                    if k >= 2:
                                        t.wait_ge(p.sem("f3_Vx"),
                                                  B["f3_Vx"] + gTx(r, tb) -
                                                  B["f3_Tx"] + k - 1)
                                    ins = t.transpose(
                                        ps[k % 2][:, 0:128],
                                        acc[:, tt * D + dc * 128:
                                            tt * D + (dc + 1) * 128],
                                        id_f[:, :])
                                    p.inc("f3_Tx", ins)

            @b.vector
            def _(v):
                for r in range(NR):
                    for tb8 in range(NTB):
                        gi = r * NTB + tb8
                        v.wait_ge(p.sem("f3_Tt"), B["f3_Tt"] + gi + 1)
                        p.inc("f3_Vt", v.tensor_copy(
                            lgn[:, tb8 * 16:(tb8 + 1) * 16], ps[1][:, 0:16]))
                    c = [lgn_e(lgn, e) for e in range(4)]
                    v.tensor_tensor(W(0), c[0], c[1], ALU.max)
                    v.tensor_tensor(W(1), c[2], c[3], ALU.max)
                    v.tensor_tensor(W(2), c[0], c[1], ALU.min)
                    v.tensor_tensor(W(3), c[2], c[3], ALU.min)
                    v.tensor_tensor(W(4), W(0), W(1), ALU.max)
                    v.tensor_tensor(W(5), W(0), W(1), ALU.min)
                    v.tensor_tensor(W(6), W(2), W(3), ALU.max)
                    v.tensor_tensor(W(5), W(5), W(6), ALU.max)
                    v.tensor_tensor(exio[:, 0:32], W(5), W(4), ALU.subtract)
                    for e in range(4):
                        v.tensor_tensor(exio[:, (e + 1) * 32:(e + 2) * 32],
                                        c[e], W(4), ALU.subtract)
                    p.inc("f3_Vg1", v.tensor_scalar(W(7), exio[:, 128:160], 1.0,
                                                    None, ALU.mult))
                    v.wait_ge(p.sem("f3_Sg2"), B["f3_Sg2"] + r + 1)
                    p.inc("f3_Va", v.tensor_scalar(W(6), exio[:, 160:192], 1.0,
                                                   None, ALU.add))
                    v.wait_ge(p.sem("f3_Va"), p.cnt["f3_Va"])
                    p.inc("f3_Vb", v.reciprocal(W(6), W(6)))
                    v.wait_ge(p.sem("f3_Vb"), p.cnt["f3_Vb"])
                    for e in range(4):
                        v.tensor_tensor(wk2[:, (2 * e) * 32:(2 * e + 1) * 32],
                                        c[e], W(5), ALU.is_ge)
                    for e in range(4):
                        v.tensor_tensor(wk2[:, (2 * e + 1) * 32:(2 * e + 2) * 32],
                                        wk2[:, (2 * e) * 32:(2 * e + 1) * 32],
                                        exio[:, (6 + e) * 32:(7 + e) * 32],
                                        ALU.mult)
                    for e in range(4):
                        ins = v.tensor_tensor(
                            lgn_e(cwn, e),
                            wk2[:, (2 * e + 1) * 32:(2 * e + 2) * 32],
                            W(6), ALU.mult)
                    p.inc("f3_Vg", ins)
                    for tb in range(NTB4):
                        idx = r * NTB4 + tb
                        for e in range(NE):
                            for pas in range(2):
                                v.wait_ge(p.sem("f3_Tw"),
                                          gW(r, tb, e, pas) + NFC)
                                if e == 0 and pas == 0:
                                    v.wait_ge(p.sem("f3_Lc"),
                                              B["f3_Lc"] + 16 * (idx + 1))
                                for tt in range(NSUB):
                                    cw = cwn[:, (tb * NSUB + tt) * 4 + e:
                                             (tb * NSUB + tt) * 4 + e + 1]
                                    p.inc("f3_Ve", v.scalar_tensor_tensor(
                                        acc[:, tt * D + pas * 512:
                                            tt * D + pas * 512 + 512],
                                        ps[tt][:, :], cw,
                                        acc[:, tt * D + pas * 512:
                                            tt * D + pas * 512 + 512],
                                        ALU.mult, ALU.add))
                        if r < 3:
                            for tt in range(NSUB):
                                for dc in range(NDC):
                                    k = tt * NDC + dc
                                    v.wait_ge(p.sem("f3_Tx"),
                                              gTx(r, tb) + k + 1)
                                    ins = v.tensor_copy(
                                        curT[:, dc * T + tb * 1024 + tt * 128:
                                             dc * T + tb * 1024 +
                                             (tt + 1) * 128],
                                        ps[k % 2][:, 0:128])
                                    if k < NSUB * NDC - 1:
                                        p.inc("f3_Vx", ins)
                            p.inc("f3_Vx", v.tensor_copy(
                                exchk[:, :],
                                curT[:, (NDC - 1) * T + tb * 1024 +
                                     (NSUB - 1) * 128 + 96:
                                     (NDC - 1) * T + tb * 1024 + NSUB * 128]))

    phase_barrier(p)


def kernel(**inputs):
    x_full = np.ascontiguousarray(np.asarray(inputs["L0_attn_X"], dtype=np.float32))
    b_nd = np.asarray(inputs["L0_attn_b_nd"], dtype=np.float32)

    ii = np.arange(128)
    dgrid = ii[:, None] - ii[None, :]
    DT = np.empty((9, 11, 128, 128), dtype=np.float32)
    for k in range(9):
        dk = 128 * k + dgrid
        idx = np.clip(dk, 0, ML - 1)
        DT[k, :NB] = b_nd[:, idx]
        DT[k, NB] = np.where((dk < 0) | (dk >= ML), NEG, 0.0).astype(np.float32)

    weight_names = [k for k in inputs if k != "L0_attn_X"]
    common = {k: np.ascontiguousarray(np.asarray(inputs[k], dtype=np.float32))
              for k in weight_names}
    common["DT"] = DT

    p = build_program()
    in_maps = [dict(common, x=x_full[b]) for b in range(8)]
    res = run_bass_kernel_spmd(p.nc, in_maps, core_ids=list(range(8)))
    out = np.stack([np.asarray(res.results[b]["y"]) for b in range(8)], axis=0)
    return out.astype(np.float32)


def build_program():
    p = Prog()
    nc = p.nc
    x_in = nc.dram_tensor("x", [T, D], F32, kind="ExternalInput")
    y_out = nc.dram_tensor("y", [T, D], F32, kind="ExternalOutput")
    win = {}
    for nm, shape in [
        ("L0_attn_Wq", [D, D]), ("L0_attn_bq", [D]), ("L0_attn_Wk", [D, D]),
        ("L0_attn_Wv", [D, D]), ("L0_attn_Wproj", [D, D]), ("L0_attn_bproj", [D]),
        ("L0_attn_Wr", [D, H * NB]), ("L0_attn_br", [H * NB]),
        ("L0_attn_b_nd", [NB, ML]), ("DT", [9, 11, 128, 128]),
        ("L0_ffn_Wg", [4, D, 4]), ("L0_ffn_bg", [4, 4]),
        ("L0_ffn_W1", [4, D, FF]), ("L0_ffn_b1", [4, FF]),
        ("L0_ffn_W2", [4, FF, D]), ("L0_ffn_b2", [4, D]),
        ("L1_attn_Wq", [D, D]), ("L1_attn_bq", [D]), ("L1_attn_Wk", [D, D]),
        ("L1_attn_bk", [D]), ("L1_attn_Wv", [D, D]), ("L1_attn_bv", [D]),
        ("L1_attn_Wo", [D, D]), ("L1_attn_bo", [D]),
        ("L1_attn_gamma", [D]), ("L1_attn_beta", [D]),
        ("L1_ffn_w_router", [D, 8]), ("L1_ffn_w_up", [7, D, FF]),
        ("L1_ffn_w_gate", [7, D, FF]), ("L1_ffn_w_down", [7, FF, D]),
        ("L1_ffn_ws_up", [D, FF]), ("L1_ffn_ws_gate", [D, FF]),
        ("L1_ffn_ws_down", [FF, D]),
    ]:
        win[nm] = nc.dram_tensor(nm, shape, F32, kind="ExternalInput")

    id_b = p.sbuf("id_b", [128, 128], BF16)
    ones_b = p.sbuf("ones_b", [128, 128], BF16)
    negt_b = p.sbuf("negt_b", [128, 128], BF16)
    iota_f = p.sbuf("iota_f", [128, 128], F32)
    iota_p = p.sbuf("iota_p", [128, 1], F32)
    id_f = p.sbuf("id_f", [128, 128], F32)

    const_vals = [EPS]
    const_ts = []
    for i, val in enumerate(const_vals):
        t = p.sbuf(f"cst{i}", [128, 1], F32)
        nc.const_aps.register(F32, val, t[:, 0:1])
        const_ts.append((t, val))

    with nc.Block() as b:
        @b.gpsimd
        def _(g):
            g.iota(iota_f[:, :], [[1, 128]], channel_multiplier=0,
                   allow_small_or_imprecise_dtypes=True)
            p.inc("INIT_G", g.iota(iota_p[:, :], [[1, 1]], channel_multiplier=1,
                                   allow_small_or_imprecise_dtypes=True))

        @b.vector
        def _(v):
            v.wait_ge(p.sem("INIT_G"), 1)
            v.tensor_scalar(id_f[:, :], iota_f[:, :], iota_p[:, 0:1], None,
                            ALU.subtract)
            v.tensor_single_scalar(id_f[:, :], id_f[:, :], 0.0, ALU.is_equal)
            v.tensor_copy(id_b[:, :], id_f[:, :])
            v.memset(ones_b[:, :], 1.0)
            for t, val in const_ts:
                v.memset(t[:, :], val)
            p.inc("INIT_V", v.memset(negt_b[:, :], NEG))
    phase_barrier(p)

    consts = dict(id_b=id_b, ones_b=ones_b, negt_b=negt_b, id_f=id_f)
    r_scope = ExitStack()
    R_sb = r_scope.enter_context(nc.sbuf_tensor("R_sb", [128, NTT * H * NB], F32))
    consts["R_sb"] = R_sb

    phases = [
        lambda: emit_ln_phase(p, "p0", x_in.ap(), "xnT", "cur_f32", id_b),
        lambda: emit_attn_proj(p, "p1", "xnT",
                               win["L0_attn_Wq"], win["L0_attn_bq"],
                               win["L0_attn_Wk"], None,
                               win["L0_attn_Wv"], None,
                               wr=win["L0_attn_Wr"], br=win["L0_attn_br"],
                               R_sb=R_sb),
        lambda: (emit_l0_attn(p, consts, win), r_scope.close()),
        lambda: emit_proj_nat(p, "p2b", "a0_A2T", win["L0_attn_Wproj"],
                              win["L0_attn_bproj"], "s0"),
        lambda: emit_l0_ffn(p, consts, win),
        lambda: emit_ln_phase(p, "p4", x_in.ap(), "xn1T", None, id_b,
                              merge=(p.scratch("s0", [T, D], F32),
                                     p.scratch("cur_f32", [T, D], F32),
                                     p.scratch("x1", [T, D], F32))),
        lambda: emit_attn_proj(p, "p5", "xn1T",
                               win["L1_attn_Wq"], win["L1_attn_bq"],
                               win["L1_attn_Wk"], win["L1_attn_bk"],
                               win["L1_attn_Wv"], win["L1_attn_bv"]),
        lambda: emit_l1_attn(p, consts, win),
        lambda: emit_proj_nat(p, "p6b", "a1_A2T", win["L1_attn_Wo"],
                              win["L1_attn_bo"], "s1",
                              post_ln=(win["L1_attn_gamma"],
                                       win["L1_attn_beta"])),
        lambda: emit_l1_ffn(p, consts, win),
        lambda: emit_final(p, y_out),
    ]
    for i, ph in enumerate(phases):
        if PHASES_SELECT is not None and i not in PHASES_SELECT:
            continue
        if PHASE_LIMIT is not None and i >= PHASE_LIMIT:
            break
        ph()

    if "R_dump" in DEBUG_OUTPUTS:
        rd = p.scratch("R_dump", [128, NTT * H * NB], BF16)
        with nc.Block() as b:
            @b.gpsimd
            def _(g):
                p.inc("RD", g.dma_start(out=rd.ap(), in_=R_sb[:, :]), 16)
                g.wait_ge(p.sem("RD"), p.cnt["RD"])
        phase_barrier(p)

    if PHASE_LIMIT is not None or PHASES_SELECT is not None:
        finalize(p, y_out, consts, win)
    p.ctx.close()
    return p


def finalize(p, y_out, consts, win):
    nc = p.nc
    with ExitStack() as scope:
        z = scope.enter_context(nc.sbuf_tensor("zf", [128, D], F32))
        with nc.Block() as b:
            @b.vector
            def _(v):
                p.inc("FINV", v.memset(z[:, :], 0.0))

            @b.gpsimd
            def _(g):
                g.wait_ge(p.sem("FINV"), p.cnt["FINV"])
                for i in range(NTT):
                    p.inc("FIN", g.dma_start(
                        out=y_out.ap()[i * 128:(i + 1) * 128, :], in_=z[:, :]), 16)
                g.wait_ge(p.sem("FIN"), p.cnt["FIN"])
